# version 08
# baseline (speedup 1.0000x reference)
import contextlib
import numpy as np

import concourse.bass as bass
import concourse.mybir as mybir
from concourse.bass_utils import run_bass_kernel_spmd

F32 = mybir.dt.float32
BF16 = mybir.dt.bfloat16
AF = mybir.ActivationFunctionType
ALU = mybir.AluOpType
AX = mybir.AxisListType

MAGIC = float(np.float32(1.5 * 2 ** 23))
QMAX = 32767.0
WQMAX = 7.0
EPS_S = 1e-12
LN_EPS = 1e-5
SIGCLIP = float(np.float32(65535.0))
INV65536 = 1.0 / 65536.0

NCORE = 8
SEQ = 2048
D = 1024
F = 3072
SPC = 4
N = SPC * SEQ
TT = N // 128
NB = N // 512
DC = D // 128
FC = F // 128
CH = 256
NCH = SEQ // CH
TPCH = CH // 128

ENGINES = ("pe", "ve", "act", "sp", "g")


class Prog:

    DMA_POOL = {"sp": 12, "g": 4}

    def __init__(self, nc):
        self.nc = nc
        self.streams = {e: [] for e in ENGINES}
        self.cnt = {(e, "c"): 0 for e in ENGINES}
        for e, npool in self.DMA_POOL.items():
            for i in range(npool):
                self.cnt[(e + "_d", i)] = 0
        self.dma_idx = {e: 0 for e in self.DMA_POOL}
        self.sems = {}
        self.last_w = {}
        self.readers = {}
        self.watermark = {e: {} for e in ENGINES}
        self.n_cc = 0

    def _need(self, eng, keys_r, keys_w):
        need = {}
        for kk in keys_r:
            lw = self.last_w.get(kk)
            if lw is not None:
                need[lw[0]] = max(need.get(lw[0], 0), lw[1])
        for kk in keys_w:
            lw = self.last_w.get(kk)
            if lw is not None:
                need[lw[0]] = max(need.get(lw[0], 0), lw[1])
            for (ref, c) in self.readers.get(kk, ()):
                need[ref] = max(need.get(ref, 0), c)
        wm = self.watermark[eng]
        out = []
        for ref, c in need.items():
            if wm.get(ref, 0) >= c:
                continue
            out.append((ref, c))
            wm[ref] = c
        return out

    def _record(self, ref, mycnt, reads, writes):
        for kk in writes:
            self.last_w[kk] = (ref, mycnt)
            self.readers[kk] = []
        for kk in reads:
            self.readers.setdefault(kk, []).append((ref, mycnt))

    @staticmethod
    def _is_psum(k):
        return (isinstance(k, tuple) and k[0] in ("pmm", "ptrb")) or k == "pms"

    def _excl(self, reads, writes):
        pr = [k for k in reads if self._is_psum(k)]
        if not pr:
            return reads, writes
        return [k for k in reads if not self._is_psum(k)], list(writes) + pr

    def emit(self, eng, fn, reads=(), writes=(), kind="c"):
        reads, writes = self._excl(reads, writes)
        waits = self._need(eng, reads, writes)
        if kind == "d":
            npool = self.DMA_POOL[eng]
            si = self.dma_idx[eng] % npool
            self.dma_idx[eng] += 1
            ref = (eng + "_d", si)
            step = 16
            prev = self.cnt[ref]
            if prev > 0 and self.watermark[eng].get(ref, 0) < prev:
                waits = waits + [(ref, prev)]
                self.watermark[eng][ref] = prev
        else:
            ref = (eng, kind)
            step = 1
        self.cnt[ref] += step
        mycnt = self.cnt[ref]

        def thunk(q, waits=waits, fn=fn, ref=ref, step=step):
            for (r, c) in waits:
                q.wait_ge(self.sems[r], c)
            fn(q).then_inc(self.sems[ref], step)

        self.streams[eng].append(thunk)
        self._record(ref, mycnt, reads, writes)

    def emit_cc(self, fn, reads=(), writes=()):
        waits = self._need("g", reads, writes)
        ref = ("cc", self.n_cc)
        self.n_cc += 1
        self.cnt[ref] = 1

        def thunk(q, waits=waits, fn=fn, ref=ref):
            for (r, c) in waits:
                q.wait_ge(self.sems[r], c)
            fn(q).then_inc(self.sems[ref], 1)

        self.streams["g"].append(thunk)
        self._record(ref, 1, reads, writes)

    def mm_group(self, mms, reads=(), writes=()):
        reads, writes = self._excl(reads, writes)
        waits = self._need("pe", reads, writes)
        ref = ("pe", "c")
        self.cnt[ref] += 1
        mycnt = self.cnt[ref]
        n = len(mms)

        def thunk(q, waits=waits, mms=mms, n=n):
            for (r, c) in waits:
                q.wait_ge(self.sems[r], c)
            for i, f in enumerate(mms):
                inst = f(q, i == 0, i == n - 1)
                if i == n - 1:
                    inst.then_inc(self.sems[("pe", "c")], 1)

        self.streams["pe"].append(thunk)
        self._record(ref, mycnt, reads, writes)

    def full_barrier(self):
        snap = [(ref, c) for ref, c in self.cnt.items() if c > 0]
        for e in ENGINES:
            wm = self.watermark[e]
            waits = [(ref, c) for ref, c in snap if wm.get(ref, 0) < c]
            for ref, c in waits:
                wm[ref] = c

            def thunk(q, waits=waits):
                for (r, c) in waits:
                    q.wait_ge(self.sems[r], c)

            self.streams[e].append(thunk)

    def build(self, block, stack):
        for ref in self.cnt:
            self.sems[ref] = stack.enter_context(
                self.nc.semaphore(f"s_{ref[0]}_{ref[1]}"))

        def runner(e):
            def run(q):
                for t in self.streams[e]:
                    t(q)
            return run

        block.tensor(runner("pe"))
        block.vector(runner("ve"))
        block.scalar(runner("act"))
        block.sync(runner("sp"))
        block.gpsimd(runner("g"))



def build_nc(debug_out=(), stage="full"):
    nc = bass.Bass("TRN2", target_bir_lowering=False, debug=False)
    stack = contextlib.ExitStack()
    import ml_dtypes

    x0_h = nc.dram_tensor("x", [N, D], F32, kind="ExternalInput")
    wq_h = nc.dram_tensor("Wq", [D, D], F32, kind="ExternalInput")
    wk_h = nc.dram_tensor("Wk", [D, D], F32, kind="ExternalInput")
    wv_h = nc.dram_tensor("Wv", [D, D], F32, kind="ExternalInput")
    wo_h = nc.dram_tensor("Wo", [D, D], F32, kind="ExternalInput")
    bq_h = nc.dram_tensor("bq", [1, D], F32, kind="ExternalInput")
    bk_h = nc.dram_tensor("bk", [1, D], F32, kind="ExternalInput")
    bv_h = nc.dram_tensor("bv", [1, D], F32, kind="ExternalInput")
    bo_h = nc.dram_tensor("bo", [1, D], F32, kind="ExternalInput")
    dec_h = nc.dram_tensor("decay", [1, 1], F32, kind="ExternalInput")
    wgate_h = nc.dram_tensor("w_gate", [F, D], F32, kind="ExternalInput")
    wup_h = nc.dram_tensor("w_up", [F, D], F32, kind="ExternalInput")
    wdown_h = nc.dram_tensor("w_down", [D, F], F32, kind="ExternalInput")
    out_h = nc.dram_tensor("out", [N, D], F32, kind="ExternalOutput")

    x1_h = nc.dram_tensor("x1s", [N, D], F32)
    x2_h = nc.dram_tensor("x2s", [N, D], F32)
    gate_s = nc.dram_tensor("gate_s", [FC, 128, N], BF16)
    up_s = nc.dram_tensor("up_s", [FC, 128, N], BF16)
    m1_s = nc.dram_tensor("m1_s", [FC, 128, N], BF16)
    o_s = nc.dram_tensor("o_s", [N // 256, 128, FC * 256], BF16)
    wupT_h = nc.dram_tensor("wupT_s", [128, DC * F], BF16)
    wgateT_h = nc.dram_tensor("wgateT_s", [128, DC * F], BF16)
    wdownT_h = nc.dram_tensor("wdownT_s", [128, FC * D], BF16)
    cc_in_h = nc.dram_tensor("cc_in", [1, 4], F32)
    cc_out_h = nc.dram_tensor("cc_out", [1, 4], F32)

    dbg = {}
    for name in debug_out:
        if name in ("x1", "x2"):
            dbg[name] = nc.dram_tensor("dbg_" + name, [N, D], F32,
                                       kind="ExternalOutput")
        elif name in ("gate", "up", "m1"):
            dbg[name] = nc.dram_tensor("dbg_" + name, [FC, 128, N], BF16,
                                       kind="ExternalOutput")
        elif name == "scal":
            dbg[name] = nc.dram_tensor("dbg_scal", [1, 16], F32,
                                       kind="ExternalOutput")

    ident_f_h = nc.inline_tensor(np.eye(128, dtype=np.float32), name="c_idf")
    ident_b_h = nc.inline_tensor(np.eye(128, dtype=ml_dtypes.bfloat16), name="c_idb")
    ones_f_h = nc.inline_tensor(np.ones((1, 128), dtype=np.float32), name="c_ones")

    p = Prog(nc)
    sb = lambda name, shape, dt: stack.enter_context(nc.sbuf_tensor(name, shape, dt))
    psal = lambda name, shape, dt: stack.enter_context(nc.psum_tensor(name, shape, dt))

    idf = sb("idf", [128, 128], F32)
    idb = sb("idb", [128, 128], BF16)
    onesf = sb("onesf", [1, 128], F32)
    mv1 = sb("mv1", [128, 2 * TT], F32)
    mva = sb("mva", [128, 2 * TT], F32)
    mv2 = sb("mv2", [128, 2 * TT], F32)
    rstd1, b1 = sb("rstd1", [128, TT], F32), sb("b1", [128, TT], F32)
    rstda, ba = sb("rstda", [128, TT], F32), sb("ba", [128, TT], F32)
    rstd2, b2 = sb("rstd2", [128, TT], F32), sb("b2", [128, TT], F32)
    xnm1 = sb("xnm1", [128, TT], F32)
    xnm2 = sb("xnm2", [128, TT], F32)
    scbc = sb("scbc", [128, 16], F32)
    scrow = sb("scrow", [1, 8], F32)
    ntmp = sb("ntmp", [128, 1], F32)

    SC_A = 0
    SC_B = 2
    SC_NM = 6
    SC_C = 7
    SC_D = 10

    pmm = [psal(f"pmm{i}", [128, 512], F32) for i in range(4)]
    ptrb = [psal(f"ptrb{i}", [128, 1024], BF16) for i in range(2)]
    pms = psal("pms", [128, 512], F32)
    mmctr = [0]

    def next_bank():
        mmctr[0] += 1
        return pmm[mmctr[0] % 4], ("pmm", mmctr[0] % 4)

    ptrctr = [0]

    def dma(queue, out, in_, reads=(), writes=()):
        p.emit(queue, lambda q, o=out, i=in_: q.dma_start(out=o, in_=i),
               reads=reads, writes=writes, kind="d")

    def ve(fn, reads=(), writes=()):
        p.emit("ve", fn, reads=reads, writes=writes)

    def act(fn, reads=(), writes=()):
        p.emit("act", fn, reads=reads, writes=writes)

    K_IDF, K_IDB, K_ONES = "idf", "idb", "onesf"

    def bcast_row(row_ap, out_ap, row_keys, out_keys, ncols):
        p.mm_group([lambda q, s, e: q.matmul(pms[:, 0:ncols], onesf[0:1, :],
                                             row_ap, start=s, stop=e)],
                   reads=[K_ONES] + list(row_keys), writes=["pms"])
        ve(lambda q: q.tensor_copy(out_ap, pms[:, 0:ncols]),
           reads=["pms"], writes=list(out_keys))

    def tr_batch(src_aps, read_keys):
        i = ptrctr[0] % 2
        ptrctr[0] += 1
        bank, bkey = ptrb[i], ("ptrb", i)
        mms = [(lambda q, s2, e2, j=j, src=src: q.transpose(
            bank[:, j * 128:(j + 1) * 128], src, idb[:, :]))
               for j, src in enumerate(src_aps)]
        p.mm_group(mms, reads=list(read_keys) + [K_IDB], writes=[bkey])
        return bank, bkey

    def colmax_to_row(col_ap, col_key, out_col):
        p.mm_group([lambda q, s, e: q.transpose(pms[0:1, 0:128], col_ap,
                                                idf[:, :])],
                   reads=[col_key, K_IDF], writes=["pms"])
        ve(lambda q: q.tensor_reduce(scrow[0:1, out_col:out_col + 1],
                                     pms[0:1, 0:128], axis=AX.X, op=ALU.max),
           reads=["pms"], writes=[("scrow", out_col)])

    def collective_max(ncols):
        dma("g", cc_in_h.ap()[0:1, 0:ncols], scrow[0:1, 0:ncols],
            reads=[("scrow", c) for c in range(ncols)], writes=["cc_in"])
        p.emit_cc(lambda q: q.collective_compute(
            "AllReduce", ALU.max, replica_groups=[list(range(NCORE))],
            ins=[cc_in_h.ap()[0:1, 0:ncols].opt()],
            outs=[cc_out_h.ap()[0:1, 0:ncols].opt()]),
            reads=["cc_in"], writes=["cc_out"])
        dma("g", scrow[0:1, 0:ncols], cc_out_h.ap()[0:1, 0:ncols],
            reads=["cc_out"],
            writes=["scrow"] + [("scrow", c) for c in range(ncols)])

    def scale_pair(src_col, dst_cols, pre=1.0):
        c0, c1 = dst_cols
        ve(lambda q: q.tensor_scalar(scrow[0:1, c0:c0 + 1],
                                     scrow[0:1, src_col:src_col + 1],
                                     pre / QMAX, EPS_S, ALU.mult, ALU.max),
           reads=[("scrow", src_col)], writes=[("scrow", c0)])
        ve(lambda q: q.reciprocal(scrow[0:1, c1:c1 + 1], scrow[0:1, c0:c0 + 1]),
           reads=[("scrow", c0)], writes=[("scrow", c1)])

    dma("sp", idf[:, :], ident_f_h.ap(), writes=[K_IDF])
    dma("sp", idb[:, :], ident_b_h.ap(), writes=[K_IDB])
    dma("sp", onesf[:, :], ones_f_h.ap(), writes=[K_ONES])
    zcol = sb("zcol", [128, 1], F32)
    onecol = sb("onecol", [128, 1], F32)
    mgcol = sb("mgcol", [128, 1], F32)
    ve(lambda q: q.memset(zcol[:, :], 0.0), writes=["zcol"])
    ve(lambda q: q.memset(onecol[:, :], 1.0), writes=["onecol"])
    ve(lambda q: q.memset(mgcol[:, :], MAGIC), writes=["mgcol"])
    nc.const_aps.register(F32, 0.0, zcol[:, 0:1])
    nc.const_aps.register(F32, 1.0, onecol[:, 0:1])
    nc.const_aps.register(F32, MAGIC, mgcol[:, 0:1])

    import os
    PREP_OC = int(os.environ.get("KPREP_OC", "999"))
    PREP_SKIP = os.environ.get("KPREP_SKIP", "")

    def prep_ffn_weight(w_h, wT_h, O, I, name):
        OC = min(O // 128, PREP_OC)
        IC = I // 128
        nblk = I // 32
        with contextlib.ExitStack() as st2:
            wst = [st2.enter_context(nc.sbuf_tensor(f"pw_{name}_w{i}", [128, I], F32))
                   for i in range(2)]
            sblk = [st2.enter_context(nc.sbuf_tensor(f"pw_{name}_s{i}", [128, nblk],
                                                     F32)) for i in range(2)]
            invb = st2.enter_context(nc.sbuf_tensor(f"pw_{name}_i", [128, nblk], F32))
            r1 = st2.enter_context(nc.sbuf_tensor(f"pw_{name}_r", [128, I], F32))
            deq = [st2.enter_context(nc.sbuf_tensor(f"pw_{name}_d{i}", [128, I],
                                                    BF16)) for i in range(2)]
            tout = st2.enter_context(nc.sbuf_tensor(f"pw_{name}_t", [128, 1024], BF16))

            def w_front(oc):
                par = oc % 2
                dma("sp", wst[par][:, :], w_h.ap()[oc * 128:(oc + 1) * 128, :],
                    writes=[(name, "w", par)])
                ve(lambda q, par=par: q.tensor_reduce(
                    sblk[par][:, :],
                    wst[par][:, :].rearrange("p (b c) -> p b c", c=32),
                    axis=AX.X, op=ALU.max, apply_absolute_value=True),
                   reads=[(name, "w", par)], writes=[(name, "s", par)])

            def w_rest(oc):
                par = oc % 2
                ve(lambda q, par=par: q.tensor_scalar(
                    sblk[par][:, :], sblk[par][:, :],
                    1.0 / WQMAX, EPS_S, ALU.mult, ALU.max),
                   reads=[(name, "s", par)], writes=[(name, "s", par)])
                ve(lambda q, par=par: q.reciprocal(invb[:, :], sblk[par][:, :]),
                   reads=[(name, "s", par)], writes=[(name, "inv")])
                ve(lambda q, par=par: q.tensor_tensor(
                    r1[:, :].rearrange("p (b c) -> p b c", c=32),
                    wst[par][:, :].rearrange("p (b c) -> p b c", c=32),
                    invb[:, :].rearrange("p (b o) -> p b o", o=1)
                        .broadcast_to([128, nblk, 32]),
                    ALU.mult),
                   reads=[(name, "w", par), (name, "inv")], writes=[(name, "r1")])
                ve(lambda q: q.tensor_scalar(r1[:, :], r1[:, :], MAGIC, MAGIC,
                                             ALU.add, ALU.subtract),
                   reads=[(name, "r1")], writes=[(name, "r1")])
                ve(lambda q, par=par: q.tensor_tensor(
                    deq[par][:, :].rearrange("p (b c) -> p b c", c=32),
                    r1[:, :].rearrange("p (b c) -> p b c", c=32),
                    sblk[par][:, :].rearrange("p (b o) -> p b o", o=1)
                        .broadcast_to([128, nblk, 32]),
                    ALU.mult),
                   reads=[(name, "r1"), (name, "s", par)],
                   writes=[(name, "deq", par)])
                _w_tail(oc, par)

            def _w_tail(oc, par):
                for ic0 in range(0, IC, 8):
                    if "T" in PREP_SKIP:
                        continue
                    nb = min(8, IC - ic0)
                    srcs = [deq[par][:, (ic0 + j) * 128:(ic0 + j + 1) * 128]
                            for j in range(nb)]
                    bank, bkey = tr_batch(srcs, [(name, "deq", par)])
                    tk = (name, "tout")
                    ve(lambda q, nb=nb, bank=bank: q.tensor_copy(
                        tout[:, 0:nb * 128], bank[:, 0:nb * 128]),
                       reads=[bkey], writes=[tk])
                    if "S" in PREP_SKIP:
                        continue
                    dma("g",
                        wT_h.ap()[:, :].rearrange("p (i t) -> p i t", t=O)
                        [:, ic0:ic0 + nb, oc * 128:(oc + 1) * 128],
                        tout[:, 0:nb * 128].rearrange("p (i t) -> p i t", t=128),
                        reads=[tk], writes=[(name, "spill")])

            w_front(0)
            for oc in range(OC):
                if oc + 1 < OC:
                    w_front(oc + 1)
                w_rest(oc)
            p.full_barrier()

    def zero_out_finish():
        zo = sb("zot", [128, D], F32)
        ve(lambda q: q.memset(zo[:, :], 0.0), writes=["zot"])
        for t in range(TT):
            dma("sp", out_h.ap()[t * 128:(t + 1) * 128, :], zo[:, :],
                reads=["zot"], writes=["outw"])

    if stage == "consts":
        zero_out_finish()
        p.full_barrier()
        with nc.Block() as block:
            p.build(block, stack)
        stack.close()
        return nc

    prep_ffn_weight(wup_h, wupT_h, F, D, "wup")
    if stage == "prep1":
        zero_out_finish()
        p.full_barrier()
        with nc.Block() as block:
            p.build(block, stack)
        stack.close()
        return nc
    prep_ffn_weight(wgate_h, wgateT_h, F, D, "wgate")
    prep_ffn_weight(wdown_h, wdownT_h, D, F, "wdown")

    def ffn_call(tag, xin_h, xout_h, mv_t, rstd_t, b_t, xnm_t,
                 need_pass_a, fuse_stats_mv=None, dbg_x=None, upto="E"):
        kmv = tag + "mv"
        krstd, kb, kxnm = tag + "rstd", tag + "b", tag + "xnm"
        kxin, kxout = xin_h.name, xout_h.name

        if need_pass_a:
            with contextlib.ExitStack() as st2:
                xa = [st2.enter_context(nc.sbuf_tensor(f"{tag}Axa{i}", [128, D], F32))
                      for i in range(2)]
                bnst = st2.enter_context(nc.sbuf_tensor(f"{tag}Abn", [128, 12], F32))
                tsub = [st2.enter_context(nc.sbuf_tensor(f"{tag}Ats{i}", [128, D], F32))
                        for i in range(2)]
                for t in range(TT):
                    xk = (tag, "Axa", t % 2)
                    dma("sp", xa[t % 2][:, :], xin_h.ap()[t * 128:(t + 1) * 128, :],
                        reads=[kxin], writes=[xk])
                    ve(lambda q, t=t: q.bn_stats(bnst[:, 0:6], xa[t % 2][:, 0:512]),
                       reads=[xk], writes=[(tag, "Abn0")])
                    ve(lambda q, t=t: q.bn_stats(bnst[:, 6:12], xa[t % 2][:, 512:1024]),
                       reads=[xk], writes=[(tag, "Abn1")])
                    ve(lambda q, t=t: q.bn_aggr(mv_t[:, 2 * t:2 * t + 2], bnst[:, :]),
                       reads=[(tag, "Abn0"), (tag, "Abn1")],
                       writes=[(kmv, t), kmv])
                    ve(lambda q, t=t: q.tensor_scalar(
                        ntmp[:, :], mv_t[:, 2 * t:2 * t + 1], -1.0, None, ALU.mult),
                       reads=[(kmv, t)], writes=["ntmp"])
                    act(lambda q, t=t: q.activation(
                        tsub[t % 2][:, :], xa[t % 2][:, :], AF.Identity,
                        bias=ntmp[:, :], scale=1.0),
                        reads=[xk, "ntmp"], writes=[(tag, "Atsub", t % 2)])
                    ve(lambda q, t=t: q.tensor_reduce(
                        xnm_t[:, t:t + 1], tsub[t % 2][:, :], axis=AX.X,
                        op=ALU.max, apply_absolute_value=True),
                       reads=[(tag, "Atsub", t % 2)], writes=[kxnm])
                p.full_barrier()

        var_view = mv_t[:, :].rearrange("p (t two) -> p t two", two=2)[:, :, 1]
        mean_view = mv_t[:, :].rearrange("p (t two) -> p t two", two=2)[:, :, 0]
        ve(lambda q: q.tensor_scalar(rstd_t[:, :], var_view, LN_EPS, None, ALU.add),
           reads=[kmv], writes=[krstd])
        act(lambda q: q.activation(rstd_t[:, :], rstd_t[:, :], AF.Sqrt),
            reads=[krstd], writes=[krstd])
        ve(lambda q: q.reciprocal(rstd_t[:, :], rstd_t[:, :]),
           reads=[krstd], writes=[krstd])
        ve(lambda q: q.scalar_tensor_tensor(b_t[:, :], mean_view, -1.0,
                                            rstd_t[:, :], ALU.mult, ALU.mult),
           reads=[kmv, krstd], writes=[kb])
        ve(lambda q: q.tensor_tensor(xnm_t[:, :], xnm_t[:, :], rstd_t[:, :],
                                     ALU.mult),
           reads=[kxnm, krstd], writes=[kxnm])
        ve(lambda q: q.tensor_reduce(scbc[:, 15:16], xnm_t[:, :], axis=AX.X,
                                     op=ALU.max),
           reads=[kxnm], writes=[(tag, "xcol")])
        colmax_to_row(scbc[:, 15:16], (tag, "xcol"), 0)
        collective_max(1)
        scale_pair(0, (0, 1))
        bcast_row(scrow[0:1, 0:2], scbc[:, SC_A:SC_A + 2],
                  [("scrow", 0), ("scrow", 1)], ["scbcA"], 2)
        if upto == "A":
            return

        with contextlib.ExitStack() as st2:
            wupT = st2.enter_context(nc.sbuf_tensor(f"{tag}BwU", [128, DC * F], BF16))
            wgateT = st2.enter_context(nc.sbuf_tensor(f"{tag}BwG", [128, DC * F], BF16))
            xba = [st2.enter_context(nc.sbuf_tensor(f"{tag}Bxa{i}", [128, D], F32))
                   for i in range(2)]
            xn = [st2.enter_context(nc.sbuf_tensor(f"{tag}Bxn{i}", [128, D], F32))
                  for i in range(2)]
            xq = [st2.enter_context(nc.sbuf_tensor(f"{tag}Bxq{i}", [128, D], BF16))
                  for i in range(2)]
            xqT = [st2.enter_context(nc.sbuf_tensor(f"{tag}BxqT{i}", [128, DC * 512],
                                                    BF16)) for i in range(2)]
            evu = [st2.enter_context(nc.sbuf_tensor(f"{tag}Bevu{i}", [128, 4 * 512],
                                                    BF16)) for i in range(2)]
            evg = [st2.enter_context(nc.sbuf_tensor(f"{tag}Bevg{i}", [128, 4 * 512],
                                                    BF16)) for i in range(2)]
            upmaxs = st2.enter_context(nc.sbuf_tensor(f"{tag}Bum", [128, FC * NB], F32))
            gatemaxs = st2.enter_context(nc.sbuf_tensor(f"{tag}Bgm", [128, FC * NB],
                                                        F32))
            dma("sp", wupT[:, :], wupT_h.ap(), reads=[("wup", "spill")],
                writes=[(tag, "BwU")])
            dma("sp", wgateT[:, :], wgateT_h.ap(), reads=[("wgate", "spill")],
                writes=[(tag, "BwG")])
            def b_prep(blk):
                par = blk % 2
                kxqT = (tag, "BxqT", par)
                for ti in range(4):
                    t = blk * 4 + ti
                    tp = t % 2
                    xk = (tag, "Bxa", tp)
                    dma("sp", xba[tp][:, :], xin_h.ap()[t * 128:(t + 1) * 128, :],
                        reads=[kxin], writes=[xk])
                    act(lambda q, t=t, tp=tp: q.activation(
                        xn[tp][:, :], xba[tp][:, :], AF.Identity,
                        bias=b_t[:, t:t + 1], scale=rstd_t[:, t:t + 1]),
                        reads=[xk, krstd, kb], writes=[(tag, "Bxn", tp)])
                    ve(lambda q, tp=tp: q.tensor_scalar(
                        xn[tp][:, :], xn[tp][:, :], scbc[:, SC_A + 1:SC_A + 2],
                        MAGIC, ALU.mult, ALU.add),
                       reads=[(tag, "Bxn", tp), "scbcA"],
                       writes=[(tag, "Bxn", tp)])
                    ve(lambda q, tp=tp: q.tensor_scalar(
                        xq[tp][:, :], xn[tp][:, :], MAGIC,
                        scbc[:, SC_A:SC_A + 1], ALU.subtract, ALU.mult),
                       reads=[(tag, "Bxn", tp), "scbcA"],
                       writes=[(tag, "Bxq", tp)])
                    srcs = [xq[tp][:, dc * 128:(dc + 1) * 128] for dc in range(DC)]
                    bank, bkey = tr_batch(srcs, [(tag, "Bxq", tp)])
                    ve(lambda q, ti=ti, par=par, bank=bank: q.tensor_copy(
                        xqT[par][:, :].rearrange("p (d t) -> p d t", t=512)
                        [:, :, ti * 128:(ti + 1) * 128],
                        bank[:, :].rearrange("p (d t) -> p d t", t=128)),
                       reads=[bkey], writes=[kxqT])
            def b_mms(blk):
                par = blk % 2
                kxqT = (tag, "BxqT", par)
                for (wT, kwT, scr_h, ev2, kev, maxsl, kmaxsl) in (
                        (wupT, (tag, "BwU"), up_s, evu, (tag, "Bevu"), upmaxs,
                         "upmaxs"),
                        (wgateT, (tag, "BwG"), gate_s, evg, (tag, "Bevg"),
                         gatemaxs, "gatemaxs")):
                    for fc in range(FC):
                        bank, bkey = next_bank()
                        p.mm_group(
                            [(lambda q, s, e, dc=dc2, bank=bank, wT=wT, fc=fc,
                              par=par, xqT=xqT: q.matmul(
                                  bank[:, 0:512],
                                  wT[:, dc * F + fc * 128: dc * F + fc * 128 + 128],
                                  xqT[par][:, dc * 512:(dc + 1) * 512],
                                  start=s, stop=e))
                             for dc2 in range(DC)],
                            reads=[kwT, kxqT], writes=[bkey])
                        gi = (fc // 4) % 2
                        sl = fc % 4
                        act(lambda q, bank=bank, ev2=ev2, gi=gi, sl=sl: q.copy(
                            ev2[gi][:, sl * 512:(sl + 1) * 512], bank[:, :]),
                            reads=[bkey], writes=[(kev, gi, sl)])
                        ve(lambda q, bank=bank, maxsl=maxsl, col=blk * FC + fc:
                           q.tensor_reduce(maxsl[:, col:col + 1], bank[:, :],
                                           axis=AX.X, op=ALU.max,
                                           apply_absolute_value=True),
                           reads=[bkey], writes=[kmaxsl])
                        if sl == 3:
                            dma("g",
                                scr_h.ap()[fc - 3:fc + 1, :,
                                           blk * 512:(blk + 1) * 512]
                                .rearrange("f p t -> p f t"),
                                ev2[gi][:, :].rearrange("p (f t) -> p f t", t=512),
                                reads=[(kev, gi, 0), (kev, gi, 1), (kev, gi, 2),
                                       (kev, gi, 3)],
                                writes=[scr_h.name])

            b_prep(0)
            for blk in range(NB):
                if blk + 1 < NB:
                    b_prep(blk + 1)
                b_mms(blk)
            ve(lambda q, upmaxs=upmaxs: q.tensor_reduce(
                scbc[:, 15:16], upmaxs[:, :], axis=AX.X,
                op=ALU.max), reads=["upmaxs"],
               writes=[(tag, "ucol")])
            colmax_to_row(scbc[:, 15:16], (tag, "ucol"), 0)
            ve(lambda q, gatemaxs=gatemaxs: q.tensor_reduce(
                scbc[:, 15:16], gatemaxs[:, :], axis=AX.X,
                op=ALU.max), reads=["gatemaxs"],
               writes=[(tag, "gcol")])
            colmax_to_row(scbc[:, 15:16], (tag, "gcol"), 1)
            p.full_barrier()

        collective_max(2)
        scale_pair(0, (2, 3))
        scale_pair(1, (4, 5))
        ve(lambda q: q.tensor_scalar(scrow[0:1, 6:7], scrow[0:1, 4:5],
                                     -MAGIC, None, ALU.mult),
           reads=[("scrow", 4)], writes=[("scrow", 6)])
        bcast_row(scrow[0:1, 2:7], scbc[:, SC_B:SC_B + 5],
                  [("scrow", c) for c in (2, 3, 4, 5, 6)], ["scbcB"], 5)
        if upto == "B":
            return

        HP = N // 2
        with contextlib.ExitStack() as st2:
            gtp = [st2.enter_context(nc.sbuf_tensor(f"{tag}Cg{i}", [128, HP], BF16))
                   for i in range(2)]
            ct1 = [st2.enter_context(nc.sbuf_tensor(f"{tag}Ct1{i}", [128, HP], F32))
                   for i in range(2)]
            ct2 = [st2.enter_context(nc.sbuf_tensor(f"{tag}Ct2{i}", [128, HP], F32))
                   for i in range(2)]
            ct3 = [st2.enter_context(nc.sbuf_tensor(f"{tag}Ct3{i}", [128, HP], F32))
                   for i in range(2)]
            m1p = [st2.enter_context(nc.sbuf_tensor(f"{tag}Cm{i}", [128, HP], BF16))
                   for i in range(2)]
            m1maxs = st2.enter_context(nc.sbuf_tensor(f"{tag}Cmm", [128, 2 * FC],
                                                      F32))
            def c_front(idx):
                fc, h = idx // 2, idx % 2
                par = idx % 2
                kg = (tag, "Cg", par)
                dma("sp", gtp[par][:, :], gate_s.ap()[fc, :, h * HP:(h + 1) * HP],
                    reads=[gate_s.name], writes=[kg])
                act(lambda q, par=par: q.activation(
                    ct1[par][:, :], gtp[par][:, :], AF.Identity,
                    scale=scbc[:, SC_B + 3:SC_B + 4], bias=mgcol[:, :]),
                    reads=[kg, "scbcB", "mgcol"], writes=[(tag, "Ct1", par)])

            def c_rest(idx):
                fc, h = idx // 2, idx % 2
                par = idx % 2
                km = (tag, "Cm", par)
                act(lambda q, par=par: q.activation(
                    ct3[par][:, :], ct1[par][:, :], AF.Sigmoid,
                    scale=scbc[:, SC_B + 2:SC_B + 3],
                    bias=scbc[:, SC_NM:SC_NM + 1]),
                    reads=[(tag, "Ct1", par), "scbcB"], writes=[(tag, "Ct3", par)])
                act(lambda q, par=par: q.activation(
                    ct2[par][:, :], ct3[par][:, :], AF.Identity,
                    scale=65536.0, bias=mgcol[:, :]),
                    reads=[(tag, "Ct3", par), "mgcol"], writes=[(tag, "Ct2", par)])
                ve(lambda q, par=par: q.tensor_scalar(
                    ct3[par][:, :], ct2[par][:, :], MAGIC,
                    SIGCLIP, ALU.subtract, ALU.min),
                   reads=[(tag, "Ct2", par)], writes=[(tag, "Ct3", par)])
                ve(lambda q, par=par: q.scalar_tensor_tensor(
                    m1p[par][:, :], ct1[par][:, :], MAGIC, ct3[par][:, :],
                    ALU.subtract, ALU.mult),
                   reads=[(tag, "Ct1", par), (tag, "Ct3", par)], writes=[km])
                ve(lambda q, par=par, idx=idx, m1maxs=m1maxs, m1p=m1p:
                   q.tensor_reduce(m1maxs[:, idx:idx + 1], m1p[par][:, :],
                                   axis=AX.X, op=ALU.max,
                                   apply_absolute_value=True),
                   reads=[km], writes=["m1maxs"])
                dma("g", m1_s.ap()[fc, :, h * HP:(h + 1) * HP], m1p[par][:, :],
                    reads=[km], writes=[m1_s.name])

            c_front(0)
            c_front(1)
            for idx in range(2 * FC):
                c_rest(idx)
                if idx + 2 < 2 * FC:
                    c_front(idx + 2)
            ve(lambda q, m1maxs=m1maxs: q.tensor_reduce(
                scbc[:, 15:16], m1maxs[:, :], axis=AX.X,
                op=ALU.max), reads=["m1maxs"],
               writes=[(tag, "mcol")])
            colmax_to_row(scbc[:, 15:16], (tag, "mcol"), 0)
            p.full_barrier()

        collective_max(1)
        ve(lambda q: q.tensor_tensor(scrow[0:1, 0:1], scrow[0:1, 0:1],
                                     scrow[0:1, 4:5], ALU.mult),
           reads=[("scrow", 0), ("scrow", 4)], writes=[("scrow", 0)])
        scale_pair(0, (0, 1), pre=INV65536)
        ve(lambda q: q.tensor_scalar(scrow[0:1, 2:3], scrow[0:1, 1:2],
                                     scrow[0:1, 4:5], INV65536, ALU.mult,
                                     ALU.mult),
           reads=[("scrow", 1), ("scrow", 4)], writes=[("scrow", 2)])
        bcast_row(scrow[0:1, 0:3], scbc[:, SC_C:SC_C + 3],
                  [("scrow", 0), ("scrow", 1), ("scrow", 2)], ["scbcC"], 3)
        if upto == "C":
            return

        with contextlib.ExitStack() as st2:
            m1d = [st2.enter_context(nc.sbuf_tensor(f"{tag}Dm{i}", [128, HP], BF16))
                   for i in range(2)]
            upd = [st2.enter_context(nc.sbuf_tensor(f"{tag}Du{i}", [128, HP], BF16))
                   for i in range(2)]
            dt1 = [st2.enter_context(nc.sbuf_tensor(f"{tag}Dt1{i}", [128, HP], F32))
                   for i in range(2)]
            dt2 = [st2.enter_context(nc.sbuf_tensor(f"{tag}Dt2{i}", [128, HP], F32))
                   for i in range(2)]
            dt3 = [st2.enter_context(nc.sbuf_tensor(f"{tag}Dt3{i}", [128, HP], BF16))
                   for i in range(2)]
            omaxs = st2.enter_context(nc.sbuf_tensor(f"{tag}Dom", [128, 2 * FC],
                                                     F32))
            def d_front(idx):
                fc, h = idx // 2, idx % 2
                par = idx % 2
                km = (tag, "Dm", par)
                ku = (tag, "Du", par)
                dma("sp", m1d[par][:, :], m1_s.ap()[fc, :, h * HP:(h + 1) * HP],
                    reads=[m1_s.name], writes=[km])
                dma("sp", upd[par][:, :], up_s.ap()[fc, :, h * HP:(h + 1) * HP],
                    reads=[up_s.name], writes=[ku])
                act(lambda q, par=par: q.activation(
                    dt1[par][:, :], m1d[par][:, :], AF.Identity,
                    scale=scbc[:, SC_C + 2:SC_C + 3], bias=mgcol[:, :]),
                    reads=[km, "scbcC", "mgcol"], writes=[(tag, "Dt1", par)])
                act(lambda q, par=par: q.activation(
                    dt2[par][:, :], upd[par][:, :], AF.Identity,
                    scale=scbc[:, SC_B + 1:SC_B + 2], bias=mgcol[:, :]),
                    reads=[ku, "scbcB", "mgcol"], writes=[(tag, "Dt2", par)])

            def d_rest(idx):
                fc, h = idx // 2, idx % 2
                par = idx % 2
                ve(lambda q, par=par: q.tensor_scalar(
                    dt1[par][:, :], dt1[par][:, :], MAGIC, scbc[:, SC_C:SC_C + 1],
                    ALU.subtract, ALU.mult),
                   reads=[(tag, "Dt1", par), "scbcC"], writes=[(tag, "Dt1", par)])
                p.emit("g", lambda q, par=par: q.tensor_scalar(
                    dt2[par][:, :], dt2[par][:, :], -MAGIC, scbc[:, SC_B:SC_B + 1],
                    ALU.add, ALU.mult),
                       reads=[(tag, "Dt2", par), "scbcB"],
                       writes=[(tag, "Dt2", par)])
                eng_o = "g" if (idx % 2 == 0) else "ve"
                p.emit(eng_o, lambda q, par=par: q.tensor_tensor(
                    dt3[par][:, :], dt1[par][:, :], dt2[par][:, :], ALU.mult),
                       reads=[(tag, "Dt1", par), (tag, "Dt2", par)],
                       writes=[(tag, "Dt3", par)])
                ve(lambda q, idx=idx, par=par, omaxs=omaxs: q.tensor_reduce(
                    omaxs[:, idx:idx + 1], dt3[par][:, :], axis=AX.X, op=ALU.max,
                    apply_absolute_value=True),
                   reads=[(tag, "Dt3", par)], writes=["omaxs"])
                ng_h = HP // 256
                dma("g",
                    o_s.ap()[h * ng_h:(h + 1) * ng_h, :, fc * 256:(fc + 1) * 256]
                    .rearrange("g p t -> p g t"),
                    dt3[par][:, :].rearrange("p (g t) -> p g t", t=256),
                    reads=[(tag, "Dt3", par)], writes=[o_s.name])

            d_front(0)
            d_front(1)
            for idx in range(2 * FC):
                d_rest(idx)
                if idx + 2 < 2 * FC:
                    d_front(idx + 2)
            ve(lambda q, omaxs=omaxs: q.tensor_reduce(
                scbc[:, 15:16], omaxs[:, :], axis=AX.X,
                op=ALU.max), reads=["omaxs"],
               writes=[(tag, "ocol")])
            colmax_to_row(scbc[:, 15:16], (tag, "ocol"), 0)
            p.full_barrier()

        collective_max(1)
        scale_pair(0, (0, 1))
        ve(lambda q: q.tensor_scalar(scrow[0:1, 2:3], scrow[0:1, 0:1], 0.5,
                                     None, ALU.mult),
           reads=[("scrow", 0)], writes=[("scrow", 2)])
        ve(lambda q: q.tensor_copy(scrow[0:1, 3:4], scrow[0:1, 1:2]),
           reads=[("scrow", 1)], writes=[("scrow", 3)])
        bcast_row(scrow[0:1, 2:4], scbc[:, SC_D:SC_D + 2],
                  [("scrow", 2), ("scrow", 3)], ["scbcD"], 2)
        if upto == "D":
            return

        EG = 256
        NEG = N // EG
        W = FC * EG
        with contextlib.ExitStack() as st2:
            wdT = st2.enter_context(nc.sbuf_tensor(f"{tag}EwD", [128, FC * D], BF16))
            oe = [st2.enter_context(nc.sbuf_tensor(f"{tag}Eo{i}", [128, W], BF16))
                  for i in range(2)]
            xe = [st2.enter_context(nc.sbuf_tensor(f"{tag}Ex{i}", [128, 2 * D], F32))
                  for i in range(2)]
            et1 = [st2.enter_context(nc.sbuf_tensor(f"{tag}Et1{i}", [128, W], F32))
                   for i in range(2)]
            oq = [st2.enter_context(nc.sbuf_tensor(f"{tag}Eoq{i}", [128, W], BF16))
                  for i in range(2)]
            xo = [st2.enter_context(nc.sbuf_tensor(f"{tag}Exo{i}", [128, 2 * D], F32))
                  for i in range(2)]
            ebn = st2.enter_context(nc.sbuf_tensor(f"{tag}Ebn", [128, 12], F32))
            dma("sp", wdT[:, :], wdownT_h.ap(), reads=[("wdown", "spill")],
                writes=[(tag, "EwD")])
            def e_front(g):
                par = g % 2
                ko = (tag, "Eo", par)
                kx = (tag, "Ex", par)
                dma("sp", oe[par][:, :], o_s.ap()[g, :, :],
                    reads=[o_s.name], writes=[ko])
                dma("sp", xe[par][:, :].rearrange("p (a b) -> p a b", b=D),
                    xin_h.ap()[g * EG:(g + 1) * EG, :]
                    .rearrange("(a t) b -> t a b", t=128),
                    reads=[kxin], writes=[kx])
                act(lambda q, par=par: q.activation(
                    et1[par][:, :], oe[par][:, :], AF.Identity,
                    scale=scbc[:, SC_D + 1:SC_D + 2], bias=mgcol[:, :]),
                    reads=[ko, "scbcD", "mgcol"], writes=[(tag, "Et1", par)])

            def e_rest(g):
                par = g % 2
                kx = (tag, "Ex", par)
                koq = (tag, "Eoq", par)
                kxo = (tag, "Exo", par)
                ve(lambda q, par=par: q.tensor_scalar(
                    oq[par][:, :], et1[par][:, :], MAGIC, scbc[:, SC_D:SC_D + 1],
                    ALU.subtract, ALU.mult),
                   reads=[(tag, "Et1", par), "scbcD"], writes=[koq])
                for tc in range(2):
                    banks = []
                    for hh in range(2):
                        bank, bkey = next_bank()
                        banks.append((bank, bkey))
                        p.mm_group(
                            [(lambda q, s, e, fc=fc2, bank=bank, hh=hh, par=par,
                              tc=tc, oq=oq: q.matmul(
                                  bank[:, 0:512],
                                  oq[par][:, fc * EG + tc * 128:
                                          fc * EG + tc * 128 + 128],
                                  wdT[:, fc * D + hh * 512:
                                      fc * D + hh * 512 + 512],
                                  start=s, stop=e))
                             for fc2 in range(FC)],
                            reads=[koq, (tag, "EwD")], writes=[bkey])
                    for hh, (bank, bkey) in enumerate(banks):
                        ve(lambda q, bank=bank, hh=hh, tc=tc, par=par:
                           q.tensor_tensor(
                               xo[par][:, tc * D + hh * 512:
                                       tc * D + hh * 512 + 512],
                               bank[:, 0:512],
                               xe[par][:, tc * D + hh * 512:
                                       tc * D + hh * 512 + 512],
                               ALU.add),
                           reads=[bkey, kx], writes=[kxo])
                dma("g",
                    xout_h.ap()[g * EG:(g + 1) * EG, :]
                    .rearrange("(a t) b -> t a b", t=128),
                    xo[par][:, :].rearrange("p (a b) -> p a b", b=D),
                    reads=[kxo], writes=[kxout])
                if dbg_x is not None:
                    dma("g",
                        dbg_x.ap()[g * EG:(g + 1) * EG, :]
                        .rearrange("(a t) b -> t a b", t=128),
                        xo[par][:, :].rearrange("p (a b) -> p a b", b=D),
                        reads=[kxo], writes=["dbgx" + tag])
                if fuse_stats_mv is not None:
                    fmv, kfmv = fuse_stats_mv
                    for tc in range(2):
                        tt2 = g * 2 + tc
                        ve(lambda q, par=par, tc=tc: q.bn_stats(
                            ebn[:, 0:6], xo[par][:, tc * D:tc * D + 512]),
                           reads=[kxo], writes=[(tag, "Ebn0")])
                        ve(lambda q, par=par, tc=tc: q.bn_stats(
                            ebn[:, 6:12], xo[par][:, tc * D + 512:tc * D + 1024]),
                           reads=[kxo], writes=[(tag, "Ebn1")])
                        ve(lambda q, tt2=tt2, fmv=fmv: q.bn_aggr(
                            fmv[:, 2 * tt2:2 * tt2 + 2], ebn[:, :]),
                           reads=[(tag, "Ebn0"), (tag, "Ebn1")],
                           writes=[(kfmv, tt2), kfmv])

            e_front(0)
            for g in range(NEG):
                e_rest(g)
                if g + 1 < NEG:
                    e_front(g + 1)
            p.full_barrier()


    def finish():
        p.full_barrier()
        with nc.Block() as block:
            p.build(block, stack)
        stack.close()
        return nc

    if stage == "prep":
        zo = sb("zot", [128, D], F32)
        ve(lambda q: q.memset(zo[:, :], 0.0), writes=["zot"])
        for t in range(TT):
            dma("sp", out_h.ap()[t * 128:(t + 1) * 128, :], zo[:, :],
                reads=["zot"], writes=["outw"])
        return finish()

    _upto = stage[2] if stage.startswith("f1") and len(stage) > 2 else "E"
    ffn_call("f1", x0_h, x1_h, mv1, rstd1, b1, xnm1, need_pass_a=True,
             fuse_stats_mv=(mva, "amv"), dbg_x=dbg.get("x1"), upto=_upto)
    if stage.startswith("f1"):
        zero_out_finish()
        return finish()
    for nm, scr in (("gate", gate_s), ("up", up_s), ("m1", m1_s)):
        if nm in dbg:
            for fc in range(FC):
                dma("g", dbg[nm].ap()[fc, :, :], scr.ap()[fc, :, :],
                    reads=[scr.name], writes=["dbg_" + nm])

    def attention():
        kx1, kx2 = x1_h.name, x2_h.name
        with contextlib.ExitStack() as st2:
            asb = lambda nm, sh, dt: st2.enter_context(nc.sbuf_tensor(nm, sh, dt))
            wqs = asb("at_wq", [128, DC * D], BF16)
            wks = asb("at_wk", [128, DC * D], BF16)
            wvs = asb("at_wv", [128, DC * D], BF16)
            wos = asb("at_wo", [128, DC * D], BF16)
            brow = asb("at_brow", [1, D], F32)
            bq_c = asb("at_bqc", [128, DC], F32)
            bk_c = asb("at_bkc", [128, DC], F32)
            bv_bc = asb("at_bvbc", [128, D], F32)
            bo_bc = asb("at_bobc", [128, D], F32)
            lam = asb("at_lam", [1, 4], F32)
            lnl_bc = asb("at_lnlbc", [128, 1], F32)
            lamC = asb("at_lamC", [128, 1], F32)
            lamrow = asb("at_lamrow", [1, CH], F32)
            lamr_bc = asb("at_lamrbc", [128, CH], F32)
            lamK = asb("at_lamK", [128, TPCH], F32)
            iot = asb("at_iot", [128, CH], F32)
            msk1 = asb("at_msk1", [128, CH], F32)
            msk2 = asb("at_msk2", [128, CH], F32)
            mask = asb("at_mask", [128, TPCH * CH], F32)
            mem = asb("at_mem", [128, DC * D], F32)
            memb = asb("at_memb", [128, DC * D], BF16)
            xat = [asb(f"at_xa{i}", [128, TPCH * D], F32) for i in range(2)]
            xnb = asb("at_xnb", [128, D], BF16)
            xnT = asb("at_xnT", [128, DC * CH], BF16)
            qT = asb("at_qT", [128, DC * CH], BF16)
            qs = [asb(f"at_qs{i}", [128, DC * CH], BF16) for i in range(2)]
            kT = asb("at_kT", [128, DC * CH], BF16)
            tmpA = asb("at_tmpA", [128, CH], F32)
            tmpB = asb("at_tmpB", [128, CH], F32)
            tmpC = asb("at_tmpC", [128, CH], F32)
            Kp = [asb(f"at_Kp{i}", [128, TPCH * D], BF16) for i in range(2)]
            Vs = [asb(f"at_V{i}", [128, TPCH * D], BF16) for i in range(2)]
            scm = [asb(f"at_scm{i}", [128, TPCH * CH], BF16) for i in range(2)]
            aoT = asb("at_aoT", [128, DC * CH], BF16)
            xot = asb("at_xot", [128, D], F32)
            bnst = asb("at_bn", [128, 12], F32)

            for (w_h, ws, kws) in ((wq_h, wqs, "at_wq"), (wk_h, wks, "at_wk"),
                                   (wv_h, wvs, "at_wv"), (wo_h, wos, "at_wo")):
                dma("g", ws[:, :].rearrange("p (c d) -> p c d", d=D),
                    w_h.ap().rearrange("(c p) d -> p c d", p=128),
                    writes=[kws])
            for (b_h, dst, kdst) in ((bq_h, bq_c, "at_bqc"), (bk_h, bk_c, "at_bkc")):
                dma("sp", brow[:, :], b_h.ap(), writes=["at_brow"])
                for ec in range(DC):
                    p.mm_group([lambda q, s, e, ec=ec: q.transpose(
                        pms[0:128, 0:1], brow[0:1, ec * 128:(ec + 1) * 128],
                        idf[0:1, 0:1])],
                        reads=["at_brow", K_IDF], writes=["pms"])
                    ve(lambda q, ec=ec, dst=dst: q.tensor_copy(
                        dst[:, ec:ec + 1], pms[:, 0:1]),
                       reads=["pms"], writes=[kdst])
            for (b_h, dst, kdst) in ((bv_h, bv_bc, "at_bvbc"), (bo_h, bo_bc, "at_bobc")):
                dma("sp", brow[:, :], b_h.ap(), writes=["at_brow"])
                for h in range(2):
                    bcast_row(brow[0:1, h * 512:(h + 1) * 512],
                              dst[:, h * 512:(h + 1) * 512], ["at_brow"], [kdst], 512)
            dma("sp", lam[0:1, 0:1], dec_h.ap(), writes=["at_lam"])
            act(lambda q: q.activation(lam[0:1, 1:2], lam[0:1, 0:1], AF.Sigmoid),
                reads=["at_lam"], writes=[("at_lam", 1)])
            ve(lambda q: q.tensor_scalar(lam[0:1, 2:3], lam[0:1, 1:2],
                                         float(2.0 ** -60), None, ALU.max),
               reads=[("at_lam", 1)], writes=[("at_lam", 2)])
            act(lambda q: q.activation(lam[0:1, 3:4], lam[0:1, 2:3], AF.Ln),
                reads=[("at_lam", 2)], writes=[("at_lam", 3)])
            bcast_row(lam[0:1, 3:4], lnl_bc[:, :], [("at_lam", 3)], ["at_lnlbc"], 1)
            act(lambda q: q.activation(lamC[:, :], lnl_bc[:, :], AF.Exp,
                                       scale=float(CH)),
                reads=["at_lnlbc"], writes=["at_lamC"])
            p.emit("g", lambda q: q.iota(lamrow[0:1, :], pattern=[[1, CH]], base=0,
                                         channel_multiplier=0,
                                         allow_small_or_imprecise_dtypes=True),
                   writes=["at_lamrow"])
            act(lambda q: q.activation(lamrow[0:1, :], lamrow[0:1, :], AF.Exp,
                                       scale=lam[0:1, 3:4]),
                reads=["at_lamrow", ("at_lam", 3)], writes=["at_lamrow"])
            bcast_row(lamrow[0:1, :], lamr_bc[:, :], ["at_lamrow"], ["at_lamrbc"], CH)
            p.emit("g", lambda q: q.iota(iot[:, 0:TPCH], pattern=[[-128, TPCH]],
                                         base=CH - 1, channel_multiplier=-1,
                                         allow_small_or_imprecise_dtypes=True),
                   writes=["at_iot"])
            act(lambda q: q.activation(lamK[:, :], iot[:, 0:TPCH], AF.Exp,
                                       scale=lnl_bc[:, 0:1]),
                reads=["at_iot", "at_lnlbc"], writes=["at_lamK"])
            for jc in range(TPCH):
                p.emit("g", lambda q, jc=jc: q.iota(
                    iot[:, :], pattern=[[1, CH]], base=-1 - jc * 128,
                    channel_multiplier=-1,
                    allow_small_or_imprecise_dtypes=True),
                    writes=["at_iot"])
                act(lambda q: q.activation(msk1[:, :], iot[:, :], AF.Relu),
                    reads=["at_iot"], writes=["at_msk1"])
                act(lambda q: q.activation(msk2[:, :], iot[:, :], AF.Relu,
                                           bias=onecol[:, :]),
                    reads=["at_iot", "onecol"], writes=["at_msk2"])
                ve(lambda q: q.tensor_tensor(msk2[:, :], msk2[:, :], msk1[:, :],
                                             ALU.subtract),
                   reads=["at_msk1", "at_msk2"], writes=["at_msk2"])
                act(lambda q: q.activation(msk1[:, :], msk1[:, :], AF.Exp,
                                           scale=lnl_bc[:, 0:1]),
                    reads=["at_msk1", "at_lnlbc"], writes=["at_msk1"])
                ve(lambda q, jc=jc: q.tensor_tensor(
                    mask[:, jc * CH:(jc + 1) * CH], msk1[:, :], msk2[:, :],
                    ALU.mult),
                   reads=["at_msk1", "at_msk2"], writes=["at_mask"])

            var_view = mva[:, :].rearrange("p (t two) -> p t two", two=2)[:, :, 1]
            mean_view = mva[:, :].rearrange("p (t two) -> p t two", two=2)[:, :, 0]
            ve(lambda q: q.tensor_scalar(rstda[:, :], var_view, LN_EPS, None,
                                         ALU.add), reads=["amv"], writes=["arstd"])
            act(lambda q: q.activation(rstda[:, :], rstda[:, :], AF.Sqrt),
                reads=["arstd"], writes=["arstd"])
            ve(lambda q: q.reciprocal(rstda[:, :], rstda[:, :]),
               reads=["arstd"], writes=["arstd"])
            ve(lambda q: q.scalar_tensor_tensor(ba[:, :], mean_view, -1.0,
                                                rstda[:, :], ALU.mult, ALU.mult),
               reads=["amv", "arstd"], writes=["ab"])

            def stage1(ci):
                s, c = divmod(ci, NCH)
                par = ci % 2
                base = s * SEQ + c * CH
                kxa = ("at_xa", par)
                kqs, kKp, kV, kscm = (("at_qs", par), ("at_Kp", par),
                                      ("at_V", par), ("at_scm", par))
                for tc in range(TPCH):
                    t = base // 128 + tc
                    dma("sp", xat[par][:, tc * D:(tc + 1) * D],
                        x1_h.ap()[t * 128:(t + 1) * 128, :],
                        reads=[kx1], writes=[kxa])
                    act(lambda q, t=t, tc=tc: q.activation(
                        xnb[:, :], xat[par][:, tc * D:(tc + 1) * D], AF.Identity,
                        bias=ba[:, t:t + 1], scale=rstda[:, t:t + 1]),
                        reads=[kxa, "arstd", "ab"], writes=["at_xnb"])
                    srcs = [xnb[:, dc * 128:(dc + 1) * 128] for dc in range(DC)]
                    bank, bkey = tr_batch(srcs, ["at_xnb"])
                    ve(lambda q, tc=tc, bank=bank: q.tensor_copy(
                        xnT[:, :].rearrange("p (d t) -> p d t", t=CH)
                        [:, :, tc * 128:(tc + 1) * 128],
                        bank[:, :].rearrange("p (d t) -> p d t", t=128)),
                       reads=[bkey], writes=["at_xnT"])
                for ec in range(DC):
                    bank, bkey = next_bank()
                    p.mm_group(
                        [(lambda q, s2, e2, dc=dc2, ec=ec, bank=bank: q.matmul(
                            bank[:, 0:CH],
                            wqs[:, dc * D + ec * 128: dc * D + ec * 128 + 128],
                            xnT[:, dc * CH:(dc + 1) * CH], start=s2, stop=e2))
                         for dc2 in range(DC)],
                        reads=["at_wq", "at_xnT"], writes=[bkey])
                    act(lambda q, ec=ec, bank=bank: q.activation(
                        qT[:, ec * CH:(ec + 1) * CH], bank[:, 0:CH], AF.Identity,
                        bias=bq_c[:, ec:ec + 1]),
                        reads=[bkey, "at_bqc"], writes=["at_qT"])
                ve(lambda q, par=par: q.tensor_tensor(
                    qs[par][:, :].rearrange("p (e i) -> p e i", i=CH),
                    qT[:, :].rearrange("p (e i) -> p e i", i=CH),
                    lamr_bc[:, :].rearrange("p (o i) -> p o i", o=1)
                    .broadcast_to([128, DC, CH]),
                    ALU.mult),
                   reads=["at_qT", "at_lamrbc"], writes=[kqs])
                for ec in range(DC):
                    bank, bkey = next_bank()
                    p.mm_group(
                        [(lambda q, s2, e2, dc=dc2, ec=ec, bank=bank: q.matmul(
                            bank[:, 0:CH],
                            wks[:, dc * D + ec * 128: dc * D + ec * 128 + 128],
                            xnT[:, dc * CH:(dc + 1) * CH], start=s2, stop=e2))
                         for dc2 in range(DC)],
                        reads=["at_wk", "at_xnT"], writes=[bkey])
                    act(lambda q, ec=ec, bank=bank: q.activation(
                        tmpA[:, :], bank[:, 0:CH], AF.Relu,
                        bias=bk_c[:, ec:ec + 1]),
                        reads=[bkey, "at_bkc"], writes=["at_tmpA"])
                    ve(lambda q, ec=ec, bank=bank: q.tensor_scalar(
                        tmpB[:, :], bank[:, 0:CH], bk_c[:, ec:ec + 1], 0.0,
                        ALU.add, ALU.min),
                       reads=[bkey, "at_bkc"], writes=["at_tmpB"])
                    act(lambda q: q.activation(tmpC[:, :], tmpB[:, :], AF.Exp),
                        reads=["at_tmpB"], writes=["at_tmpC"])
                    ve(lambda q, ec=ec: q.tensor_tensor(
                        kT[:, ec * CH:(ec + 1) * CH], tmpA[:, :], tmpC[:, :],
                        ALU.add),
                       reads=["at_tmpA", "at_tmpC"], writes=["at_kT"])
                for tc in range(TPCH):
                    for h in range(2):
                        bank, bkey = next_bank()
                        p.mm_group(
                            [(lambda q, s2, e2, dc=dc2, tc=tc, h=h, bank=bank:
                              q.matmul(
                                  bank[:, 0:512],
                                  xnT[:, dc * CH + tc * 128: dc * CH + tc * 128 + 128],
                                  wvs[:, dc * D + h * 512: dc * D + h * 512 + 512],
                                  start=s2, stop=e2))
                             for dc2 in range(DC)],
                            reads=["at_wv", "at_xnT"], writes=[bkey])
                        ve(lambda q, tc=tc, h=h, par=par, bank=bank: q.tensor_tensor(
                            Vs[par][:, tc * D + h * 512: tc * D + h * 512 + 512],
                            bank[:, 0:512], bv_bc[:, h * 512:(h + 1) * 512],
                            ALU.add),
                           reads=[bkey, "at_bvbc"], writes=[kV])
                for tc in range(TPCH):
                    srcs = [kT[:, ec * CH + tc * 128: ec * CH + tc * 128 + 128]
                            for ec in range(DC)]
                    bank, bkey = tr_batch(srcs, ["at_kT"])
                    act(lambda q, tc=tc, par=par, bank=bank: q.activation(
                        Kp[par][:, tc * D:(tc + 1) * D], bank[:, :],
                        AF.Identity, scale=lamK[:, tc:tc + 1]),
                        reads=[bkey, "at_lamK"], writes=[kKp])
                for jc in range(TPCH):
                    bank, bkey = next_bank()
                    p.mm_group(
                        [(lambda q, s2, e2, ec=ec2, jc=jc, bank=bank: q.matmul(
                            bank[:, 0:CH],
                            kT[:, ec * CH + jc * 128: ec * CH + jc * 128 + 128],
                            qT[:, ec * CH:(ec + 1) * CH], start=s2, stop=e2))
                         for ec2 in range(DC)],
                        reads=["at_kT", "at_qT"], writes=[bkey])
                    ve(lambda q, jc=jc, par=par, bank=bank: q.tensor_tensor(
                        scm[par][:, jc * CH:(jc + 1) * CH], bank[:, 0:CH],
                        mask[:, jc * CH:(jc + 1) * CH], ALU.mult),
                       reads=[bkey, "at_mask"], writes=[kscm])

            def stage2(ci):
                s, c = divmod(ci, NCH)
                par = ci % 2
                base = s * SEQ + c * CH
                kqs, kKp, kV, kscm = (("at_qs", par), ("at_Kp", par),
                                      ("at_V", par), ("at_scm", par))
                kxa = ("at_xa", par)
                first = (c == 0)
                last = (c == NCH - 1)
                for vc in range(DC):
                    bank, bkey = next_bank()
                    mms = [(lambda q, s2, e2, jc=jc2, vc=vc, par=par, bank=bank:
                            q.matmul(
                                bank[:, 0:CH],
                                Vs[par][:, jc * D + vc * 128: jc * D + vc * 128 + 128],
                                scm[par][:, jc * CH:(jc + 1) * CH],
                                start=s2, stop=e2))
                           for jc2 in range(TPCH)]
                    if not first:
                        mms += [(lambda q, s2, e2, ec=ec2, vc=vc, par=par,
                                 bank=bank: q.matmul(
                                     bank[:, 0:CH],
                                     memb[:, ec * D + vc * 128: ec * D + vc * 128 + 128],
                                     qs[par][:, ec * CH:(ec + 1) * CH],
                                     start=s2, stop=e2))
                                for ec2 in range(DC)]
                    p.mm_group(mms, reads=[kV, kscm, "at_memb", kqs],
                               writes=[bkey])
                    act(lambda q, vc=vc, bank=bank: q.copy(
                        aoT[:, vc * CH:(vc + 1) * CH], bank[:, 0:CH]),
                        reads=[bkey], writes=["at_aoT"])
                if not last:
                    for ec in range(DC):
                        for h in range(2):
                            bank, bkey = next_bank()
                            p.mm_group(
                                [(lambda q, s2, e2, tc=tc2, ec=ec, h=h, par=par,
                                  bank=bank: q.matmul(
                                      bank[:, 0:512],
                                      Kp[par][:, tc * D + ec * 128: tc * D + ec * 128 + 128],
                                      Vs[par][:, tc * D + h * 512: tc * D + h * 512 + 512],
                                      start=s2, stop=e2))
                                 for tc2 in range(TPCH)],
                                reads=[kKp, kV], writes=[bkey])
                            off = ec * D + h * 512
                            if first:
                                ve(lambda q, off=off, bank=bank: q.tensor_copy(
                                    mem[:, off:off + 512], bank[:, 0:512]),
                                   reads=[bkey], writes=["at_mem"])
                            else:
                                ve(lambda q, off=off, bank=bank:
                                   q.scalar_tensor_tensor(
                                       mem[:, off:off + 512], mem[:, off:off + 512],
                                       lamC[:, 0:1], bank[:, 0:512],
                                       ALU.mult, ALU.add),
                                   reads=[bkey, "at_lamC", "at_mem"],
                                   writes=["at_mem"])
                        p.emit("g", lambda q, ec=ec: q.tensor_copy(
                            memb[:, ec * D:(ec + 1) * D],
                            mem[:, ec * D:(ec + 1) * D]),
                               reads=["at_mem"], writes=["at_memb"])

                for ic in range(TPCH):
                    t = base // 128 + ic
                    banks = []
                    for h in range(2):
                        bank, bkey = next_bank()
                        banks.append((bank, bkey))
                        p.mm_group(
                            [(lambda q, s2, e2, vc=vc2, ic=ic, h=h, bank=bank:
                              q.matmul(
                                  bank[:, 0:512],
                                  aoT[:, vc * CH + ic * 128: vc * CH + ic * 128 + 128],
                                  wos[:, vc * D + h * 512: vc * D + h * 512 + 512],
                                  start=s2, stop=e2))
                             for vc2 in range(DC)],
                            reads=["at_aoT", "at_wo"], writes=[bkey])
                    for h, (bank, bkey) in enumerate(banks):
                        ve(lambda q, h=h, bank=bank: q.tensor_tensor(
                            xot[:, h * 512:(h + 1) * 512], bank[:, 0:512],
                            bo_bc[:, h * 512:(h + 1) * 512], ALU.add),
                           reads=[bkey, "at_bobc"], writes=["at_xot"])
                    ve(lambda q, ic=ic, par=par: q.tensor_tensor(
                        xot[:, :], xot[:, :], xat[par][:, ic * D:(ic + 1) * D],
                        ALU.add),
                       reads=["at_xot", kxa], writes=["at_xot"])
                    dma("g", x2_h.ap()[t * 128:(t + 1) * 128, :], xot[:, :],
                        reads=["at_xot"], writes=[kx2])
                    if "x2" in dbg:
                        dma("g", dbg["x2"].ap()[t * 128:(t + 1) * 128, :],
                            xot[:, :], reads=["at_xot"], writes=["dbg_x2w"])
                    ve(lambda q: q.bn_stats(bnst[:, 0:6], xot[:, 0:512]),
                       reads=["at_xot"], writes=["at_bn0"])
                    ve(lambda q: q.bn_stats(bnst[:, 6:12], xot[:, 512:1024]),
                       reads=["at_xot"], writes=["at_bn1"])
                    ve(lambda q, t=t: q.bn_aggr(mv2[:, 2 * t:2 * t + 2],
                                                bnst[:, :]),
                       reads=["at_bn0", "at_bn1"], writes=[("f2mv", t), "f2mv"])
                    ve(lambda q, t=t: q.tensor_scalar(
                        ntmp[:, :], mv2[:, 2 * t:2 * t + 1], -1.0, None,
                        ALU.mult),
                       reads=[("f2mv", t)], writes=["ntmp"])
                    act(lambda q, t=t: q.activation(
                        tmpD[:, :], xot[:, :], AF.Identity, bias=ntmp[:, :],
                        scale=1.0),
                        reads=["at_xot", "ntmp"], writes=["at_tmpD"])
                    ve(lambda q, t=t: q.tensor_reduce(
                        xnm2[:, t:t + 1], tmpD[:, :], axis=AX.X, op=ALU.max,
                        apply_absolute_value=True),
                       reads=["at_tmpD"], writes=["f2xnm"])


            tmpD = asb("at_tmpD", [128, D], F32)
            total = SPC * NCH
            stage1(0)
            for ci in range(total):
                if ci + 1 < total:
                    stage1(ci + 1)
                stage2(ci)
            p.full_barrier()


    if stage == "ffn1":
        zo = sb("zot", [128, D], F32)
        ve(lambda q: q.memset(zo[:, :], 0.0), writes=["zot"])
        for t in range(TT):
            dma("sp", out_h.ap()[t * 128:(t + 1) * 128, :], zo[:, :],
                reads=["zot"], writes=["outw"])
        return finish()

    attention()

    if stage == "attn":
        zo = sb("zot", [128, D], F32)
        ve(lambda q: q.memset(zo[:, :], 0.0), writes=["zot"])
        for t in range(TT):
            dma("sp", out_h.ap()[t * 128:(t + 1) * 128, :], zo[:, :],
                reads=["zot"], writes=["outw"])
        return finish()

    ffn_call("f2", x2_h, out_h, mv2, rstd2, b2, xnm2, need_pass_a=False,
             fuse_stats_mv=None, dbg_x=None)

    if "scal" in dbg:
        dma("g", dbg["scal"].ap()[0:1, 0:8], scrow[0:1, 0:8],
            reads=["scrow"], writes=["dbgscal"])

    return finish()


_NC_CACHE = {}


def _get_nc(debug_out=(), stage="full"):
    key = (tuple(sorted(debug_out)), stage)
    if key not in _NC_CACHE:
        _NC_CACHE[key] = build_nc(debug_out=key[0], stage=stage)
    return _NC_CACHE[key]


def _in_maps(inputs):
    f32 = lambda a: np.ascontiguousarray(np.asarray(a, dtype=np.float32))
    x = f32(inputs["L0_attn_x"]).reshape(NCORE * SPC, SEQ, D)
    maps = []
    for c in range(NCORE):
        m = {
            "x": x[c * SPC:(c + 1) * SPC].reshape(N, D),
            "Wq": f32(inputs["L0_attn_Wq"]),
            "Wk": f32(inputs["L0_attn_Wk"]),
            "Wv": f32(inputs["L0_attn_Wv"]),
            "Wo": f32(inputs["L0_attn_Wo"]),
            "bq": f32(inputs["L0_attn_bq"]).reshape(1, D),
            "bk": f32(inputs["L0_attn_bk"]).reshape(1, D),
            "bv": f32(inputs["L0_attn_bv"]).reshape(1, D),
            "bo": f32(inputs["L0_attn_bo"]).reshape(1, D),
            "decay": f32(inputs["L0_attn_decay_param"]).reshape(1, 1),
            "w_gate": f32(inputs["L0_ffn_w_gate"]),
            "w_up": f32(inputs["L0_ffn_w_up"]),
            "w_down": f32(inputs["L0_ffn_w_down"]),
        }
        maps.append(m)
    return maps


def kernel(**inputs):
    nc = _get_nc()
    res = run_bass_kernel_spmd(nc, _in_maps(inputs), core_ids=list(range(NCORE)))
    out = np.concatenate([r["out"].reshape(SPC, SEQ, D) for r in res.results],
                         axis=0)
    return out.astype(np.float32)


def kernel_debug(debug_out, stage="full", **inputs):
    nc = _get_nc(debug_out, stage)
    res = run_bass_kernel_spmd(nc, _in_maps(inputs), core_ids=list(range(NCORE)))
    return res


# version 09
# speedup vs baseline: 1.0376x; 1.0376x over previous
import contextlib
import numpy as np

import concourse.bass as bass
import concourse.mybir as mybir
from concourse.bass_utils import run_bass_kernel_spmd

F32 = mybir.dt.float32
BF16 = mybir.dt.bfloat16
AF = mybir.ActivationFunctionType
ALU = mybir.AluOpType
AX = mybir.AxisListType

MAGIC = float(np.float32(1.5 * 2 ** 23))
QMAX = 32767.0
WQMAX = 7.0
EPS_S = 1e-12
LN_EPS = 1e-5
SIGCLIP = float(np.float32(65535.0))
INV65536 = 1.0 / 65536.0

NCORE = 8
SEQ = 2048
D = 1024
F = 3072
SPC = 4
N = SPC * SEQ
TT = N // 128
NB = N // 512
DC = D // 128
FC = F // 128
CH = 256
NCH = SEQ // CH
TPCH = CH // 128

ENGINES = ("pe", "ve", "act", "sp", "g")


class Prog:

    DMA_POOL = {"sp": 12, "g": 4}

    def __init__(self, nc):
        self.nc = nc
        self.streams = {e: [] for e in ENGINES}
        self.cnt = {(e, "c"): 0 for e in ENGINES}
        for e, npool in self.DMA_POOL.items():
            for i in range(npool):
                self.cnt[(e + "_d", i)] = 0
        self.dma_idx = {e: 0 for e in self.DMA_POOL}
        self.sems = {}
        self.last_w = {}
        self.readers = {}
        self.watermark = {e: {} for e in ENGINES}
        self.n_cc = 0

    def _need(self, eng, keys_r, keys_w):
        need = {}
        for kk in keys_r:
            lw = self.last_w.get(kk)
            if lw is not None:
                need[lw[0]] = max(need.get(lw[0], 0), lw[1])
        for kk in keys_w:
            lw = self.last_w.get(kk)
            if lw is not None:
                need[lw[0]] = max(need.get(lw[0], 0), lw[1])
            for (ref, c) in self.readers.get(kk, ()):
                need[ref] = max(need.get(ref, 0), c)
        wm = self.watermark[eng]
        out = []
        for ref, c in need.items():
            if wm.get(ref, 0) >= c:
                continue
            out.append((ref, c))
            wm[ref] = c
        return out

    def _record(self, ref, mycnt, reads, writes):
        for kk in writes:
            self.last_w[kk] = (ref, mycnt)
            self.readers[kk] = []
        for kk in reads:
            self.readers.setdefault(kk, []).append((ref, mycnt))

    @staticmethod
    def _is_psum(k):
        return (isinstance(k, tuple) and k[0] in ("pmm", "ptrb")) or k == "pms"

    def _excl(self, reads, writes):
        pr = [k for k in reads if self._is_psum(k)]
        if not pr:
            return reads, writes
        return [k for k in reads if not self._is_psum(k)], list(writes) + pr

    def emit(self, eng, fn, reads=(), writes=(), kind="c"):
        reads, writes = self._excl(reads, writes)
        waits = self._need(eng, reads, writes)
        if kind == "d":
            npool = self.DMA_POOL[eng]
            si = self.dma_idx[eng] % npool
            self.dma_idx[eng] += 1
            ref = (eng + "_d", si)
            step = 16
            prev = self.cnt[ref]
            if prev > 0 and self.watermark[eng].get(ref, 0) < prev:
                waits = waits + [(ref, prev)]
                self.watermark[eng][ref] = prev
        else:
            ref = (eng, kind)
            step = 1
        self.cnt[ref] += step
        mycnt = self.cnt[ref]

        def thunk(q, waits=waits, fn=fn, ref=ref, step=step):
            for (r, c) in waits:
                q.wait_ge(self.sems[r], c)
            fn(q).then_inc(self.sems[ref], step)

        self.streams[eng].append(thunk)
        self._record(ref, mycnt, reads, writes)

    def emit_cc(self, fn, reads=(), writes=()):
        waits = self._need("g", reads, writes)
        ref = ("cc", self.n_cc)
        self.n_cc += 1
        self.cnt[ref] = 1

        def thunk(q, waits=waits, fn=fn, ref=ref):
            for (r, c) in waits:
                q.wait_ge(self.sems[r], c)
            fn(q).then_inc(self.sems[ref], 1)

        self.streams["g"].append(thunk)
        self._record(ref, 1, reads, writes)

    def mm_group(self, mms, reads=(), writes=()):
        reads, writes = self._excl(reads, writes)
        waits = self._need("pe", reads, writes)
        ref = ("pe", "c")
        self.cnt[ref] += 1
        mycnt = self.cnt[ref]
        n = len(mms)

        def thunk(q, waits=waits, mms=mms, n=n):
            for (r, c) in waits:
                q.wait_ge(self.sems[r], c)
            for i, f in enumerate(mms):
                inst = f(q, i == 0, i == n - 1)
                if i == n - 1:
                    inst.then_inc(self.sems[("pe", "c")], 1)

        self.streams["pe"].append(thunk)
        self._record(ref, mycnt, reads, writes)

    def full_barrier(self):
        snap = [(ref, c) for ref, c in self.cnt.items() if c > 0]
        for e in ENGINES:
            wm = self.watermark[e]
            waits = [(ref, c) for ref, c in snap if wm.get(ref, 0) < c]
            for ref, c in waits:
                wm[ref] = c

            def thunk(q, waits=waits):
                for (r, c) in waits:
                    q.wait_ge(self.sems[r], c)

            self.streams[e].append(thunk)

    def build(self, block, stack):
        for ref in self.cnt:
            self.sems[ref] = stack.enter_context(
                self.nc.semaphore(f"s_{ref[0]}_{ref[1]}"))

        def runner(e):
            def run(q):
                for t in self.streams[e]:
                    t(q)
            return run

        block.tensor(runner("pe"))
        block.vector(runner("ve"))
        block.scalar(runner("act"))
        block.sync(runner("sp"))
        block.gpsimd(runner("g"))



def build_nc(debug_out=(), stage="full"):
    nc = bass.Bass("TRN2", target_bir_lowering=False, debug=False)
    stack = contextlib.ExitStack()
    import ml_dtypes

    x0_h = nc.dram_tensor("x", [N, D], F32, kind="ExternalInput")
    wq_h = nc.dram_tensor("Wq", [D, D], F32, kind="ExternalInput")
    wk_h = nc.dram_tensor("Wk", [D, D], F32, kind="ExternalInput")
    wv_h = nc.dram_tensor("Wv", [D, D], F32, kind="ExternalInput")
    wo_h = nc.dram_tensor("Wo", [D, D], F32, kind="ExternalInput")
    bq_h = nc.dram_tensor("bq", [1, D], F32, kind="ExternalInput")
    bk_h = nc.dram_tensor("bk", [1, D], F32, kind="ExternalInput")
    bv_h = nc.dram_tensor("bv", [1, D], F32, kind="ExternalInput")
    bo_h = nc.dram_tensor("bo", [1, D], F32, kind="ExternalInput")
    dec_h = nc.dram_tensor("decay", [1, 1], F32, kind="ExternalInput")
    wgate_h = nc.dram_tensor("w_gate", [F, D], F32, kind="ExternalInput")
    wup_h = nc.dram_tensor("w_up", [F, D], F32, kind="ExternalInput")
    wdown_h = nc.dram_tensor("w_down", [D, F], F32, kind="ExternalInput")
    out_h = nc.dram_tensor("out", [N, D], F32, kind="ExternalOutput")

    x1_h = nc.dram_tensor("x1s", [N, D], F32)
    x2_h = nc.dram_tensor("x2s", [N, D], F32)
    gate_s = nc.dram_tensor("gate_s", [FC, 128, N], BF16)
    up_s = nc.dram_tensor("up_s", [FC, 128, N], BF16)
    m1_s = nc.dram_tensor("m1_s", [FC, 128, N], BF16)
    o_s = nc.dram_tensor("o_s", [N // 256, 128, FC * 256], BF16)
    wupT_h = nc.dram_tensor("wupT_s", [128, DC * F], BF16)
    wgateT_h = nc.dram_tensor("wgateT_s", [128, DC * F], BF16)
    wdownT_h = nc.dram_tensor("wdownT_s", [128, FC * D], BF16)
    cc_in_h = nc.dram_tensor("cc_in", [1, 4], F32)
    cc_out_h = nc.dram_tensor("cc_out", [1, 4], F32)

    dbg = {}
    for name in debug_out:
        if name in ("x1", "x2"):
            dbg[name] = nc.dram_tensor("dbg_" + name, [N, D], F32,
                                       kind="ExternalOutput")
        elif name in ("gate", "up", "m1"):
            dbg[name] = nc.dram_tensor("dbg_" + name, [FC, 128, N], BF16,
                                       kind="ExternalOutput")
        elif name == "scal":
            dbg[name] = nc.dram_tensor("dbg_scal", [1, 16], F32,
                                       kind="ExternalOutput")

    ident_f_h = nc.inline_tensor(np.eye(128, dtype=np.float32), name="c_idf")
    ident_b_h = nc.inline_tensor(np.eye(128, dtype=ml_dtypes.bfloat16), name="c_idb")
    ones_f_h = nc.inline_tensor(np.ones((1, 128), dtype=np.float32), name="c_ones")

    p = Prog(nc)
    sb = lambda name, shape, dt: stack.enter_context(nc.sbuf_tensor(name, shape, dt))
    psal = lambda name, shape, dt: stack.enter_context(nc.psum_tensor(name, shape, dt))

    idf = sb("idf", [128, 128], F32)
    idb = sb("idb", [128, 128], BF16)
    onesf = sb("onesf", [1, 128], F32)
    mv1 = sb("mv1", [128, 2 * TT], F32)
    mva = sb("mva", [128, 2 * TT], F32)
    mv2 = sb("mv2", [128, 2 * TT], F32)
    rstd1, b1 = sb("rstd1", [128, TT], F32), sb("b1", [128, TT], F32)
    rstda, ba = sb("rstda", [128, TT], F32), sb("ba", [128, TT], F32)
    rstd2, b2 = sb("rstd2", [128, TT], F32), sb("b2", [128, TT], F32)
    xnm1 = sb("xnm1", [128, TT], F32)
    xnm2 = sb("xnm2", [128, TT], F32)
    scbc = sb("scbc", [128, 16], F32)
    scrow = sb("scrow", [1, 8], F32)
    ntmp = sb("ntmp", [128, 1], F32)

    SC_A = 0
    SC_B = 2
    SC_NM = 6
    SC_C = 7
    SC_D = 10

    pmm = [psal(f"pmm{i}", [128, 512], F32) for i in range(4)]
    ptrb = [psal(f"ptrb{i}", [128, 1024], BF16) for i in range(2)]
    pms = psal("pms", [128, 512], F32)
    mmctr = [0]

    def next_bank():
        mmctr[0] += 1
        return pmm[mmctr[0] % 4], ("pmm", mmctr[0] % 4)

    ptrctr = [0]

    def dma(queue, out, in_, reads=(), writes=()):
        p.emit(queue, lambda q, o=out, i=in_: q.dma_start(out=o, in_=i),
               reads=reads, writes=writes, kind="d")

    def ve(fn, reads=(), writes=()):
        p.emit("ve", fn, reads=reads, writes=writes)

    def act(fn, reads=(), writes=()):
        p.emit("act", fn, reads=reads, writes=writes)

    K_IDF, K_IDB, K_ONES = "idf", "idb", "onesf"

    def bcast_row(row_ap, out_ap, row_keys, out_keys, ncols):
        p.mm_group([lambda q, s, e: q.matmul(pms[:, 0:ncols], onesf[0:1, :],
                                             row_ap, start=s, stop=e)],
                   reads=[K_ONES] + list(row_keys), writes=["pms"])
        ve(lambda q: q.tensor_copy(out_ap, pms[:, 0:ncols]),
           reads=["pms"], writes=list(out_keys))

    def tr_batch(src_aps, read_keys):
        i = ptrctr[0] % 2
        ptrctr[0] += 1
        bank, bkey = ptrb[i], ("ptrb", i)
        mms = [(lambda q, s2, e2, j=j, src=src: q.transpose(
            bank[:, j * 128:(j + 1) * 128], src, idb[:, :]))
               for j, src in enumerate(src_aps)]
        p.mm_group(mms, reads=list(read_keys) + [K_IDB], writes=[bkey])
        return bank, bkey

    def colmax_to_row(col_ap, col_key, out_col):
        p.mm_group([lambda q, s, e: q.transpose(pms[0:1, 0:128], col_ap,
                                                idf[:, :])],
                   reads=[col_key, K_IDF], writes=["pms"])
        ve(lambda q: q.tensor_reduce(scrow[0:1, out_col:out_col + 1],
                                     pms[0:1, 0:128], axis=AX.X, op=ALU.max),
           reads=["pms"], writes=[("scrow", out_col)])

    def collective_max(ncols):
        dma("g", cc_in_h.ap()[0:1, 0:ncols], scrow[0:1, 0:ncols],
            reads=[("scrow", c) for c in range(ncols)], writes=["cc_in"])
        p.emit_cc(lambda q: q.collective_compute(
            "AllReduce", ALU.max, replica_groups=[list(range(NCORE))],
            ins=[cc_in_h.ap()[0:1, 0:ncols].opt()],
            outs=[cc_out_h.ap()[0:1, 0:ncols].opt()]),
            reads=["cc_in"], writes=["cc_out"])
        dma("g", scrow[0:1, 0:ncols], cc_out_h.ap()[0:1, 0:ncols],
            reads=["cc_out"],
            writes=["scrow"] + [("scrow", c) for c in range(ncols)])

    def scale_pair(src_col, dst_cols, pre=1.0):
        c0, c1 = dst_cols
        ve(lambda q: q.tensor_scalar(scrow[0:1, c0:c0 + 1],
                                     scrow[0:1, src_col:src_col + 1],
                                     pre / QMAX, EPS_S, ALU.mult, ALU.max),
           reads=[("scrow", src_col)], writes=[("scrow", c0)])
        ve(lambda q: q.reciprocal(scrow[0:1, c1:c1 + 1], scrow[0:1, c0:c0 + 1]),
           reads=[("scrow", c0)], writes=[("scrow", c1)])

    dma("sp", idf[:, :], ident_f_h.ap(), writes=[K_IDF])
    dma("sp", idb[:, :], ident_b_h.ap(), writes=[K_IDB])
    dma("sp", onesf[:, :], ones_f_h.ap(), writes=[K_ONES])
    zcol = sb("zcol", [128, 1], F32)
    onecol = sb("onecol", [128, 1], F32)
    mgcol = sb("mgcol", [128, 1], F32)
    ve(lambda q: q.memset(zcol[:, :], 0.0), writes=["zcol"])
    ve(lambda q: q.memset(onecol[:, :], 1.0), writes=["onecol"])
    ve(lambda q: q.memset(mgcol[:, :], MAGIC), writes=["mgcol"])
    nc.const_aps.register(F32, 0.0, zcol[:, 0:1])
    nc.const_aps.register(F32, 1.0, onecol[:, 0:1])
    nc.const_aps.register(F32, MAGIC, mgcol[:, 0:1])

    import os
    PREP_OC = int(os.environ.get("KPREP_OC", "999"))
    PREP_SKIP = os.environ.get("KPREP_SKIP", "")

    def prep_ffn_weight(w_h, wT_h, O, I, name):
        OC = min(O // 128, PREP_OC)
        IC = I // 128
        nblk = I // 32
        with contextlib.ExitStack() as st2:
            wst = [st2.enter_context(nc.sbuf_tensor(f"pw_{name}_w{i}", [128, I], F32))
                   for i in range(2)]
            sblk = [st2.enter_context(nc.sbuf_tensor(f"pw_{name}_s{i}", [128, nblk],
                                                     F32)) for i in range(2)]
            invb = st2.enter_context(nc.sbuf_tensor(f"pw_{name}_i", [128, nblk], F32))
            r1 = st2.enter_context(nc.sbuf_tensor(f"pw_{name}_r", [128, I], F32))
            deq = [st2.enter_context(nc.sbuf_tensor(f"pw_{name}_d{i}", [128, I],
                                                    BF16)) for i in range(2)]
            tout = st2.enter_context(nc.sbuf_tensor(f"pw_{name}_t", [128, 1024], BF16))

            def w_front(oc):
                par = oc % 2
                dma("sp", wst[par][:, :], w_h.ap()[oc * 128:(oc + 1) * 128, :],
                    writes=[(name, "w", par)])
                ve(lambda q, par=par: q.tensor_reduce(
                    sblk[par][:, :],
                    wst[par][:, :].rearrange("p (b c) -> p b c", c=32),
                    axis=AX.X, op=ALU.max, apply_absolute_value=True),
                   reads=[(name, "w", par)], writes=[(name, "s", par)])

            def w_rest(oc):
                par = oc % 2
                ve(lambda q, par=par: q.tensor_scalar(
                    sblk[par][:, :], sblk[par][:, :],
                    1.0 / WQMAX, EPS_S, ALU.mult, ALU.max),
                   reads=[(name, "s", par)], writes=[(name, "s", par)])
                ve(lambda q, par=par: q.reciprocal(invb[:, :], sblk[par][:, :]),
                   reads=[(name, "s", par)], writes=[(name, "inv")])
                ve(lambda q, par=par: q.tensor_tensor(
                    r1[:, :].rearrange("p (b c) -> p b c", c=32),
                    wst[par][:, :].rearrange("p (b c) -> p b c", c=32),
                    invb[:, :].rearrange("p (b o) -> p b o", o=1)
                        .broadcast_to([128, nblk, 32]),
                    ALU.mult),
                   reads=[(name, "w", par), (name, "inv")], writes=[(name, "r1")])
                ve(lambda q: q.tensor_scalar(r1[:, :], r1[:, :], MAGIC, MAGIC,
                                             ALU.add, ALU.subtract),
                   reads=[(name, "r1")], writes=[(name, "r1")])
                ve(lambda q, par=par: q.tensor_tensor(
                    deq[par][:, :].rearrange("p (b c) -> p b c", c=32),
                    r1[:, :].rearrange("p (b c) -> p b c", c=32),
                    sblk[par][:, :].rearrange("p (b o) -> p b o", o=1)
                        .broadcast_to([128, nblk, 32]),
                    ALU.mult),
                   reads=[(name, "r1"), (name, "s", par)],
                   writes=[(name, "deq", par)])
                _w_tail(oc, par)

            def _w_tail(oc, par):
                for ic0 in range(0, IC, 8):
                    if "T" in PREP_SKIP:
                        continue
                    nb = min(8, IC - ic0)
                    srcs = [deq[par][:, (ic0 + j) * 128:(ic0 + j + 1) * 128]
                            for j in range(nb)]
                    bank, bkey = tr_batch(srcs, [(name, "deq", par)])
                    tk = (name, "tout")
                    ve(lambda q, nb=nb, bank=bank: q.tensor_copy(
                        tout[:, 0:nb * 128], bank[:, 0:nb * 128]),
                       reads=[bkey], writes=[tk])
                    if "S" in PREP_SKIP:
                        continue
                    dma("g",
                        wT_h.ap()[:, :].rearrange("p (i t) -> p i t", t=O)
                        [:, ic0:ic0 + nb, oc * 128:(oc + 1) * 128],
                        tout[:, 0:nb * 128].rearrange("p (i t) -> p i t", t=128),
                        reads=[tk], writes=[(name, "spill")])

            w_front(0)
            for oc in range(OC):
                if oc + 1 < OC:
                    w_front(oc + 1)
                w_rest(oc)
            p.full_barrier()

    def zero_out_finish():
        zo = sb("zot", [128, D], F32)
        ve(lambda q: q.memset(zo[:, :], 0.0), writes=["zot"])
        for t in range(TT):
            dma("sp", out_h.ap()[t * 128:(t + 1) * 128, :], zo[:, :],
                reads=["zot"], writes=["outw"])

    if stage == "consts":
        zero_out_finish()
        p.full_barrier()
        with nc.Block() as block:
            p.build(block, stack)
        stack.close()
        return nc

    prep_ffn_weight(wup_h, wupT_h, F, D, "wup")
    if stage == "prep1":
        zero_out_finish()
        p.full_barrier()
        with nc.Block() as block:
            p.build(block, stack)
        stack.close()
        return nc
    prep_ffn_weight(wgate_h, wgateT_h, F, D, "wgate")
    prep_ffn_weight(wdown_h, wdownT_h, D, F, "wdown")

    def ffn_call(tag, xin_h, xout_h, mv_t, rstd_t, b_t, xnm_t,
                 need_pass_a, fuse_stats_mv=None, dbg_x=None, upto="E"):
        kmv = tag + "mv"
        krstd, kb, kxnm = tag + "rstd", tag + "b", tag + "xnm"
        kxin, kxout = xin_h.name, xout_h.name

        if need_pass_a:
            with contextlib.ExitStack() as st2:
                xa = [st2.enter_context(nc.sbuf_tensor(f"{tag}Axa{i}", [128, D], F32))
                      for i in range(2)]
                bnst = st2.enter_context(nc.sbuf_tensor(f"{tag}Abn", [128, 12], F32))
                tsub = [st2.enter_context(nc.sbuf_tensor(f"{tag}Ats{i}", [128, D], F32))
                        for i in range(2)]
                for t in range(TT):
                    xk = (tag, "Axa", t % 2)
                    dma("sp", xa[t % 2][:, :], xin_h.ap()[t * 128:(t + 1) * 128, :],
                        reads=[kxin], writes=[xk])
                    ve(lambda q, t=t: q.bn_stats(bnst[:, 0:6], xa[t % 2][:, 0:512]),
                       reads=[xk], writes=[(tag, "Abn0")])
                    ve(lambda q, t=t: q.bn_stats(bnst[:, 6:12], xa[t % 2][:, 512:1024]),
                       reads=[xk], writes=[(tag, "Abn1")])
                    ve(lambda q, t=t: q.bn_aggr(mv_t[:, 2 * t:2 * t + 2], bnst[:, :]),
                       reads=[(tag, "Abn0"), (tag, "Abn1")],
                       writes=[(kmv, t), kmv])
                    ve(lambda q, t=t: q.tensor_scalar(
                        ntmp[:, :], mv_t[:, 2 * t:2 * t + 1], -1.0, None, ALU.mult),
                       reads=[(kmv, t)], writes=["ntmp"])
                    act(lambda q, t=t: q.activation(
                        tsub[t % 2][:, :], xa[t % 2][:, :], AF.Identity,
                        bias=ntmp[:, :], scale=1.0),
                        reads=[xk, "ntmp"], writes=[(tag, "Atsub", t % 2)])
                    ve(lambda q, t=t: q.tensor_reduce(
                        xnm_t[:, t:t + 1], tsub[t % 2][:, :], axis=AX.X,
                        op=ALU.max, apply_absolute_value=True),
                       reads=[(tag, "Atsub", t % 2)], writes=[kxnm])
                p.full_barrier()

        var_view = mv_t[:, :].rearrange("p (t two) -> p t two", two=2)[:, :, 1]
        mean_view = mv_t[:, :].rearrange("p (t two) -> p t two", two=2)[:, :, 0]
        ve(lambda q: q.tensor_scalar(rstd_t[:, :], var_view, LN_EPS, None, ALU.add),
           reads=[kmv], writes=[krstd])
        act(lambda q: q.activation(rstd_t[:, :], rstd_t[:, :], AF.Sqrt),
            reads=[krstd], writes=[krstd])
        ve(lambda q: q.reciprocal(rstd_t[:, :], rstd_t[:, :]),
           reads=[krstd], writes=[krstd])
        ve(lambda q: q.scalar_tensor_tensor(b_t[:, :], mean_view, -1.0,
                                            rstd_t[:, :], ALU.mult, ALU.mult),
           reads=[kmv, krstd], writes=[kb])
        ve(lambda q: q.tensor_tensor(xnm_t[:, :], xnm_t[:, :], rstd_t[:, :],
                                     ALU.mult),
           reads=[kxnm, krstd], writes=[kxnm])
        ve(lambda q: q.tensor_reduce(scbc[:, 15:16], xnm_t[:, :], axis=AX.X,
                                     op=ALU.max),
           reads=[kxnm], writes=[(tag, "xcol")])
        colmax_to_row(scbc[:, 15:16], (tag, "xcol"), 0)
        collective_max(1)
        scale_pair(0, (0, 1))
        bcast_row(scrow[0:1, 0:2], scbc[:, SC_A:SC_A + 2],
                  [("scrow", 0), ("scrow", 1)], ["scbcA"], 2)
        if upto == "A":
            return

        with contextlib.ExitStack() as st2:
            wupT = st2.enter_context(nc.sbuf_tensor(f"{tag}BwU", [128, DC * F], BF16))
            wgateT = st2.enter_context(nc.sbuf_tensor(f"{tag}BwG", [128, DC * F], BF16))
            xba = [st2.enter_context(nc.sbuf_tensor(f"{tag}Bxa{i}", [128, D], F32))
                   for i in range(2)]
            xn = [st2.enter_context(nc.sbuf_tensor(f"{tag}Bxn{i}", [128, D], F32))
                  for i in range(2)]
            xq = [st2.enter_context(nc.sbuf_tensor(f"{tag}Bxq{i}", [128, D], BF16))
                  for i in range(2)]
            xqT = [st2.enter_context(nc.sbuf_tensor(f"{tag}BxqT{i}", [128, DC * 512],
                                                    BF16)) for i in range(2)]
            evu = [st2.enter_context(nc.sbuf_tensor(f"{tag}Bevu{i}", [128, 4 * 512],
                                                    BF16)) for i in range(2)]
            evg = [st2.enter_context(nc.sbuf_tensor(f"{tag}Bevg{i}", [128, 4 * 512],
                                                    BF16)) for i in range(2)]
            upmaxs = st2.enter_context(nc.sbuf_tensor(f"{tag}Bum", [128, FC * NB], F32))
            gatemaxs = st2.enter_context(nc.sbuf_tensor(f"{tag}Bgm", [128, FC * NB],
                                                        F32))
            dma("sp", wupT[:, :], wupT_h.ap(), reads=[("wup", "spill")],
                writes=[(tag, "BwU")])
            dma("sp", wgateT[:, :], wgateT_h.ap(), reads=[("wgate", "spill")],
                writes=[(tag, "BwG")])
            def b_prep(blk):
                par = blk % 2
                kxqT = (tag, "BxqT", par)
                for ti in range(4):
                    t = blk * 4 + ti
                    tp = t % 2
                    xk = (tag, "Bxa", tp)
                    dma("sp", xba[tp][:, :], xin_h.ap()[t * 128:(t + 1) * 128, :],
                        reads=[kxin], writes=[xk])
                    act(lambda q, t=t, tp=tp: q.activation(
                        xn[tp][:, :], xba[tp][:, :], AF.Identity,
                        bias=b_t[:, t:t + 1], scale=rstd_t[:, t:t + 1]),
                        reads=[xk, krstd, kb], writes=[(tag, "Bxn", tp)])
                    ve(lambda q, tp=tp: q.tensor_scalar(
                        xn[tp][:, :], xn[tp][:, :], scbc[:, SC_A + 1:SC_A + 2],
                        MAGIC, ALU.mult, ALU.add),
                       reads=[(tag, "Bxn", tp), "scbcA"],
                       writes=[(tag, "Bxn", tp)])
                    ve(lambda q, tp=tp: q.tensor_scalar(
                        xq[tp][:, :], xn[tp][:, :], MAGIC,
                        scbc[:, SC_A:SC_A + 1], ALU.subtract, ALU.mult),
                       reads=[(tag, "Bxn", tp), "scbcA"],
                       writes=[(tag, "Bxq", tp)])
                    srcs = [xq[tp][:, dc * 128:(dc + 1) * 128] for dc in range(DC)]
                    bank, bkey = tr_batch(srcs, [(tag, "Bxq", tp)])
                    ve(lambda q, ti=ti, par=par, bank=bank: q.tensor_copy(
                        xqT[par][:, :].rearrange("p (d t) -> p d t", t=512)
                        [:, :, ti * 128:(ti + 1) * 128],
                        bank[:, :].rearrange("p (d t) -> p d t", t=128)),
                       reads=[bkey], writes=[kxqT])
            def b_mms(blk):
                par = blk % 2
                kxqT = (tag, "BxqT", par)
                for (wT, kwT, scr_h, ev2, kev, maxsl, kmaxsl) in (
                        (wupT, (tag, "BwU"), up_s, evu, (tag, "Bevu"), upmaxs,
                         "upmaxs"),
                        (wgateT, (tag, "BwG"), gate_s, evg, (tag, "Bevg"),
                         gatemaxs, "gatemaxs")):
                    for fc in range(FC):
                        bank, bkey = next_bank()
                        p.mm_group(
                            [(lambda q, s, e, dc=dc2, bank=bank, wT=wT, fc=fc,
                              par=par, xqT=xqT: q.matmul(
                                  bank[:, 0:512],
                                  wT[:, dc * F + fc * 128: dc * F + fc * 128 + 128],
                                  xqT[par][:, dc * 512:(dc + 1) * 512],
                                  start=s, stop=e))
                             for dc2 in range(DC)],
                            reads=[kwT, kxqT], writes=[bkey])
                        gi = (fc // 4) % 2
                        sl = fc % 4
                        act(lambda q, bank=bank, ev2=ev2, gi=gi, sl=sl: q.copy(
                            ev2[gi][:, sl * 512:(sl + 1) * 512], bank[:, :]),
                            reads=[bkey], writes=[(kev, gi, sl)])
                        ve(lambda q, bank=bank, maxsl=maxsl, col=blk * FC + fc:
                           q.tensor_reduce(maxsl[:, col:col + 1], bank[:, :],
                                           axis=AX.X, op=ALU.max,
                                           apply_absolute_value=True),
                           reads=[bkey], writes=[kmaxsl])
                        if sl == 3:
                            dma("g",
                                scr_h.ap()[fc - 3:fc + 1, :,
                                           blk * 512:(blk + 1) * 512]
                                .rearrange("f p t -> p f t"),
                                ev2[gi][:, :].rearrange("p (f t) -> p f t", t=512),
                                reads=[(kev, gi, 0), (kev, gi, 1), (kev, gi, 2),
                                       (kev, gi, 3)],
                                writes=[scr_h.name])

            b_prep(0)
            for blk in range(NB):
                if blk + 1 < NB:
                    b_prep(blk + 1)
                b_mms(blk)
            ve(lambda q, upmaxs=upmaxs: q.tensor_reduce(
                scbc[:, 15:16], upmaxs[:, :], axis=AX.X,
                op=ALU.max), reads=["upmaxs"],
               writes=[(tag, "ucol")])
            colmax_to_row(scbc[:, 15:16], (tag, "ucol"), 0)
            ve(lambda q, gatemaxs=gatemaxs: q.tensor_reduce(
                scbc[:, 15:16], gatemaxs[:, :], axis=AX.X,
                op=ALU.max), reads=["gatemaxs"],
               writes=[(tag, "gcol")])
            colmax_to_row(scbc[:, 15:16], (tag, "gcol"), 1)
            p.full_barrier()

        collective_max(2)
        scale_pair(0, (2, 3))
        scale_pair(1, (4, 5))
        ve(lambda q: q.tensor_scalar(scrow[0:1, 6:7], scrow[0:1, 4:5],
                                     -MAGIC, None, ALU.mult),
           reads=[("scrow", 4)], writes=[("scrow", 6)])
        bcast_row(scrow[0:1, 2:7], scbc[:, SC_B:SC_B + 5],
                  [("scrow", c) for c in (2, 3, 4, 5, 6)], ["scbcB"], 5)
        if upto == "B":
            return

        HP = N // 2
        with contextlib.ExitStack() as st2:
            gtp = [st2.enter_context(nc.sbuf_tensor(f"{tag}Cg{i}", [128, HP], BF16))
                   for i in range(2)]
            ct1 = [st2.enter_context(nc.sbuf_tensor(f"{tag}Ct1{i}", [128, HP], F32))
                   for i in range(2)]
            ct2 = [st2.enter_context(nc.sbuf_tensor(f"{tag}Ct2{i}", [128, HP], F32))
                   for i in range(2)]
            ct3 = [st2.enter_context(nc.sbuf_tensor(f"{tag}Ct3{i}", [128, HP], F32))
                   for i in range(2)]
            m1p = [st2.enter_context(nc.sbuf_tensor(f"{tag}Cm{i}", [128, HP], BF16))
                   for i in range(2)]
            m1maxs = st2.enter_context(nc.sbuf_tensor(f"{tag}Cmm", [128, 2 * FC],
                                                      F32))
            def c_front(idx):
                fc, h = idx // 2, idx % 2
                par = idx % 2
                kg = (tag, "Cg", par)
                dma("sp", gtp[par][:, :], gate_s.ap()[fc, :, h * HP:(h + 1) * HP],
                    reads=[gate_s.name], writes=[kg])
                act(lambda q, par=par: q.activation(
                    ct1[par][:, :], gtp[par][:, :], AF.Identity,
                    scale=scbc[:, SC_B + 3:SC_B + 4], bias=mgcol[:, :]),
                    reads=[kg, "scbcB", "mgcol"], writes=[(tag, "Ct1", par)])

            def c_rest(idx):
                fc, h = idx // 2, idx % 2
                par = idx % 2
                km = (tag, "Cm", par)
                act(lambda q, par=par: q.activation(
                    ct3[par][:, :], ct1[par][:, :], AF.Sigmoid,
                    scale=scbc[:, SC_B + 2:SC_B + 3],
                    bias=scbc[:, SC_NM:SC_NM + 1]),
                    reads=[(tag, "Ct1", par), "scbcB"], writes=[(tag, "Ct3", par)])
                act(lambda q, par=par: q.activation(
                    ct2[par][:, :], ct3[par][:, :], AF.Identity,
                    scale=65536.0, bias=mgcol[:, :]),
                    reads=[(tag, "Ct3", par), "mgcol"], writes=[(tag, "Ct2", par)])
                ve(lambda q, par=par: q.tensor_scalar(
                    ct3[par][:, :], ct2[par][:, :], MAGIC,
                    SIGCLIP, ALU.subtract, ALU.min),
                   reads=[(tag, "Ct2", par)], writes=[(tag, "Ct3", par)])
                ve(lambda q, par=par: q.scalar_tensor_tensor(
                    m1p[par][:, :], ct1[par][:, :], MAGIC, ct3[par][:, :],
                    ALU.subtract, ALU.mult),
                   reads=[(tag, "Ct1", par), (tag, "Ct3", par)], writes=[km])
                ve(lambda q, par=par, idx=idx, m1maxs=m1maxs, m1p=m1p:
                   q.tensor_reduce(m1maxs[:, idx:idx + 1], m1p[par][:, :],
                                   axis=AX.X, op=ALU.max,
                                   apply_absolute_value=True),
                   reads=[km], writes=["m1maxs"])
                dma("g", m1_s.ap()[fc, :, h * HP:(h + 1) * HP], m1p[par][:, :],
                    reads=[km], writes=[m1_s.name])

            c_front(0)
            c_front(1)
            for idx in range(2 * FC):
                c_rest(idx)
                if idx + 2 < 2 * FC:
                    c_front(idx + 2)
            ve(lambda q, m1maxs=m1maxs: q.tensor_reduce(
                scbc[:, 15:16], m1maxs[:, :], axis=AX.X,
                op=ALU.max), reads=["m1maxs"],
               writes=[(tag, "mcol")])
            colmax_to_row(scbc[:, 15:16], (tag, "mcol"), 0)
            p.full_barrier()

        collective_max(1)
        ve(lambda q: q.tensor_tensor(scrow[0:1, 0:1], scrow[0:1, 0:1],
                                     scrow[0:1, 4:5], ALU.mult),
           reads=[("scrow", 0), ("scrow", 4)], writes=[("scrow", 0)])
        scale_pair(0, (0, 1), pre=INV65536)
        ve(lambda q: q.tensor_scalar(scrow[0:1, 2:3], scrow[0:1, 1:2],
                                     scrow[0:1, 4:5], INV65536, ALU.mult,
                                     ALU.mult),
           reads=[("scrow", 1), ("scrow", 4)], writes=[("scrow", 2)])
        bcast_row(scrow[0:1, 0:3], scbc[:, SC_C:SC_C + 3],
                  [("scrow", 0), ("scrow", 1), ("scrow", 2)], ["scbcC"], 3)
        if upto == "C":
            return

        with contextlib.ExitStack() as st2:
            m1d = [st2.enter_context(nc.sbuf_tensor(f"{tag}Dm{i}", [128, HP], BF16))
                   for i in range(2)]
            upd = [st2.enter_context(nc.sbuf_tensor(f"{tag}Du{i}", [128, HP], BF16))
                   for i in range(2)]
            dt1 = [st2.enter_context(nc.sbuf_tensor(f"{tag}Dt1{i}", [128, HP], F32))
                   for i in range(2)]
            dt2 = [st2.enter_context(nc.sbuf_tensor(f"{tag}Dt2{i}", [128, HP], F32))
                   for i in range(2)]
            dt3 = [st2.enter_context(nc.sbuf_tensor(f"{tag}Dt3{i}", [128, HP], BF16))
                   for i in range(2)]
            omaxs = st2.enter_context(nc.sbuf_tensor(f"{tag}Dom", [128, 2 * FC],
                                                     F32))
            def d_front(idx):
                fc, h = idx // 2, idx % 2
                par = idx % 2
                km = (tag, "Dm", par)
                ku = (tag, "Du", par)
                dma("sp", m1d[par][:, :], m1_s.ap()[fc, :, h * HP:(h + 1) * HP],
                    reads=[m1_s.name], writes=[km])
                dma("sp", upd[par][:, :], up_s.ap()[fc, :, h * HP:(h + 1) * HP],
                    reads=[up_s.name], writes=[ku])
                act(lambda q, par=par: q.activation(
                    dt1[par][:, :], m1d[par][:, :], AF.Identity,
                    scale=scbc[:, SC_C + 2:SC_C + 3], bias=mgcol[:, :]),
                    reads=[km, "scbcC", "mgcol"], writes=[(tag, "Dt1", par)])
                act(lambda q, par=par: q.activation(
                    dt2[par][:, :], upd[par][:, :], AF.Identity,
                    scale=scbc[:, SC_B + 1:SC_B + 2], bias=mgcol[:, :]),
                    reads=[ku, "scbcB", "mgcol"], writes=[(tag, "Dt2", par)])

            def d_rest(idx):
                fc, h = idx // 2, idx % 2
                par = idx % 2
                ve(lambda q, par=par: q.tensor_scalar(
                    dt1[par][:, :], dt1[par][:, :], MAGIC, scbc[:, SC_C:SC_C + 1],
                    ALU.subtract, ALU.mult),
                   reads=[(tag, "Dt1", par), "scbcC"], writes=[(tag, "Dt1", par)])
                p.emit("g", lambda q, par=par: q.tensor_scalar(
                    dt2[par][:, :], dt2[par][:, :], -MAGIC, scbc[:, SC_B:SC_B + 1],
                    ALU.add, ALU.mult),
                       reads=[(tag, "Dt2", par), "scbcB"],
                       writes=[(tag, "Dt2", par)])
                eng_o = "g" if (idx % 2 == 0) else "ve"
                p.emit(eng_o, lambda q, par=par: q.tensor_tensor(
                    dt3[par][:, :], dt1[par][:, :], dt2[par][:, :], ALU.mult),
                       reads=[(tag, "Dt1", par), (tag, "Dt2", par)],
                       writes=[(tag, "Dt3", par)])
                ve(lambda q, idx=idx, par=par, omaxs=omaxs: q.tensor_reduce(
                    omaxs[:, idx:idx + 1], dt3[par][:, :], axis=AX.X, op=ALU.max,
                    apply_absolute_value=True),
                   reads=[(tag, "Dt3", par)], writes=["omaxs"])
                ng_h = HP // 256
                dma("g",
                    o_s.ap()[h * ng_h:(h + 1) * ng_h, :, fc * 256:(fc + 1) * 256]
                    .rearrange("g p t -> p g t"),
                    dt3[par][:, :].rearrange("p (g t) -> p g t", t=256),
                    reads=[(tag, "Dt3", par)], writes=[o_s.name])

            d_front(0)
            d_front(1)
            for idx in range(2 * FC):
                d_rest(idx)
                if idx + 2 < 2 * FC:
                    d_front(idx + 2)
            ve(lambda q, omaxs=omaxs: q.tensor_reduce(
                scbc[:, 15:16], omaxs[:, :], axis=AX.X,
                op=ALU.max), reads=["omaxs"],
               writes=[(tag, "ocol")])
            colmax_to_row(scbc[:, 15:16], (tag, "ocol"), 0)
            p.full_barrier()

        collective_max(1)
        scale_pair(0, (0, 1))
        ve(lambda q: q.tensor_scalar(scrow[0:1, 2:3], scrow[0:1, 0:1], 0.5,
                                     None, ALU.mult),
           reads=[("scrow", 0)], writes=[("scrow", 2)])
        ve(lambda q: q.tensor_copy(scrow[0:1, 3:4], scrow[0:1, 1:2]),
           reads=[("scrow", 1)], writes=[("scrow", 3)])
        bcast_row(scrow[0:1, 2:4], scbc[:, SC_D:SC_D + 2],
                  [("scrow", 2), ("scrow", 3)], ["scbcD"], 2)
        if upto == "D":
            return

        EG = 256
        NEG = N // EG
        W = FC * EG
        with contextlib.ExitStack() as st2:
            wdT = st2.enter_context(nc.sbuf_tensor(f"{tag}EwD", [128, FC * D], BF16))
            oe = [st2.enter_context(nc.sbuf_tensor(f"{tag}Eo{i}", [128, W], BF16))
                  for i in range(2)]
            xe = [st2.enter_context(nc.sbuf_tensor(f"{tag}Ex{i}", [128, 2 * D], F32))
                  for i in range(2)]
            et1 = [st2.enter_context(nc.sbuf_tensor(f"{tag}Et1{i}", [128, W], F32))
                   for i in range(2)]
            oq = [st2.enter_context(nc.sbuf_tensor(f"{tag}Eoq{i}", [128, W], BF16))
                  for i in range(2)]
            xo = [st2.enter_context(nc.sbuf_tensor(f"{tag}Exo{i}", [128, 2 * D], F32))
                  for i in range(2)]
            ebn = st2.enter_context(nc.sbuf_tensor(f"{tag}Ebn", [128, 12], F32))
            dma("sp", wdT[:, :], wdownT_h.ap(), reads=[("wdown", "spill")],
                writes=[(tag, "EwD")])
            def e_front(g):
                par = g % 2
                ko = (tag, "Eo", par)
                kx = (tag, "Ex", par)
                dma("sp", oe[par][:, :], o_s.ap()[g, :, :],
                    reads=[o_s.name], writes=[ko])
                dma("sp", xe[par][:, :].rearrange("p (a b) -> p a b", b=D),
                    xin_h.ap()[g * EG:(g + 1) * EG, :]
                    .rearrange("(a t) b -> t a b", t=128),
                    reads=[kxin], writes=[kx])
                act(lambda q, par=par: q.activation(
                    et1[par][:, :], oe[par][:, :], AF.Identity,
                    scale=scbc[:, SC_D + 1:SC_D + 2], bias=mgcol[:, :]),
                    reads=[ko, "scbcD", "mgcol"], writes=[(tag, "Et1", par)])

            def e_rest(g):
                par = g % 2
                kx = (tag, "Ex", par)
                koq = (tag, "Eoq", par)
                kxo = (tag, "Exo", par)
                ve(lambda q, par=par: q.tensor_scalar(
                    oq[par][:, :], et1[par][:, :], MAGIC, scbc[:, SC_D:SC_D + 1],
                    ALU.subtract, ALU.mult),
                   reads=[(tag, "Et1", par), "scbcD"], writes=[koq])
                for tc in range(2):
                    banks = []
                    for hh in range(2):
                        bank, bkey = next_bank()
                        banks.append((bank, bkey))
                        p.mm_group(
                            [(lambda q, s, e, fc=fc2, bank=bank, hh=hh, par=par,
                              tc=tc, oq=oq: q.matmul(
                                  bank[:, 0:512],
                                  oq[par][:, fc * EG + tc * 128:
                                          fc * EG + tc * 128 + 128],
                                  wdT[:, fc * D + hh * 512:
                                      fc * D + hh * 512 + 512],
                                  start=s, stop=e))
                             for fc2 in range(FC)],
                            reads=[koq, (tag, "EwD")], writes=[bkey])
                    for hh, (bank, bkey) in enumerate(banks):
                        ve(lambda q, bank=bank, hh=hh, tc=tc, par=par:
                           q.tensor_tensor(
                               xo[par][:, tc * D + hh * 512:
                                       tc * D + hh * 512 + 512],
                               bank[:, 0:512],
                               xe[par][:, tc * D + hh * 512:
                                       tc * D + hh * 512 + 512],
                               ALU.add),
                           reads=[bkey, kx], writes=[kxo])
                dma("g",
                    xout_h.ap()[g * EG:(g + 1) * EG, :]
                    .rearrange("(a t) b -> t a b", t=128),
                    xo[par][:, :].rearrange("p (a b) -> p a b", b=D),
                    reads=[kxo], writes=[kxout])
                if dbg_x is not None:
                    dma("g",
                        dbg_x.ap()[g * EG:(g + 1) * EG, :]
                        .rearrange("(a t) b -> t a b", t=128),
                        xo[par][:, :].rearrange("p (a b) -> p a b", b=D),
                        reads=[kxo], writes=["dbgx" + tag])
                if fuse_stats_mv is not None:
                    fmv, kfmv = fuse_stats_mv
                    for tc in range(2):
                        tt2 = g * 2 + tc
                        ve(lambda q, par=par, tc=tc: q.bn_stats(
                            ebn[:, 0:6], xo[par][:, tc * D:tc * D + 512]),
                           reads=[kxo], writes=[(tag, "Ebn0")])
                        ve(lambda q, par=par, tc=tc: q.bn_stats(
                            ebn[:, 6:12], xo[par][:, tc * D + 512:tc * D + 1024]),
                           reads=[kxo], writes=[(tag, "Ebn1")])
                        ve(lambda q, tt2=tt2, fmv=fmv: q.bn_aggr(
                            fmv[:, 2 * tt2:2 * tt2 + 2], ebn[:, :]),
                           reads=[(tag, "Ebn0"), (tag, "Ebn1")],
                           writes=[(kfmv, tt2), kfmv])

            e_front(0)
            for g in range(NEG):
                e_rest(g)
                if g + 1 < NEG:
                    e_front(g + 1)
            p.full_barrier()


    def finish():
        p.full_barrier()
        with nc.Block() as block:
            p.build(block, stack)
        stack.close()
        return nc

    if stage == "prep":
        zo = sb("zot", [128, D], F32)
        ve(lambda q: q.memset(zo[:, :], 0.0), writes=["zot"])
        for t in range(TT):
            dma("sp", out_h.ap()[t * 128:(t + 1) * 128, :], zo[:, :],
                reads=["zot"], writes=["outw"])
        return finish()

    _upto = stage[2] if stage.startswith("f1") and len(stage) > 2 else "E"
    ffn_call("f1", x0_h, x1_h, mv1, rstd1, b1, xnm1, need_pass_a=True,
             fuse_stats_mv=(mva, "amv"), dbg_x=dbg.get("x1"), upto=_upto)
    if stage.startswith("f1"):
        zero_out_finish()
        return finish()
    for nm, scr in (("gate", gate_s), ("up", up_s), ("m1", m1_s)):
        if nm in dbg:
            for fc in range(FC):
                dma("g", dbg[nm].ap()[fc, :, :], scr.ap()[fc, :, :],
                    reads=[scr.name], writes=["dbg_" + nm])

    def attention():
        kx1, kx2 = x1_h.name, x2_h.name
        with contextlib.ExitStack() as st2:
            asb = lambda nm, sh, dt: st2.enter_context(nc.sbuf_tensor(nm, sh, dt))
            wqs = asb("at_wq", [128, DC * D], BF16)
            wks = asb("at_wk", [128, DC * D], BF16)
            wvs = asb("at_wv", [128, DC * D], BF16)
            wos = asb("at_wo", [128, DC * D], BF16)
            brow = asb("at_brow", [1, D], F32)
            bq_c = asb("at_bqc", [128, DC], F32)
            bk_c = asb("at_bkc", [128, DC], F32)
            bv_bc = asb("at_bvbc", [128, D], F32)
            bo_bc = asb("at_bobc", [128, D], F32)
            lam = asb("at_lam", [1, 4], F32)
            lnl_bc = asb("at_lnlbc", [128, 1], F32)
            lamC = asb("at_lamC", [128, 1], F32)
            lamrow = asb("at_lamrow", [1, CH], F32)
            lamr_bc = asb("at_lamrbc", [128, CH], F32)
            lamK = asb("at_lamK", [128, TPCH], F32)
            iot = asb("at_iot", [128, CH], F32)
            msk1 = asb("at_msk1", [128, CH], F32)
            msk2 = asb("at_msk2", [128, CH], F32)
            mask = asb("at_mask", [128, TPCH * CH], F32)
            mem = asb("at_mem", [128, DC * D], F32)
            memb = asb("at_memb", [128, DC * D], BF16)
            xat = [asb(f"at_xa{i}", [128, TPCH * D], F32) for i in range(2)]
            xnb = asb("at_xnb", [128, D], BF16)
            xnT = asb("at_xnT", [128, DC * CH], BF16)
            qT = asb("at_qT", [128, DC * CH], BF16)
            qs = [asb(f"at_qs{i}", [128, DC * CH], BF16) for i in range(2)]
            kT = asb("at_kT", [128, DC * CH], BF16)
            tmpA = asb("at_tmpA", [128, CH], F32)
            tmpB = asb("at_tmpB", [128, CH], F32)
            tmpC = asb("at_tmpC", [128, CH], F32)
            Kp = [asb(f"at_Kp{i}", [128, TPCH * D], BF16) for i in range(2)]
            Vs = [asb(f"at_V{i}", [128, TPCH * D], BF16) for i in range(2)]
            scm = [asb(f"at_scm{i}", [128, TPCH * CH], BF16) for i in range(2)]
            aoT = asb("at_aoT", [128, DC * CH], BF16)
            xot = asb("at_xot", [128, D], F32)
            bnst = asb("at_bn", [128, 12], F32)

            for (w_h, ws, kws) in ((wq_h, wqs, "at_wq"), (wk_h, wks, "at_wk"),
                                   (wv_h, wvs, "at_wv"), (wo_h, wos, "at_wo")):
                dma("g", ws[:, :].rearrange("p (c d) -> p c d", d=D),
                    w_h.ap().rearrange("(c p) d -> p c d", p=128),
                    writes=[kws])
            for (b_h, dst, kdst) in ((bq_h, bq_c, "at_bqc"), (bk_h, bk_c, "at_bkc")):
                dma("sp", brow[:, :], b_h.ap(), writes=["at_brow"])
                for ec in range(DC):
                    p.mm_group([lambda q, s, e, ec=ec: q.transpose(
                        pms[0:128, 0:1], brow[0:1, ec * 128:(ec + 1) * 128],
                        idf[0:1, 0:1])],
                        reads=["at_brow", K_IDF], writes=["pms"])
                    ve(lambda q, ec=ec, dst=dst: q.tensor_copy(
                        dst[:, ec:ec + 1], pms[:, 0:1]),
                       reads=["pms"], writes=[kdst])
            for (b_h, dst, kdst) in ((bv_h, bv_bc, "at_bvbc"), (bo_h, bo_bc, "at_bobc")):
                dma("sp", brow[:, :], b_h.ap(), writes=["at_brow"])
                for h in range(2):
                    bcast_row(brow[0:1, h * 512:(h + 1) * 512],
                              dst[:, h * 512:(h + 1) * 512], ["at_brow"], [kdst], 512)
            dma("sp", lam[0:1, 0:1], dec_h.ap(), writes=["at_lam"])
            act(lambda q: q.activation(lam[0:1, 1:2], lam[0:1, 0:1], AF.Sigmoid),
                reads=["at_lam"], writes=[("at_lam", 1)])
            ve(lambda q: q.tensor_scalar(lam[0:1, 2:3], lam[0:1, 1:2],
                                         float(2.0 ** -60), None, ALU.max),
               reads=[("at_lam", 1)], writes=[("at_lam", 2)])
            act(lambda q: q.activation(lam[0:1, 3:4], lam[0:1, 2:3], AF.Ln),
                reads=[("at_lam", 2)], writes=[("at_lam", 3)])
            bcast_row(lam[0:1, 3:4], lnl_bc[:, :], [("at_lam", 3)], ["at_lnlbc"], 1)
            act(lambda q: q.activation(lamC[:, :], lnl_bc[:, :], AF.Exp,
                                       scale=float(CH)),
                reads=["at_lnlbc"], writes=["at_lamC"])
            p.emit("g", lambda q: q.iota(lamrow[0:1, :], pattern=[[1, CH]], base=0,
                                         channel_multiplier=0,
                                         allow_small_or_imprecise_dtypes=True),
                   writes=["at_lamrow"])
            act(lambda q: q.activation(lamrow[0:1, :], lamrow[0:1, :], AF.Exp,
                                       scale=lam[0:1, 3:4]),
                reads=["at_lamrow", ("at_lam", 3)], writes=["at_lamrow"])
            bcast_row(lamrow[0:1, :], lamr_bc[:, :], ["at_lamrow"], ["at_lamrbc"], CH)
            p.emit("g", lambda q: q.iota(iot[:, 0:TPCH], pattern=[[-128, TPCH]],
                                         base=CH - 1, channel_multiplier=-1,
                                         allow_small_or_imprecise_dtypes=True),
                   writes=["at_iot"])
            act(lambda q: q.activation(lamK[:, :], iot[:, 0:TPCH], AF.Exp,
                                       scale=lnl_bc[:, 0:1]),
                reads=["at_iot", "at_lnlbc"], writes=["at_lamK"])
            for jc in range(TPCH):
                p.emit("g", lambda q, jc=jc: q.iota(
                    iot[:, :], pattern=[[1, CH]], base=-1 - jc * 128,
                    channel_multiplier=-1,
                    allow_small_or_imprecise_dtypes=True),
                    writes=["at_iot"])
                act(lambda q: q.activation(msk1[:, :], iot[:, :], AF.Relu),
                    reads=["at_iot"], writes=["at_msk1"])
                act(lambda q: q.activation(msk2[:, :], iot[:, :], AF.Relu,
                                           bias=onecol[:, :]),
                    reads=["at_iot", "onecol"], writes=["at_msk2"])
                ve(lambda q: q.tensor_tensor(msk2[:, :], msk2[:, :], msk1[:, :],
                                             ALU.subtract),
                   reads=["at_msk1", "at_msk2"], writes=["at_msk2"])
                act(lambda q: q.activation(msk1[:, :], msk1[:, :], AF.Exp,
                                           scale=lnl_bc[:, 0:1]),
                    reads=["at_msk1", "at_lnlbc"], writes=["at_msk1"])
                ve(lambda q, jc=jc: q.tensor_tensor(
                    mask[:, jc * CH:(jc + 1) * CH], msk1[:, :], msk2[:, :],
                    ALU.mult),
                   reads=["at_msk1", "at_msk2"], writes=["at_mask"])

            var_view = mva[:, :].rearrange("p (t two) -> p t two", two=2)[:, :, 1]
            mean_view = mva[:, :].rearrange("p (t two) -> p t two", two=2)[:, :, 0]
            ve(lambda q: q.tensor_scalar(rstda[:, :], var_view, LN_EPS, None,
                                         ALU.add), reads=["amv"], writes=["arstd"])
            act(lambda q: q.activation(rstda[:, :], rstda[:, :], AF.Sqrt),
                reads=["arstd"], writes=["arstd"])
            ve(lambda q: q.reciprocal(rstda[:, :], rstda[:, :]),
               reads=["arstd"], writes=["arstd"])
            ve(lambda q: q.scalar_tensor_tensor(ba[:, :], mean_view, -1.0,
                                                rstda[:, :], ALU.mult, ALU.mult),
               reads=["amv", "arstd"], writes=["ab"])

            def stage1(ci):
                s, c = divmod(ci, NCH)
                par = ci % 2
                base = s * SEQ + c * CH
                kxa = ("at_xa", par)
                kqs, kKp, kV, kscm = (("at_qs", par), ("at_Kp", par),
                                      ("at_V", par), ("at_scm", par))
                for tc in range(TPCH):
                    t = base // 128 + tc
                    dma("sp", xat[par][:, tc * D:(tc + 1) * D],
                        x1_h.ap()[t * 128:(t + 1) * 128, :],
                        reads=[kx1], writes=[kxa])
                    act(lambda q, t=t, tc=tc: q.activation(
                        xnb[:, :], xat[par][:, tc * D:(tc + 1) * D], AF.Identity,
                        bias=ba[:, t:t + 1], scale=rstda[:, t:t + 1]),
                        reads=[kxa, "arstd", "ab"], writes=["at_xnb"])
                    srcs = [xnb[:, dc * 128:(dc + 1) * 128] for dc in range(DC)]
                    bank, bkey = tr_batch(srcs, ["at_xnb"])
                    ve(lambda q, tc=tc, bank=bank: q.tensor_copy(
                        xnT[:, :].rearrange("p (d t) -> p d t", t=CH)
                        [:, :, tc * 128:(tc + 1) * 128],
                        bank[:, :].rearrange("p (d t) -> p d t", t=128)),
                       reads=[bkey], writes=["at_xnT"])
                for ec in range(DC):
                    bank, bkey = next_bank()
                    p.mm_group(
                        [(lambda q, s2, e2, dc=dc2, ec=ec, bank=bank: q.matmul(
                            bank[:, 0:CH],
                            wqs[:, dc * D + ec * 128: dc * D + ec * 128 + 128],
                            xnT[:, dc * CH:(dc + 1) * CH], start=s2, stop=e2))
                         for dc2 in range(DC)],
                        reads=["at_wq", "at_xnT"], writes=[bkey])
                    act(lambda q, ec=ec, bank=bank: q.activation(
                        qT[:, ec * CH:(ec + 1) * CH], bank[:, 0:CH], AF.Identity,
                        bias=bq_c[:, ec:ec + 1]),
                        reads=[bkey, "at_bqc"], writes=["at_qT"])
                ve(lambda q, par=par: q.tensor_tensor(
                    qs[par][:, :].rearrange("p (e i) -> p e i", i=CH),
                    qT[:, :].rearrange("p (e i) -> p e i", i=CH),
                    lamr_bc[:, :].rearrange("p (o i) -> p o i", o=1)
                    .broadcast_to([128, DC, CH]),
                    ALU.mult),
                   reads=["at_qT", "at_lamrbc"], writes=[kqs])
                for ec in range(DC):
                    bank, bkey = next_bank()
                    p.mm_group(
                        [(lambda q, s2, e2, dc=dc2, ec=ec, bank=bank: q.matmul(
                            bank[:, 0:CH],
                            wks[:, dc * D + ec * 128: dc * D + ec * 128 + 128],
                            xnT[:, dc * CH:(dc + 1) * CH], start=s2, stop=e2))
                         for dc2 in range(DC)],
                        reads=["at_wk", "at_xnT"], writes=[bkey])
                    act(lambda q, ec=ec, bank=bank: q.activation(
                        tmpA[:, :], bank[:, 0:CH], AF.Relu,
                        bias=bk_c[:, ec:ec + 1]),
                        reads=[bkey, "at_bkc"], writes=["at_tmpA"])
                    ve(lambda q, ec=ec, bank=bank: q.tensor_scalar(
                        tmpB[:, :], bank[:, 0:CH], bk_c[:, ec:ec + 1], 0.0,
                        ALU.add, ALU.min),
                       reads=[bkey, "at_bkc"], writes=["at_tmpB"])
                    act(lambda q: q.activation(tmpC[:, :], tmpB[:, :], AF.Exp),
                        reads=["at_tmpB"], writes=["at_tmpC"])
                    ve(lambda q, ec=ec: q.tensor_tensor(
                        kT[:, ec * CH:(ec + 1) * CH], tmpA[:, :], tmpC[:, :],
                        ALU.add),
                       reads=["at_tmpA", "at_tmpC"], writes=["at_kT"])
                for tc in range(TPCH):
                    for h in range(2):
                        bank, bkey = next_bank()
                        p.mm_group(
                            [(lambda q, s2, e2, dc=dc2, tc=tc, h=h, bank=bank:
                              q.matmul(
                                  bank[:, 0:512],
                                  xnT[:, dc * CH + tc * 128: dc * CH + tc * 128 + 128],
                                  wvs[:, dc * D + h * 512: dc * D + h * 512 + 512],
                                  start=s2, stop=e2))
                             for dc2 in range(DC)],
                            reads=["at_wv", "at_xnT"], writes=[bkey])
                        ve(lambda q, tc=tc, h=h, par=par, bank=bank: q.tensor_tensor(
                            Vs[par][:, tc * D + h * 512: tc * D + h * 512 + 512],
                            bank[:, 0:512], bv_bc[:, h * 512:(h + 1) * 512],
                            ALU.add),
                           reads=[bkey, "at_bvbc"], writes=[kV])
                for tc in range(TPCH):
                    srcs = [kT[:, ec * CH + tc * 128: ec * CH + tc * 128 + 128]
                            for ec in range(DC)]
                    bank, bkey = tr_batch(srcs, ["at_kT"])
                    act(lambda q, tc=tc, par=par, bank=bank: q.activation(
                        Kp[par][:, tc * D:(tc + 1) * D], bank[:, :],
                        AF.Identity, scale=lamK[:, tc:tc + 1]),
                        reads=[bkey, "at_lamK"], writes=[kKp])
                for jc in range(TPCH):
                    bank, bkey = next_bank()
                    p.mm_group(
                        [(lambda q, s2, e2, ec=ec2, jc=jc, bank=bank: q.matmul(
                            bank[:, 0:CH],
                            kT[:, ec * CH + jc * 128: ec * CH + jc * 128 + 128],
                            qT[:, ec * CH:(ec + 1) * CH], start=s2, stop=e2))
                         for ec2 in range(DC)],
                        reads=["at_kT", "at_qT"], writes=[bkey])
                    ve(lambda q, jc=jc, par=par, bank=bank: q.tensor_tensor(
                        scm[par][:, jc * CH:(jc + 1) * CH], bank[:, 0:CH],
                        mask[:, jc * CH:(jc + 1) * CH], ALU.mult),
                       reads=[bkey, "at_mask"], writes=[kscm])

            def stage2(ci):
                s, c = divmod(ci, NCH)
                par = ci % 2
                base = s * SEQ + c * CH
                kqs, kKp, kV, kscm = (("at_qs", par), ("at_Kp", par),
                                      ("at_V", par), ("at_scm", par))
                kxa = ("at_xa", par)
                first = (c == 0)
                last = (c == NCH - 1)
                for vc in range(DC):
                    bank, bkey = next_bank()
                    mms = [(lambda q, s2, e2, jc=jc2, vc=vc, par=par, bank=bank:
                            q.matmul(
                                bank[:, 0:CH],
                                Vs[par][:, jc * D + vc * 128: jc * D + vc * 128 + 128],
                                scm[par][:, jc * CH:(jc + 1) * CH],
                                start=s2, stop=e2))
                           for jc2 in range(TPCH)]
                    if not first:
                        mms += [(lambda q, s2, e2, ec=ec2, vc=vc, par=par,
                                 bank=bank: q.matmul(
                                     bank[:, 0:CH],
                                     memb[:, ec * D + vc * 128: ec * D + vc * 128 + 128],
                                     qs[par][:, ec * CH:(ec + 1) * CH],
                                     start=s2, stop=e2))
                                for ec2 in range(DC)]
                    p.mm_group(mms, reads=[kV, kscm, "at_memb", kqs],
                               writes=[bkey])
                    ve(lambda q, vc=vc, bank=bank: q.tensor_copy(
                        aoT[:, vc * CH:(vc + 1) * CH], bank[:, 0:CH]),
                       reads=[bkey], writes=["at_aoT"])
                if not last:
                    for ec in range(DC):
                        for h in range(2):
                            bank, bkey = next_bank()
                            p.mm_group(
                                [(lambda q, s2, e2, tc=tc2, ec=ec, h=h, par=par,
                                  bank=bank: q.matmul(
                                      bank[:, 0:512],
                                      Kp[par][:, tc * D + ec * 128: tc * D + ec * 128 + 128],
                                      Vs[par][:, tc * D + h * 512: tc * D + h * 512 + 512],
                                      start=s2, stop=e2))
                                 for tc2 in range(TPCH)],
                                reads=[kKp, kV], writes=[bkey])
                            off = ec * D + h * 512
                            if first:
                                ve(lambda q, off=off, bank=bank: q.tensor_copy(
                                    mem[:, off:off + 512], bank[:, 0:512]),
                                   reads=[bkey], writes=["at_mem"])
                            else:
                                ve(lambda q, off=off, bank=bank:
                                   q.scalar_tensor_tensor(
                                       mem[:, off:off + 512], mem[:, off:off + 512],
                                       lamC[:, 0:1], bank[:, 0:512],
                                       ALU.mult, ALU.add),
                                   reads=[bkey, "at_lamC", "at_mem"],
                                   writes=["at_mem"])
                        act(lambda q, ec=ec: q.copy(
                            memb[:, ec * D:(ec + 1) * D],
                            mem[:, ec * D:(ec + 1) * D]),
                            reads=["at_mem"], writes=["at_memb"])

                for ic in range(TPCH):
                    t = base // 128 + ic
                    banks = []
                    for h in range(2):
                        bank, bkey = next_bank()
                        banks.append((bank, bkey))
                        p.mm_group(
                            [(lambda q, s2, e2, vc=vc2, ic=ic, h=h, bank=bank:
                              q.matmul(
                                  bank[:, 0:512],
                                  aoT[:, vc * CH + ic * 128: vc * CH + ic * 128 + 128],
                                  wos[:, vc * D + h * 512: vc * D + h * 512 + 512],
                                  start=s2, stop=e2))
                             for vc2 in range(DC)],
                            reads=["at_aoT", "at_wo"], writes=[bkey])
                    for h, (bank, bkey) in enumerate(banks):
                        ve(lambda q, h=h, bank=bank: q.tensor_tensor(
                            xot[:, h * 512:(h + 1) * 512], bank[:, 0:512],
                            bo_bc[:, h * 512:(h + 1) * 512], ALU.add),
                           reads=[bkey, "at_bobc"], writes=["at_xot"])
                    ve(lambda q, ic=ic, par=par: q.tensor_tensor(
                        xot[:, :], xot[:, :], xat[par][:, ic * D:(ic + 1) * D],
                        ALU.add),
                       reads=["at_xot", kxa], writes=["at_xot"])
                    dma("g", x2_h.ap()[t * 128:(t + 1) * 128, :], xot[:, :],
                        reads=["at_xot"], writes=[kx2])
                    if "x2" in dbg:
                        dma("g", dbg["x2"].ap()[t * 128:(t + 1) * 128, :],
                            xot[:, :], reads=["at_xot"], writes=["dbg_x2w"])
                    ve(lambda q: q.bn_stats(bnst[:, 0:6], xot[:, 0:512]),
                       reads=["at_xot"], writes=["at_bn0"])
                    ve(lambda q: q.bn_stats(bnst[:, 6:12], xot[:, 512:1024]),
                       reads=["at_xot"], writes=["at_bn1"])
                    ve(lambda q, t=t: q.bn_aggr(mv2[:, 2 * t:2 * t + 2],
                                                bnst[:, :]),
                       reads=["at_bn0", "at_bn1"], writes=[("f2mv", t), "f2mv"])
                    ve(lambda q, t=t: q.tensor_scalar(
                        ntmp[:, :], mv2[:, 2 * t:2 * t + 1], -1.0, None,
                        ALU.mult),
                       reads=[("f2mv", t)], writes=["ntmp"])
                    act(lambda q, t=t: q.activation(
                        tmpD[:, :], xot[:, :], AF.Identity, bias=ntmp[:, :],
                        scale=1.0),
                        reads=["at_xot", "ntmp"], writes=["at_tmpD"])
                    ve(lambda q, t=t: q.tensor_reduce(
                        xnm2[:, t:t + 1], tmpD[:, :], axis=AX.X, op=ALU.max,
                        apply_absolute_value=True),
                       reads=["at_tmpD"], writes=["f2xnm"])


            tmpD = asb("at_tmpD", [128, D], F32)
            total = SPC * NCH
            stage1(0)
            for ci in range(total):
                if ci + 1 < total:
                    stage1(ci + 1)
                stage2(ci)
            p.full_barrier()


    if stage == "ffn1":
        zo = sb("zot", [128, D], F32)
        ve(lambda q: q.memset(zo[:, :], 0.0), writes=["zot"])
        for t in range(TT):
            dma("sp", out_h.ap()[t * 128:(t + 1) * 128, :], zo[:, :],
                reads=["zot"], writes=["outw"])
        return finish()

    attention()

    if stage == "attn":
        zo = sb("zot", [128, D], F32)
        ve(lambda q: q.memset(zo[:, :], 0.0), writes=["zot"])
        for t in range(TT):
            dma("sp", out_h.ap()[t * 128:(t + 1) * 128, :], zo[:, :],
                reads=["zot"], writes=["outw"])
        return finish()

    ffn_call("f2", x2_h, out_h, mv2, rstd2, b2, xnm2, need_pass_a=False,
             fuse_stats_mv=None, dbg_x=None)

    if "scal" in dbg:
        dma("g", dbg["scal"].ap()[0:1, 0:8], scrow[0:1, 0:8],
            reads=["scrow"], writes=["dbgscal"])

    return finish()


_NC_CACHE = {}


def _get_nc(debug_out=(), stage="full"):
    key = (tuple(sorted(debug_out)), stage)
    if key not in _NC_CACHE:
        _NC_CACHE[key] = build_nc(debug_out=key[0], stage=stage)
    return _NC_CACHE[key]


def _in_maps(inputs):
    f32 = lambda a: np.ascontiguousarray(np.asarray(a, dtype=np.float32))
    x = f32(inputs["L0_attn_x"]).reshape(NCORE * SPC, SEQ, D)
    maps = []
    for c in range(NCORE):
        m = {
            "x": x[c * SPC:(c + 1) * SPC].reshape(N, D),
            "Wq": f32(inputs["L0_attn_Wq"]),
            "Wk": f32(inputs["L0_attn_Wk"]),
            "Wv": f32(inputs["L0_attn_Wv"]),
            "Wo": f32(inputs["L0_attn_Wo"]),
            "bq": f32(inputs["L0_attn_bq"]).reshape(1, D),
            "bk": f32(inputs["L0_attn_bk"]).reshape(1, D),
            "bv": f32(inputs["L0_attn_bv"]).reshape(1, D),
            "bo": f32(inputs["L0_attn_bo"]).reshape(1, D),
            "decay": f32(inputs["L0_attn_decay_param"]).reshape(1, 1),
            "w_gate": f32(inputs["L0_ffn_w_gate"]),
            "w_up": f32(inputs["L0_ffn_w_up"]),
            "w_down": f32(inputs["L0_ffn_w_down"]),
        }
        maps.append(m)
    return maps


def kernel(**inputs):
    nc = _get_nc()
    res = run_bass_kernel_spmd(nc, _in_maps(inputs), core_ids=list(range(NCORE)))
    out = np.concatenate([r["out"].reshape(SPC, SEQ, D) for r in res.results],
                         axis=0)
    return out.astype(np.float32)


def kernel_debug(debug_out, stage="full", **inputs):
    nc = _get_nc(debug_out, stage)
    res = run_bass_kernel_spmd(nc, _in_maps(inputs), core_ids=list(range(NCORE)))
    return res


# version 10 (final)
# speedup vs baseline: 1.0457x; 1.0077x over previous
import contextlib
import numpy as np

import concourse.bass as bass
import concourse.mybir as mybir
from concourse.bass_utils import run_bass_kernel_spmd

F32 = mybir.dt.float32
BF16 = mybir.dt.bfloat16
AF = mybir.ActivationFunctionType
ALU = mybir.AluOpType
AX = mybir.AxisListType

MAGIC = float(np.float32(1.5 * 2 ** 23))
QMAX = 32767.0
WQMAX = 7.0
EPS_S = 1e-12
LN_EPS = 1e-5
SIGCLIP = float(np.float32(65535.0))
INV65536 = 1.0 / 65536.0

NCORE = 8
SEQ = 2048
D = 1024
F = 3072
SPC = 4
N = SPC * SEQ
TT = N // 128
NB = N // 512
DC = D // 128
FC = F // 128
CH = 256
NCH = SEQ // CH
TPCH = CH // 128

ENGINES = ("pe", "ve", "act", "sp", "g")


class Prog:

    DMA_POOL = {"sp": 12, "g": 4}

    def __init__(self, nc):
        self.nc = nc
        self.streams = {e: [] for e in ENGINES}
        self.cnt = {(e, "c"): 0 for e in ENGINES}
        for e, npool in self.DMA_POOL.items():
            for i in range(npool):
                self.cnt[(e + "_d", i)] = 0
        self.dma_idx = {e: 0 for e in self.DMA_POOL}
        self.sems = {}
        self.last_w = {}
        self.readers = {}
        self.watermark = {e: {} for e in ENGINES}
        self.n_cc = 0

    def _need(self, eng, keys_r, keys_w):
        need = {}
        for kk in keys_r:
            lw = self.last_w.get(kk)
            if lw is not None:
                need[lw[0]] = max(need.get(lw[0], 0), lw[1])
        for kk in keys_w:
            lw = self.last_w.get(kk)
            if lw is not None:
                need[lw[0]] = max(need.get(lw[0], 0), lw[1])
            for (ref, c) in self.readers.get(kk, ()):
                need[ref] = max(need.get(ref, 0), c)
        wm = self.watermark[eng]
        out = []
        for ref, c in need.items():
            if wm.get(ref, 0) >= c:
                continue
            out.append((ref, c))
            wm[ref] = c
        return out

    def _record(self, ref, mycnt, reads, writes):
        for kk in writes:
            self.last_w[kk] = (ref, mycnt)
            self.readers[kk] = []
        for kk in reads:
            self.readers.setdefault(kk, []).append((ref, mycnt))

    @staticmethod
    def _is_psum(k):
        return (isinstance(k, tuple) and k[0] in ("pmm", "ptrb")) or k == "pms"

    def _excl(self, reads, writes):
        pr = [k for k in reads if self._is_psum(k)]
        if not pr:
            return reads, writes
        return [k for k in reads if not self._is_psum(k)], list(writes) + pr

    def emit(self, eng, fn, reads=(), writes=(), kind="c"):
        reads, writes = self._excl(reads, writes)
        waits = self._need(eng, reads, writes)
        if kind == "d":
            npool = self.DMA_POOL[eng]
            si = self.dma_idx[eng] % npool
            self.dma_idx[eng] += 1
            ref = (eng + "_d", si)
            step = 16
            prev = self.cnt[ref]
            if prev > 0 and self.watermark[eng].get(ref, 0) < prev:
                waits = waits + [(ref, prev)]
                self.watermark[eng][ref] = prev
        else:
            ref = (eng, kind)
            step = 1
        self.cnt[ref] += step
        mycnt = self.cnt[ref]

        def thunk(q, waits=waits, fn=fn, ref=ref, step=step):
            for (r, c) in waits:
                q.wait_ge(self.sems[r], c)
            fn(q).then_inc(self.sems[ref], step)

        self.streams[eng].append(thunk)
        self._record(ref, mycnt, reads, writes)

    def emit_cc(self, fn, reads=(), writes=()):
        waits = self._need("g", reads, writes)
        ref = ("cc", self.n_cc)
        self.n_cc += 1
        self.cnt[ref] = 1

        def thunk(q, waits=waits, fn=fn, ref=ref):
            for (r, c) in waits:
                q.wait_ge(self.sems[r], c)
            fn(q).then_inc(self.sems[ref], 1)

        self.streams["g"].append(thunk)
        self._record(ref, 1, reads, writes)

    def mm_group(self, mms, reads=(), writes=()):
        reads, writes = self._excl(reads, writes)
        waits = self._need("pe", reads, writes)
        ref = ("pe", "c")
        self.cnt[ref] += 1
        mycnt = self.cnt[ref]
        n = len(mms)

        def thunk(q, waits=waits, mms=mms, n=n):
            for (r, c) in waits:
                q.wait_ge(self.sems[r], c)
            for i, f in enumerate(mms):
                inst = f(q, i == 0, i == n - 1)
                if i == n - 1:
                    inst.then_inc(self.sems[("pe", "c")], 1)

        self.streams["pe"].append(thunk)
        self._record(ref, mycnt, reads, writes)

    def full_barrier(self):
        snap = [(ref, c) for ref, c in self.cnt.items() if c > 0]
        for e in ENGINES:
            wm = self.watermark[e]
            waits = [(ref, c) for ref, c in snap if wm.get(ref, 0) < c]
            for ref, c in waits:
                wm[ref] = c

            def thunk(q, waits=waits):
                for (r, c) in waits:
                    q.wait_ge(self.sems[r], c)

            self.streams[e].append(thunk)

    def build(self, block, stack):
        for ref in self.cnt:
            self.sems[ref] = stack.enter_context(
                self.nc.semaphore(f"s_{ref[0]}_{ref[1]}"))

        def runner(e):
            def run(q):
                for t in self.streams[e]:
                    t(q)
            return run

        block.tensor(runner("pe"))
        block.vector(runner("ve"))
        block.scalar(runner("act"))
        block.sync(runner("sp"))
        block.gpsimd(runner("g"))



def build_nc(debug_out=(), stage="full"):
    nc = bass.Bass("TRN2", target_bir_lowering=False, debug=False)
    stack = contextlib.ExitStack()
    import ml_dtypes

    x0_h = nc.dram_tensor("x", [N, D], F32, kind="ExternalInput")
    wq_h = nc.dram_tensor("Wq", [D, D], F32, kind="ExternalInput")
    wk_h = nc.dram_tensor("Wk", [D, D], F32, kind="ExternalInput")
    wv_h = nc.dram_tensor("Wv", [D, D], F32, kind="ExternalInput")
    wo_h = nc.dram_tensor("Wo", [D, D], F32, kind="ExternalInput")
    bq_h = nc.dram_tensor("bq", [1, D], F32, kind="ExternalInput")
    bk_h = nc.dram_tensor("bk", [1, D], F32, kind="ExternalInput")
    bv_h = nc.dram_tensor("bv", [1, D], F32, kind="ExternalInput")
    bo_h = nc.dram_tensor("bo", [1, D], F32, kind="ExternalInput")
    dec_h = nc.dram_tensor("decay", [1, 1], F32, kind="ExternalInput")
    wgate_h = nc.dram_tensor("w_gate", [F, D], F32, kind="ExternalInput")
    wup_h = nc.dram_tensor("w_up", [F, D], F32, kind="ExternalInput")
    wdown_h = nc.dram_tensor("w_down", [D, F], F32, kind="ExternalInput")
    out_h = nc.dram_tensor("out", [N, D], F32, kind="ExternalOutput")

    x1_h = nc.dram_tensor("x1s", [N, D], F32)
    x2_h = nc.dram_tensor("x2s", [N, D], F32)
    gate_s = nc.dram_tensor("gate_s", [FC, 128, N], BF16)
    up_s = nc.dram_tensor("up_s", [FC, 128, N], BF16)
    m1_s = nc.dram_tensor("m1_s", [FC, 128, N], BF16)
    o_s = nc.dram_tensor("o_s", [N // 256, 128, FC * 256], BF16)
    wupT_h = nc.dram_tensor("wupT_s", [128, DC * F], BF16)
    wgateT_h = nc.dram_tensor("wgateT_s", [128, DC * F], BF16)
    wdownT_h = nc.dram_tensor("wdownT_s", [128, FC * D], BF16)
    cc_in_h = nc.dram_tensor("cc_in", [1, 4], F32)
    cc_out_h = nc.dram_tensor("cc_out", [1, 4], F32)

    dbg = {}
    for name in debug_out:
        if name in ("x1", "x2"):
            dbg[name] = nc.dram_tensor("dbg_" + name, [N, D], F32,
                                       kind="ExternalOutput")
        elif name in ("gate", "up", "m1"):
            dbg[name] = nc.dram_tensor("dbg_" + name, [FC, 128, N], BF16,
                                       kind="ExternalOutput")
        elif name == "scal":
            dbg[name] = nc.dram_tensor("dbg_scal", [1, 16], F32,
                                       kind="ExternalOutput")

    ident_f_h = nc.inline_tensor(np.eye(128, dtype=np.float32), name="c_idf")
    ident_b_h = nc.inline_tensor(np.eye(128, dtype=ml_dtypes.bfloat16), name="c_idb")
    ones_f_h = nc.inline_tensor(np.ones((1, 128), dtype=np.float32), name="c_ones")

    p = Prog(nc)
    sb = lambda name, shape, dt: stack.enter_context(nc.sbuf_tensor(name, shape, dt))
    psal = lambda name, shape, dt: stack.enter_context(nc.psum_tensor(name, shape, dt))

    idf = sb("idf", [128, 128], F32)
    idb = sb("idb", [128, 128], BF16)
    onesf = sb("onesf", [1, 128], F32)
    mv1 = sb("mv1", [128, 2 * TT], F32)
    mva = sb("mva", [128, 2 * TT], F32)
    mv2 = sb("mv2", [128, 2 * TT], F32)
    rstd1, b1 = sb("rstd1", [128, TT], F32), sb("b1", [128, TT], F32)
    rstda, ba = sb("rstda", [128, TT], F32), sb("ba", [128, TT], F32)
    rstd2, b2 = sb("rstd2", [128, TT], F32), sb("b2", [128, TT], F32)
    xnm1 = sb("xnm1", [128, TT], F32)
    xnm2 = sb("xnm2", [128, TT], F32)
    scbc = sb("scbc", [128, 16], F32)
    scrow = sb("scrow", [1, 8], F32)
    ntmp = sb("ntmp", [128, 1], F32)

    SC_A = 0
    SC_B = 2
    SC_NM = 6
    SC_C = 7
    SC_D = 10

    pmm = [psal(f"pmm{i}", [128, 512], F32) for i in range(4)]
    ptrb = [psal(f"ptrb{i}", [128, 1024], BF16) for i in range(2)]
    pms = psal("pms", [128, 512], F32)
    mmctr = [0]

    def next_bank():
        mmctr[0] += 1
        return pmm[mmctr[0] % 4], ("pmm", mmctr[0] % 4)

    ptrctr = [0]

    def dma(queue, out, in_, reads=(), writes=()):
        p.emit(queue, lambda q, o=out, i=in_: q.dma_start(out=o, in_=i),
               reads=reads, writes=writes, kind="d")

    def ve(fn, reads=(), writes=()):
        p.emit("ve", fn, reads=reads, writes=writes)

    def act(fn, reads=(), writes=()):
        p.emit("act", fn, reads=reads, writes=writes)

    K_IDF, K_IDB, K_ONES = "idf", "idb", "onesf"

    def bcast_row(row_ap, out_ap, row_keys, out_keys, ncols):
        p.mm_group([lambda q, s, e: q.matmul(pms[:, 0:ncols], onesf[0:1, :],
                                             row_ap, start=s, stop=e)],
                   reads=[K_ONES] + list(row_keys), writes=["pms"])
        ve(lambda q: q.tensor_copy(out_ap, pms[:, 0:ncols]),
           reads=["pms"], writes=list(out_keys))

    def tr_batch(src_aps, read_keys):
        i = ptrctr[0] % 2
        ptrctr[0] += 1
        bank, bkey = ptrb[i], ("ptrb", i)
        mms = [(lambda q, s2, e2, j=j, src=src: q.transpose(
            bank[:, j * 128:(j + 1) * 128], src, idb[:, :]))
               for j, src in enumerate(src_aps)]
        p.mm_group(mms, reads=list(read_keys) + [K_IDB], writes=[bkey])
        return bank, bkey

    def colmax_to_row(col_ap, col_key, out_col):
        p.mm_group([lambda q, s, e: q.transpose(pms[0:1, 0:128], col_ap,
                                                idf[:, :])],
                   reads=[col_key, K_IDF], writes=["pms"])
        ve(lambda q: q.tensor_reduce(scrow[0:1, out_col:out_col + 1],
                                     pms[0:1, 0:128], axis=AX.X, op=ALU.max),
           reads=["pms"], writes=[("scrow", out_col)])

    def collective_max(ncols):
        dma("g", cc_in_h.ap()[0:1, 0:ncols], scrow[0:1, 0:ncols],
            reads=[("scrow", c) for c in range(ncols)], writes=["cc_in"])
        p.emit_cc(lambda q: q.collective_compute(
            "AllReduce", ALU.max, replica_groups=[list(range(NCORE))],
            ins=[cc_in_h.ap()[0:1, 0:ncols].opt()],
            outs=[cc_out_h.ap()[0:1, 0:ncols].opt()]),
            reads=["cc_in"], writes=["cc_out"])
        dma("g", scrow[0:1, 0:ncols], cc_out_h.ap()[0:1, 0:ncols],
            reads=["cc_out"],
            writes=["scrow"] + [("scrow", c) for c in range(ncols)])

    def scale_pair(src_col, dst_cols, pre=1.0):
        c0, c1 = dst_cols
        ve(lambda q: q.tensor_scalar(scrow[0:1, c0:c0 + 1],
                                     scrow[0:1, src_col:src_col + 1],
                                     pre / QMAX, EPS_S, ALU.mult, ALU.max),
           reads=[("scrow", src_col)], writes=[("scrow", c0)])
        ve(lambda q: q.reciprocal(scrow[0:1, c1:c1 + 1], scrow[0:1, c0:c0 + 1]),
           reads=[("scrow", c0)], writes=[("scrow", c1)])

    dma("sp", idf[:, :], ident_f_h.ap(), writes=[K_IDF])
    dma("sp", idb[:, :], ident_b_h.ap(), writes=[K_IDB])
    dma("sp", onesf[:, :], ones_f_h.ap(), writes=[K_ONES])
    zcol = sb("zcol", [128, 1], F32)
    onecol = sb("onecol", [128, 1], F32)
    mgcol = sb("mgcol", [128, 1], F32)
    ve(lambda q: q.memset(zcol[:, :], 0.0), writes=["zcol"])
    ve(lambda q: q.memset(onecol[:, :], 1.0), writes=["onecol"])
    ve(lambda q: q.memset(mgcol[:, :], MAGIC), writes=["mgcol"])
    nc.const_aps.register(F32, 0.0, zcol[:, 0:1])
    nc.const_aps.register(F32, 1.0, onecol[:, 0:1])
    nc.const_aps.register(F32, MAGIC, mgcol[:, 0:1])

    import os
    PREP_OC = int(os.environ.get("KPREP_OC", "999"))
    PREP_SKIP = os.environ.get("KPREP_SKIP", "")

    def prep_ffn_weight(w_h, wT_h, O, I, name):
        OC = min(O // 128, PREP_OC)
        IC = I // 128
        nblk = I // 32
        with contextlib.ExitStack() as st2:
            wst = [st2.enter_context(nc.sbuf_tensor(f"pw_{name}_w{i}", [128, I], F32))
                   for i in range(2)]
            sblk = [st2.enter_context(nc.sbuf_tensor(f"pw_{name}_s{i}", [128, nblk],
                                                     F32)) for i in range(2)]
            invb = st2.enter_context(nc.sbuf_tensor(f"pw_{name}_i", [128, nblk], F32))
            r1 = st2.enter_context(nc.sbuf_tensor(f"pw_{name}_r", [128, I], F32))
            deq = [st2.enter_context(nc.sbuf_tensor(f"pw_{name}_d{i}", [128, I],
                                                    BF16)) for i in range(2)]
            tout = st2.enter_context(nc.sbuf_tensor(f"pw_{name}_t", [128, 1024], BF16))

            def w_front(oc):
                par = oc % 2
                dma("sp", wst[par][:, :], w_h.ap()[oc * 128:(oc + 1) * 128, :],
                    writes=[(name, "w", par)])
                ve(lambda q, par=par: q.tensor_reduce(
                    sblk[par][:, :],
                    wst[par][:, :].rearrange("p (b c) -> p b c", c=32),
                    axis=AX.X, op=ALU.max, apply_absolute_value=True),
                   reads=[(name, "w", par)], writes=[(name, "s", par)])

            def w_rest(oc):
                par = oc % 2
                ve(lambda q, par=par: q.tensor_scalar(
                    sblk[par][:, :], sblk[par][:, :],
                    1.0 / WQMAX, EPS_S, ALU.mult, ALU.max),
                   reads=[(name, "s", par)], writes=[(name, "s", par)])
                ve(lambda q, par=par: q.reciprocal(invb[:, :], sblk[par][:, :]),
                   reads=[(name, "s", par)], writes=[(name, "inv")])
                ve(lambda q, par=par: q.tensor_tensor(
                    r1[:, :].rearrange("p (b c) -> p b c", c=32),
                    wst[par][:, :].rearrange("p (b c) -> p b c", c=32),
                    invb[:, :].rearrange("p (b o) -> p b o", o=1)
                        .broadcast_to([128, nblk, 32]),
                    ALU.mult),
                   reads=[(name, "w", par), (name, "inv")], writes=[(name, "r1")])
                ve(lambda q: q.tensor_scalar(r1[:, :], r1[:, :], MAGIC, MAGIC,
                                             ALU.add, ALU.subtract),
                   reads=[(name, "r1")], writes=[(name, "r1")])
                ve(lambda q, par=par: q.tensor_tensor(
                    deq[par][:, :].rearrange("p (b c) -> p b c", c=32),
                    r1[:, :].rearrange("p (b c) -> p b c", c=32),
                    sblk[par][:, :].rearrange("p (b o) -> p b o", o=1)
                        .broadcast_to([128, nblk, 32]),
                    ALU.mult),
                   reads=[(name, "r1"), (name, "s", par)],
                   writes=[(name, "deq", par)])
                _w_tail(oc, par)

            def _w_tail(oc, par):
                for ic0 in range(0, IC, 8):
                    if "T" in PREP_SKIP:
                        continue
                    nb = min(8, IC - ic0)
                    srcs = [deq[par][:, (ic0 + j) * 128:(ic0 + j + 1) * 128]
                            for j in range(nb)]
                    bank, bkey = tr_batch(srcs, [(name, "deq", par)])
                    tk = (name, "tout")
                    ve(lambda q, nb=nb, bank=bank: q.tensor_copy(
                        tout[:, 0:nb * 128], bank[:, 0:nb * 128]),
                       reads=[bkey], writes=[tk])
                    if "S" in PREP_SKIP:
                        continue
                    dma("g",
                        wT_h.ap()[:, :].rearrange("p (i t) -> p i t", t=O)
                        [:, ic0:ic0 + nb, oc * 128:(oc + 1) * 128],
                        tout[:, 0:nb * 128].rearrange("p (i t) -> p i t", t=128),
                        reads=[tk], writes=[(name, "spill")])

            w_front(0)
            for oc in range(OC):
                if oc + 1 < OC:
                    w_front(oc + 1)
                w_rest(oc)
            p.full_barrier()

    def zero_out_finish():
        zo = sb("zot", [128, D], F32)
        ve(lambda q: q.memset(zo[:, :], 0.0), writes=["zot"])
        for t in range(TT):
            dma("sp", out_h.ap()[t * 128:(t + 1) * 128, :], zo[:, :],
                reads=["zot"], writes=["outw"])

    if stage == "consts":
        zero_out_finish()
        p.full_barrier()
        with nc.Block() as block:
            p.build(block, stack)
        stack.close()
        return nc

    prep_ffn_weight(wup_h, wupT_h, F, D, "wup")
    if stage == "prep1":
        zero_out_finish()
        p.full_barrier()
        with nc.Block() as block:
            p.build(block, stack)
        stack.close()
        return nc
    prep_ffn_weight(wgate_h, wgateT_h, F, D, "wgate")
    prep_ffn_weight(wdown_h, wdownT_h, D, F, "wdown")

    def ffn_call(tag, xin_h, xout_h, mv_t, rstd_t, b_t, xnm_t,
                 need_pass_a, fuse_stats_mv=None, dbg_x=None, upto="E"):
        kmv = tag + "mv"
        krstd, kb, kxnm = tag + "rstd", tag + "b", tag + "xnm"
        kxin, kxout = xin_h.name, xout_h.name

        if need_pass_a:
            with contextlib.ExitStack() as st2:
                xa = [st2.enter_context(nc.sbuf_tensor(f"{tag}Axa{i}", [128, D], F32))
                      for i in range(2)]
                bnst = st2.enter_context(nc.sbuf_tensor(f"{tag}Abn", [128, 12], F32))
                tsub = [st2.enter_context(nc.sbuf_tensor(f"{tag}Ats{i}", [128, D], F32))
                        for i in range(2)]
                for t in range(TT):
                    xk = (tag, "Axa", t % 2)
                    dma("sp", xa[t % 2][:, :], xin_h.ap()[t * 128:(t + 1) * 128, :],
                        reads=[kxin], writes=[xk])
                    ve(lambda q, t=t: q.bn_stats(bnst[:, 0:6], xa[t % 2][:, 0:512]),
                       reads=[xk], writes=[(tag, "Abn0")])
                    ve(lambda q, t=t: q.bn_stats(bnst[:, 6:12], xa[t % 2][:, 512:1024]),
                       reads=[xk], writes=[(tag, "Abn1")])
                    ve(lambda q, t=t: q.bn_aggr(mv_t[:, 2 * t:2 * t + 2], bnst[:, :]),
                       reads=[(tag, "Abn0"), (tag, "Abn1")],
                       writes=[(kmv, t), kmv])
                    ve(lambda q, t=t: q.tensor_scalar(
                        ntmp[:, :], mv_t[:, 2 * t:2 * t + 1], -1.0, None, ALU.mult),
                       reads=[(kmv, t)], writes=["ntmp"])
                    act(lambda q, t=t: q.activation(
                        tsub[t % 2][:, :], xa[t % 2][:, :], AF.Identity,
                        bias=ntmp[:, :], scale=1.0),
                        reads=[xk, "ntmp"], writes=[(tag, "Atsub", t % 2)])
                    ve(lambda q, t=t: q.tensor_reduce(
                        xnm_t[:, t:t + 1], tsub[t % 2][:, :], axis=AX.X,
                        op=ALU.max, apply_absolute_value=True),
                       reads=[(tag, "Atsub", t % 2)], writes=[kxnm])
                p.full_barrier()

        var_view = mv_t[:, :].rearrange("p (t two) -> p t two", two=2)[:, :, 1]
        mean_view = mv_t[:, :].rearrange("p (t two) -> p t two", two=2)[:, :, 0]
        ve(lambda q: q.tensor_scalar(rstd_t[:, :], var_view, LN_EPS, None, ALU.add),
           reads=[kmv], writes=[krstd])
        act(lambda q: q.activation(rstd_t[:, :], rstd_t[:, :], AF.Sqrt),
            reads=[krstd], writes=[krstd])
        ve(lambda q: q.reciprocal(rstd_t[:, :], rstd_t[:, :]),
           reads=[krstd], writes=[krstd])
        ve(lambda q: q.scalar_tensor_tensor(b_t[:, :], mean_view, -1.0,
                                            rstd_t[:, :], ALU.mult, ALU.mult),
           reads=[kmv, krstd], writes=[kb])
        ve(lambda q: q.tensor_tensor(xnm_t[:, :], xnm_t[:, :], rstd_t[:, :],
                                     ALU.mult),
           reads=[kxnm, krstd], writes=[kxnm])
        ve(lambda q: q.tensor_reduce(scbc[:, 15:16], xnm_t[:, :], axis=AX.X,
                                     op=ALU.max),
           reads=[kxnm], writes=[(tag, "xcol")])
        colmax_to_row(scbc[:, 15:16], (tag, "xcol"), 0)
        collective_max(1)
        scale_pair(0, (0, 1))
        bcast_row(scrow[0:1, 0:2], scbc[:, SC_A:SC_A + 2],
                  [("scrow", 0), ("scrow", 1)], ["scbcA"], 2)
        if upto == "A":
            return

        with contextlib.ExitStack() as st2:
            wupT = st2.enter_context(nc.sbuf_tensor(f"{tag}BwU", [128, DC * F], BF16))
            wgateT = st2.enter_context(nc.sbuf_tensor(f"{tag}BwG", [128, DC * F], BF16))
            xba = [st2.enter_context(nc.sbuf_tensor(f"{tag}Bxa{i}", [128, D], F32))
                   for i in range(2)]
            xn = [st2.enter_context(nc.sbuf_tensor(f"{tag}Bxn{i}", [128, D], F32))
                  for i in range(2)]
            xq = [st2.enter_context(nc.sbuf_tensor(f"{tag}Bxq{i}", [128, D], BF16))
                  for i in range(2)]
            xqT = [st2.enter_context(nc.sbuf_tensor(f"{tag}BxqT{i}", [128, DC * 512],
                                                    BF16)) for i in range(2)]
            evu = [st2.enter_context(nc.sbuf_tensor(f"{tag}Bevu{i}", [128, 4 * 512],
                                                    BF16)) for i in range(2)]
            evg = [st2.enter_context(nc.sbuf_tensor(f"{tag}Bevg{i}", [128, 4 * 512],
                                                    BF16)) for i in range(2)]
            upmaxs = st2.enter_context(nc.sbuf_tensor(f"{tag}Bum", [128, FC * NB], F32))
            gatemaxs = st2.enter_context(nc.sbuf_tensor(f"{tag}Bgm", [128, FC * NB],
                                                        F32))
            dma("sp", wupT[:, :], wupT_h.ap(), reads=[("wup", "spill")],
                writes=[(tag, "BwU")])
            dma("sp", wgateT[:, :], wgateT_h.ap(), reads=[("wgate", "spill")],
                writes=[(tag, "BwG")])
            def b_prep(blk):
                par = blk % 2
                kxqT = (tag, "BxqT", par)
                for ti in range(4):
                    t = blk * 4 + ti
                    tp = t % 2
                    xk = (tag, "Bxa", tp)
                    dma("sp", xba[tp][:, :], xin_h.ap()[t * 128:(t + 1) * 128, :],
                        reads=[kxin], writes=[xk])
                    act(lambda q, t=t, tp=tp: q.activation(
                        xn[tp][:, :], xba[tp][:, :], AF.Identity,
                        bias=b_t[:, t:t + 1], scale=rstd_t[:, t:t + 1]),
                        reads=[xk, krstd, kb], writes=[(tag, "Bxn", tp)])
                    ve(lambda q, tp=tp: q.tensor_scalar(
                        xn[tp][:, :], xn[tp][:, :], scbc[:, SC_A + 1:SC_A + 2],
                        MAGIC, ALU.mult, ALU.add),
                       reads=[(tag, "Bxn", tp), "scbcA"],
                       writes=[(tag, "Bxn", tp)])
                    ve(lambda q, tp=tp: q.tensor_scalar(
                        xq[tp][:, :], xn[tp][:, :], MAGIC,
                        scbc[:, SC_A:SC_A + 1], ALU.subtract, ALU.mult),
                       reads=[(tag, "Bxn", tp), "scbcA"],
                       writes=[(tag, "Bxq", tp)])
                    srcs = [xq[tp][:, dc * 128:(dc + 1) * 128] for dc in range(DC)]
                    bank, bkey = tr_batch(srcs, [(tag, "Bxq", tp)])
                    ve(lambda q, ti=ti, par=par, bank=bank: q.tensor_copy(
                        xqT[par][:, :].rearrange("p (d t) -> p d t", t=512)
                        [:, :, ti * 128:(ti + 1) * 128],
                        bank[:, :].rearrange("p (d t) -> p d t", t=128)),
                       reads=[bkey], writes=[kxqT])
            def b_mms(blk):
                par = blk % 2
                kxqT = (tag, "BxqT", par)
                for (wT, kwT, scr_h, ev2, kev, maxsl, kmaxsl) in (
                        (wupT, (tag, "BwU"), up_s, evu, (tag, "Bevu"), upmaxs,
                         "upmaxs"),
                        (wgateT, (tag, "BwG"), gate_s, evg, (tag, "Bevg"),
                         gatemaxs, "gatemaxs")):
                    for fc in range(FC):
                        bank, bkey = next_bank()
                        p.mm_group(
                            [(lambda q, s, e, dc=dc2, bank=bank, wT=wT, fc=fc,
                              par=par, xqT=xqT: q.matmul(
                                  bank[:, 0:512],
                                  wT[:, dc * F + fc * 128: dc * F + fc * 128 + 128],
                                  xqT[par][:, dc * 512:(dc + 1) * 512],
                                  start=s, stop=e))
                             for dc2 in range(DC)],
                            reads=[kwT, kxqT], writes=[bkey])
                        gi = (fc // 4) % 2
                        sl = fc % 4
                        act(lambda q, bank=bank, ev2=ev2, gi=gi, sl=sl: q.copy(
                            ev2[gi][:, sl * 512:(sl + 1) * 512], bank[:, :]),
                            reads=[bkey], writes=[(kev, gi, sl)])
                        ve(lambda q, bank=bank, maxsl=maxsl, col=blk * FC + fc:
                           q.tensor_reduce(maxsl[:, col:col + 1], bank[:, :],
                                           axis=AX.X, op=ALU.max,
                                           apply_absolute_value=True),
                           reads=[bkey], writes=[kmaxsl])
                        if sl == 3:
                            dma("g",
                                scr_h.ap()[fc - 3:fc + 1, :,
                                           blk * 512:(blk + 1) * 512]
                                .rearrange("f p t -> p f t"),
                                ev2[gi][:, :].rearrange("p (f t) -> p f t", t=512),
                                reads=[(kev, gi, 0), (kev, gi, 1), (kev, gi, 2),
                                       (kev, gi, 3)],
                                writes=[scr_h.name])

            b_prep(0)
            for blk in range(NB):
                if blk + 1 < NB:
                    b_prep(blk + 1)
                b_mms(blk)
            ve(lambda q, upmaxs=upmaxs: q.tensor_reduce(
                scbc[:, 15:16], upmaxs[:, :], axis=AX.X,
                op=ALU.max), reads=["upmaxs"],
               writes=[(tag, "ucol")])
            colmax_to_row(scbc[:, 15:16], (tag, "ucol"), 0)
            ve(lambda q, gatemaxs=gatemaxs: q.tensor_reduce(
                scbc[:, 15:16], gatemaxs[:, :], axis=AX.X,
                op=ALU.max), reads=["gatemaxs"],
               writes=[(tag, "gcol")])
            colmax_to_row(scbc[:, 15:16], (tag, "gcol"), 1)
            p.full_barrier()

        collective_max(2)
        scale_pair(0, (2, 3))
        scale_pair(1, (4, 5))
        ve(lambda q: q.tensor_scalar(scrow[0:1, 6:7], scrow[0:1, 4:5],
                                     -MAGIC, None, ALU.mult),
           reads=[("scrow", 4)], writes=[("scrow", 6)])
        bcast_row(scrow[0:1, 2:7], scbc[:, SC_B:SC_B + 5],
                  [("scrow", c) for c in (2, 3, 4, 5, 6)], ["scbcB"], 5)
        if upto == "B":
            return

        HP = N // 2
        with contextlib.ExitStack() as st2:
            gtp = [st2.enter_context(nc.sbuf_tensor(f"{tag}Cg{i}", [128, HP], BF16))
                   for i in range(2)]
            ct1 = [st2.enter_context(nc.sbuf_tensor(f"{tag}Ct1{i}", [128, HP], F32))
                   for i in range(2)]
            ct2 = [st2.enter_context(nc.sbuf_tensor(f"{tag}Ct2{i}", [128, HP], F32))
                   for i in range(2)]
            ct3 = [st2.enter_context(nc.sbuf_tensor(f"{tag}Ct3{i}", [128, HP], F32))
                   for i in range(2)]
            m1p = [st2.enter_context(nc.sbuf_tensor(f"{tag}Cm{i}", [128, HP], BF16))
                   for i in range(2)]
            m1maxs = st2.enter_context(nc.sbuf_tensor(f"{tag}Cmm", [128, 2 * FC],
                                                      F32))
            def c_front(idx):
                fc, h = idx // 2, idx % 2
                par = idx % 2
                kg = (tag, "Cg", par)
                dma("sp", gtp[par][:, :], gate_s.ap()[fc, :, h * HP:(h + 1) * HP],
                    reads=[gate_s.name], writes=[kg])
                act(lambda q, par=par: q.activation(
                    ct1[par][:, :], gtp[par][:, :], AF.Identity,
                    scale=scbc[:, SC_B + 3:SC_B + 4], bias=mgcol[:, :]),
                    reads=[kg, "scbcB", "mgcol"], writes=[(tag, "Ct1", par)])

            def c_rest(idx):
                fc, h = idx // 2, idx % 2
                par = idx % 2
                km = (tag, "Cm", par)
                act(lambda q, par=par: q.activation(
                    ct3[par][:, :], ct1[par][:, :], AF.Sigmoid,
                    scale=scbc[:, SC_B + 2:SC_B + 3],
                    bias=scbc[:, SC_NM:SC_NM + 1]),
                    reads=[(tag, "Ct1", par), "scbcB"], writes=[(tag, "Ct3", par)])
                act(lambda q, par=par: q.activation(
                    ct2[par][:, :], ct3[par][:, :], AF.Identity,
                    scale=65536.0, bias=mgcol[:, :]),
                    reads=[(tag, "Ct3", par), "mgcol"], writes=[(tag, "Ct2", par)])
                ve(lambda q, par=par: q.tensor_scalar(
                    ct3[par][:, :], ct2[par][:, :], MAGIC,
                    SIGCLIP, ALU.subtract, ALU.min),
                   reads=[(tag, "Ct2", par)], writes=[(tag, "Ct3", par)])
                ve(lambda q, par=par: q.scalar_tensor_tensor(
                    m1p[par][:, :], ct1[par][:, :], MAGIC, ct3[par][:, :],
                    ALU.subtract, ALU.mult),
                   reads=[(tag, "Ct1", par), (tag, "Ct3", par)], writes=[km])
                ve(lambda q, par=par, idx=idx, m1maxs=m1maxs, m1p=m1p:
                   q.tensor_reduce(m1maxs[:, idx:idx + 1], m1p[par][:, :],
                                   axis=AX.X, op=ALU.max,
                                   apply_absolute_value=True),
                   reads=[km], writes=["m1maxs"])
                dma("g", m1_s.ap()[fc, :, h * HP:(h + 1) * HP], m1p[par][:, :],
                    reads=[km], writes=[m1_s.name])

            c_front(0)
            c_front(1)
            for idx in range(2 * FC):
                c_rest(idx)
                if idx + 2 < 2 * FC:
                    c_front(idx + 2)
            ve(lambda q, m1maxs=m1maxs: q.tensor_reduce(
                scbc[:, 15:16], m1maxs[:, :], axis=AX.X,
                op=ALU.max), reads=["m1maxs"],
               writes=[(tag, "mcol")])
            colmax_to_row(scbc[:, 15:16], (tag, "mcol"), 0)
            p.full_barrier()

        collective_max(1)
        ve(lambda q: q.tensor_tensor(scrow[0:1, 0:1], scrow[0:1, 0:1],
                                     scrow[0:1, 4:5], ALU.mult),
           reads=[("scrow", 0), ("scrow", 4)], writes=[("scrow", 0)])
        scale_pair(0, (0, 1), pre=INV65536)
        ve(lambda q: q.tensor_scalar(scrow[0:1, 2:3], scrow[0:1, 1:2],
                                     scrow[0:1, 4:5], INV65536, ALU.mult,
                                     ALU.mult),
           reads=[("scrow", 1), ("scrow", 4)], writes=[("scrow", 2)])
        bcast_row(scrow[0:1, 0:3], scbc[:, SC_C:SC_C + 3],
                  [("scrow", 0), ("scrow", 1), ("scrow", 2)], ["scbcC"], 3)
        if upto == "C":
            return

        with contextlib.ExitStack() as st2:
            m1d = [st2.enter_context(nc.sbuf_tensor(f"{tag}Dm{i}", [128, HP], BF16))
                   for i in range(2)]
            upd = [st2.enter_context(nc.sbuf_tensor(f"{tag}Du{i}", [128, HP], BF16))
                   for i in range(2)]
            dt1 = [st2.enter_context(nc.sbuf_tensor(f"{tag}Dt1{i}", [128, HP], F32))
                   for i in range(2)]
            dt2 = [st2.enter_context(nc.sbuf_tensor(f"{tag}Dt2{i}", [128, HP], F32))
                   for i in range(2)]
            dt3 = [st2.enter_context(nc.sbuf_tensor(f"{tag}Dt3{i}", [128, HP], BF16))
                   for i in range(2)]
            omaxs = st2.enter_context(nc.sbuf_tensor(f"{tag}Dom", [128, 2 * FC],
                                                     F32))
            def d_front(idx):
                fc, h = idx // 2, idx % 2
                par = idx % 2
                km = (tag, "Dm", par)
                ku = (tag, "Du", par)
                dma("sp", m1d[par][:, :], m1_s.ap()[fc, :, h * HP:(h + 1) * HP],
                    reads=[m1_s.name], writes=[km])
                dma("sp", upd[par][:, :], up_s.ap()[fc, :, h * HP:(h + 1) * HP],
                    reads=[up_s.name], writes=[ku])
                act(lambda q, par=par: q.activation(
                    dt1[par][:, :], m1d[par][:, :], AF.Identity,
                    scale=scbc[:, SC_C + 2:SC_C + 3], bias=mgcol[:, :]),
                    reads=[km, "scbcC", "mgcol"], writes=[(tag, "Dt1", par)])
                act(lambda q, par=par: q.activation(
                    dt2[par][:, :], upd[par][:, :], AF.Identity,
                    scale=scbc[:, SC_B + 1:SC_B + 2], bias=mgcol[:, :]),
                    reads=[ku, "scbcB", "mgcol"], writes=[(tag, "Dt2", par)])

            def d_rest(idx):
                fc, h = idx // 2, idx % 2
                par = idx % 2
                ve(lambda q, par=par: q.tensor_scalar(
                    dt1[par][:, :], dt1[par][:, :], MAGIC, scbc[:, SC_C:SC_C + 1],
                    ALU.subtract, ALU.mult),
                   reads=[(tag, "Dt1", par), "scbcC"], writes=[(tag, "Dt1", par)])
                p.emit("g", lambda q, par=par: q.tensor_scalar(
                    dt2[par][:, :], dt2[par][:, :], -MAGIC, scbc[:, SC_B:SC_B + 1],
                    ALU.add, ALU.mult),
                       reads=[(tag, "Dt2", par), "scbcB"],
                       writes=[(tag, "Dt2", par)])
                eng_o = "g" if (idx % 2 == 0) else "ve"
                p.emit(eng_o, lambda q, par=par: q.tensor_tensor(
                    dt3[par][:, :], dt1[par][:, :], dt2[par][:, :], ALU.mult),
                       reads=[(tag, "Dt1", par), (tag, "Dt2", par)],
                       writes=[(tag, "Dt3", par)])
                ve(lambda q, idx=idx, par=par, omaxs=omaxs: q.tensor_reduce(
                    omaxs[:, idx:idx + 1], dt3[par][:, :], axis=AX.X, op=ALU.max,
                    apply_absolute_value=True),
                   reads=[(tag, "Dt3", par)], writes=["omaxs"])
                ng_h = HP // 256
                dma("g",
                    o_s.ap()[h * ng_h:(h + 1) * ng_h, :, fc * 256:(fc + 1) * 256]
                    .rearrange("g p t -> p g t"),
                    dt3[par][:, :].rearrange("p (g t) -> p g t", t=256),
                    reads=[(tag, "Dt3", par)], writes=[o_s.name])

            d_front(0)
            d_front(1)
            for idx in range(2 * FC):
                d_rest(idx)
                if idx + 2 < 2 * FC:
                    d_front(idx + 2)
            ve(lambda q, omaxs=omaxs: q.tensor_reduce(
                scbc[:, 15:16], omaxs[:, :], axis=AX.X,
                op=ALU.max), reads=["omaxs"],
               writes=[(tag, "ocol")])
            colmax_to_row(scbc[:, 15:16], (tag, "ocol"), 0)
            p.full_barrier()

        collective_max(1)
        scale_pair(0, (0, 1))
        ve(lambda q: q.tensor_scalar(scrow[0:1, 2:3], scrow[0:1, 0:1], 0.5,
                                     None, ALU.mult),
           reads=[("scrow", 0)], writes=[("scrow", 2)])
        ve(lambda q: q.tensor_copy(scrow[0:1, 3:4], scrow[0:1, 1:2]),
           reads=[("scrow", 1)], writes=[("scrow", 3)])
        bcast_row(scrow[0:1, 2:4], scbc[:, SC_D:SC_D + 2],
                  [("scrow", 2), ("scrow", 3)], ["scbcD"], 2)
        if upto == "D":
            return

        EG = 256
        NEG = N // EG
        W = FC * EG
        with contextlib.ExitStack() as st2:
            wdT = st2.enter_context(nc.sbuf_tensor(f"{tag}EwD", [128, FC * D], BF16))
            oe = [st2.enter_context(nc.sbuf_tensor(f"{tag}Eo{i}", [128, W], BF16))
                  for i in range(2)]
            xe = [st2.enter_context(nc.sbuf_tensor(f"{tag}Ex{i}", [128, 2 * D], F32))
                  for i in range(2)]
            et1 = [st2.enter_context(nc.sbuf_tensor(f"{tag}Et1{i}", [128, W], F32))
                   for i in range(2)]
            oq = [st2.enter_context(nc.sbuf_tensor(f"{tag}Eoq{i}", [128, W], BF16))
                  for i in range(2)]
            xo = [st2.enter_context(nc.sbuf_tensor(f"{tag}Exo{i}", [128, 2 * D], F32))
                  for i in range(2)]
            ebn = st2.enter_context(nc.sbuf_tensor(f"{tag}Ebn", [128, 12], F32))
            dma("sp", wdT[:, :], wdownT_h.ap(), reads=[("wdown", "spill")],
                writes=[(tag, "EwD")])
            def e_front(g):
                par = g % 2
                ko = (tag, "Eo", par)
                kx = (tag, "Ex", par)
                dma("sp", oe[par][:, :], o_s.ap()[g, :, :],
                    reads=[o_s.name], writes=[ko])
                dma("sp", xe[par][:, :].rearrange("p (a b) -> p a b", b=D),
                    xin_h.ap()[g * EG:(g + 1) * EG, :]
                    .rearrange("(a t) b -> t a b", t=128),
                    reads=[kxin], writes=[kx])
                act(lambda q, par=par: q.activation(
                    et1[par][:, :], oe[par][:, :], AF.Identity,
                    scale=scbc[:, SC_D + 1:SC_D + 2], bias=mgcol[:, :]),
                    reads=[ko, "scbcD", "mgcol"], writes=[(tag, "Et1", par)])

            def e_rest(g):
                par = g % 2
                kx = (tag, "Ex", par)
                koq = (tag, "Eoq", par)
                kxo = (tag, "Exo", par)
                ve(lambda q, par=par: q.tensor_scalar(
                    oq[par][:, :], et1[par][:, :], MAGIC, scbc[:, SC_D:SC_D + 1],
                    ALU.subtract, ALU.mult),
                   reads=[(tag, "Et1", par), "scbcD"], writes=[koq])
                for tc in range(2):
                    banks = []
                    for hh in range(2):
                        bank, bkey = next_bank()
                        banks.append((bank, bkey))
                        p.mm_group(
                            [(lambda q, s, e, fc=fc2, bank=bank, hh=hh, par=par,
                              tc=tc, oq=oq: q.matmul(
                                  bank[:, 0:512],
                                  oq[par][:, fc * EG + tc * 128:
                                          fc * EG + tc * 128 + 128],
                                  wdT[:, fc * D + hh * 512:
                                      fc * D + hh * 512 + 512],
                                  start=s, stop=e))
                             for fc2 in range(FC)],
                            reads=[koq, (tag, "EwD")], writes=[bkey])
                    for hh, (bank, bkey) in enumerate(banks):
                        ve(lambda q, bank=bank, hh=hh, tc=tc, par=par:
                           q.tensor_tensor(
                               xo[par][:, tc * D + hh * 512:
                                       tc * D + hh * 512 + 512],
                               bank[:, 0:512],
                               xe[par][:, tc * D + hh * 512:
                                       tc * D + hh * 512 + 512],
                               ALU.add),
                           reads=[bkey, kx], writes=[kxo])
                dma("g",
                    xout_h.ap()[g * EG:(g + 1) * EG, :]
                    .rearrange("(a t) b -> t a b", t=128),
                    xo[par][:, :].rearrange("p (a b) -> p a b", b=D),
                    reads=[kxo], writes=[kxout])
                if dbg_x is not None:
                    dma("g",
                        dbg_x.ap()[g * EG:(g + 1) * EG, :]
                        .rearrange("(a t) b -> t a b", t=128),
                        xo[par][:, :].rearrange("p (a b) -> p a b", b=D),
                        reads=[kxo], writes=["dbgx" + tag])
                if fuse_stats_mv is not None:
                    fmv, kfmv = fuse_stats_mv
                    for tc in range(2):
                        tt2 = g * 2 + tc
                        ve(lambda q, par=par, tc=tc: q.bn_stats(
                            ebn[:, 0:6], xo[par][:, tc * D:tc * D + 512]),
                           reads=[kxo], writes=[(tag, "Ebn0")])
                        ve(lambda q, par=par, tc=tc: q.bn_stats(
                            ebn[:, 6:12], xo[par][:, tc * D + 512:tc * D + 1024]),
                           reads=[kxo], writes=[(tag, "Ebn1")])
                        ve(lambda q, tt2=tt2, fmv=fmv: q.bn_aggr(
                            fmv[:, 2 * tt2:2 * tt2 + 2], ebn[:, :]),
                           reads=[(tag, "Ebn0"), (tag, "Ebn1")],
                           writes=[(kfmv, tt2), kfmv])

            e_front(0)
            for g in range(NEG):
                e_rest(g)
                if g + 1 < NEG:
                    e_front(g + 1)
            p.full_barrier()


    def finish():
        p.full_barrier()
        with nc.Block() as block:
            p.build(block, stack)
        stack.close()
        return nc

    if stage == "prep":
        zo = sb("zot", [128, D], F32)
        ve(lambda q: q.memset(zo[:, :], 0.0), writes=["zot"])
        for t in range(TT):
            dma("sp", out_h.ap()[t * 128:(t + 1) * 128, :], zo[:, :],
                reads=["zot"], writes=["outw"])
        return finish()

    _upto = stage[2] if stage.startswith("f1") and len(stage) > 2 else "E"
    ffn_call("f1", x0_h, x1_h, mv1, rstd1, b1, xnm1, need_pass_a=True,
             fuse_stats_mv=(mva, "amv"), dbg_x=dbg.get("x1"), upto=_upto)
    if stage.startswith("f1"):
        zero_out_finish()
        return finish()
    for nm, scr in (("gate", gate_s), ("up", up_s), ("m1", m1_s)):
        if nm in dbg:
            for fc in range(FC):
                dma("g", dbg[nm].ap()[fc, :, :], scr.ap()[fc, :, :],
                    reads=[scr.name], writes=["dbg_" + nm])

    def attention():
        kx1, kx2 = x1_h.name, x2_h.name
        with contextlib.ExitStack() as st2:
            asb = lambda nm, sh, dt: st2.enter_context(nc.sbuf_tensor(nm, sh, dt))
            wqs = asb("at_wq", [128, DC * D], BF16)
            wks = asb("at_wk", [128, DC * D], BF16)
            wvs = asb("at_wv", [128, DC * D], BF16)
            wos = asb("at_wo", [128, DC * D], BF16)
            brow = asb("at_brow", [1, D], F32)
            bq_c = asb("at_bqc", [128, DC], F32)
            bk_c = asb("at_bkc", [128, DC], F32)
            bv_bc = asb("at_bvbc", [128, D], BF16)
            bo_bc = asb("at_bobc", [128, D], BF16)
            lam = asb("at_lam", [1, 4], F32)
            lnl_bc = asb("at_lnlbc", [128, 1], F32)
            lamC = asb("at_lamC", [128, 1], F32)
            lamr_bc = asb("at_lamrbc", [128, CH], BF16)
            lamK = asb("at_lamK", [128, TPCH], F32)
            mask = asb("at_mask", [128, TPCH * CH], BF16)
            xot = asb("at_xot", [128, D], F32)
            iot = xot[:, 0:CH]
            msk1 = xot[:, CH:2 * CH]
            msk2 = xot[:, 2 * CH:3 * CH]
            lamrow = xot[0:1, 3 * CH:3 * CH + CH]
            mem = asb("at_mem", [128, DC * D], F32)
            memb = asb("at_memb", [128, DC * D], BF16)
            xat = [asb(f"at_xa{i}", [128, TPCH * D], F32) for i in range(3)]
            xnb = [asb(f"at_xnb{i}", [128, TPCH * D], BF16) for i in range(2)]
            xnT = asb("at_xnT", [128, DC * CH], BF16)
            qT = asb("at_qT", [128, DC * CH], BF16)
            qs = [asb(f"at_qs{i}", [128, DC * CH], BF16) for i in range(2)]
            kT = asb("at_kT", [128, DC * CH], BF16)
            tmpA = asb("at_tmpA", [128, CH], F32)
            tmpB = asb("at_tmpB", [128, CH], F32)
            tmpC = asb("at_tmpC", [128, CH], F32)
            Kp = [asb(f"at_Kp{i}", [128, TPCH * D], BF16) for i in range(2)]
            Vs = [asb(f"at_V{i}", [128, TPCH * D], BF16) for i in range(2)]
            scm = [asb(f"at_scm{i}", [128, TPCH * CH], BF16) for i in range(2)]
            aoT = asb("at_aoT", [128, DC * CH], BF16)
            bnst = asb("at_bn", [128, 12], F32)

            for (w_h, ws, kws) in ((wq_h, wqs, "at_wq"), (wk_h, wks, "at_wk"),
                                   (wv_h, wvs, "at_wv"), (wo_h, wos, "at_wo")):
                dma("g", ws[:, :].rearrange("p (c d) -> p c d", d=D),
                    w_h.ap().rearrange("(c p) d -> p c d", p=128),
                    writes=[kws])
            for (b_h, dst, kdst) in ((bq_h, bq_c, "at_bqc"), (bk_h, bk_c, "at_bkc")):
                dma("sp", brow[:, :], b_h.ap(), writes=["at_brow"])
                for ec in range(DC):
                    p.mm_group([lambda q, s, e, ec=ec: q.transpose(
                        pms[0:128, 0:1], brow[0:1, ec * 128:(ec + 1) * 128],
                        idf[0:1, 0:1])],
                        reads=["at_brow", K_IDF], writes=["pms"])
                    ve(lambda q, ec=ec, dst=dst: q.tensor_copy(
                        dst[:, ec:ec + 1], pms[:, 0:1]),
                       reads=["pms"], writes=[kdst])
            for (b_h, dst, kdst) in ((bv_h, bv_bc, "at_bvbc"), (bo_h, bo_bc, "at_bobc")):
                dma("sp", brow[:, :], b_h.ap(), writes=["at_brow"])
                for h in range(2):
                    bcast_row(brow[0:1, h * 512:(h + 1) * 512],
                              dst[:, h * 512:(h + 1) * 512], ["at_brow"], [kdst], 512)
            dma("sp", lam[0:1, 0:1], dec_h.ap(), writes=["at_lam"])
            act(lambda q: q.activation(lam[0:1, 1:2], lam[0:1, 0:1], AF.Sigmoid),
                reads=["at_lam"], writes=[("at_lam", 1)])
            ve(lambda q: q.tensor_scalar(lam[0:1, 2:3], lam[0:1, 1:2],
                                         float(2.0 ** -60), None, ALU.max),
               reads=[("at_lam", 1)], writes=[("at_lam", 2)])
            act(lambda q: q.activation(lam[0:1, 3:4], lam[0:1, 2:3], AF.Ln),
                reads=[("at_lam", 2)], writes=[("at_lam", 3)])
            bcast_row(lam[0:1, 3:4], lnl_bc[:, :], [("at_lam", 3)], ["at_lnlbc"], 1)
            act(lambda q: q.activation(lamC[:, :], lnl_bc[:, :], AF.Exp,
                                       scale=float(CH)),
                reads=["at_lnlbc"], writes=["at_lamC"])
            p.emit("g", lambda q: q.iota(lamrow, pattern=[[1, CH]], base=0,
                                         channel_multiplier=0,
                                         allow_small_or_imprecise_dtypes=True),
                   writes=["at_xot"])
            act(lambda q: q.activation(lamrow, lamrow, AF.Exp,
                                       scale=lam[0:1, 3:4]),
                reads=["at_xot", ("at_lam", 3)], writes=["at_xot"])
            bcast_row(lamrow, lamr_bc[:, :], ["at_xot"], ["at_lamrbc"], CH)
            p.emit("g", lambda q: q.iota(iot[:, 0:TPCH], pattern=[[-128, TPCH]],
                                         base=CH - 1, channel_multiplier=-1,
                                         allow_small_or_imprecise_dtypes=True),
                   writes=["at_xot"])
            act(lambda q: q.activation(lamK[:, :], iot[:, 0:TPCH], AF.Exp,
                                       scale=lnl_bc[:, 0:1]),
                reads=["at_xot", "at_lnlbc"], writes=["at_lamK"])
            for jc in range(TPCH):
                p.emit("g", lambda q, jc=jc: q.iota(
                    iot, pattern=[[1, CH]], base=-1 - jc * 128,
                    channel_multiplier=-1,
                    allow_small_or_imprecise_dtypes=True),
                    writes=["at_xot"])
                act(lambda q: q.activation(msk1, iot, AF.Relu),
                    reads=["at_xot"], writes=["at_xot"])
                act(lambda q: q.activation(msk2, iot, AF.Relu,
                                           bias=onecol[:, :]),
                    reads=["at_xot", "onecol"], writes=["at_xot"])
                ve(lambda q: q.tensor_tensor(msk2, msk2, msk1,
                                             ALU.subtract),
                   reads=["at_xot", "at_xot"], writes=["at_xot"])
                act(lambda q: q.activation(msk1, msk1, AF.Exp,
                                           scale=lnl_bc[:, 0:1]),
                    reads=["at_xot", "at_lnlbc"], writes=["at_xot"])
                ve(lambda q, jc=jc: q.tensor_tensor(
                    mask[:, jc * CH:(jc + 1) * CH], msk1, msk2,
                    ALU.mult),
                   reads=["at_xot", "at_xot"], writes=["at_mask"])

            var_view = mva[:, :].rearrange("p (t two) -> p t two", two=2)[:, :, 1]
            mean_view = mva[:, :].rearrange("p (t two) -> p t two", two=2)[:, :, 0]
            ve(lambda q: q.tensor_scalar(rstda[:, :], var_view, LN_EPS, None,
                                         ALU.add), reads=["amv"], writes=["arstd"])
            act(lambda q: q.activation(rstda[:, :], rstda[:, :], AF.Sqrt),
                reads=["arstd"], writes=["arstd"])
            ve(lambda q: q.reciprocal(rstda[:, :], rstda[:, :]),
               reads=["arstd"], writes=["arstd"])
            ve(lambda q: q.scalar_tensor_tensor(ba[:, :], mean_view, -1.0,
                                                rstda[:, :], ALU.mult, ALU.mult),
               reads=["amv", "arstd"], writes=["ab"])

            def stage0(ci):
                s, c = divmod(ci, NCH)
                par3 = ci % 3
                par = ci % 2
                base = s * SEQ + c * CH
                kxa = ("at_xa", par3)
                for tc in range(TPCH):
                    t = base // 128 + tc
                    dma("sp", xat[par3][:, tc * D:(tc + 1) * D],
                        x1_h.ap()[t * 128:(t + 1) * 128, :],
                        reads=[kx1], writes=[kxa])
                    act(lambda q, t=t, tc=tc, par3=par3, par=par: q.activation(
                        xnb[par][:, tc * D:(tc + 1) * D],
                        xat[par3][:, tc * D:(tc + 1) * D], AF.Identity,
                        bias=ba[:, t:t + 1], scale=rstda[:, t:t + 1]),
                        reads=[kxa, "arstd", "ab"], writes=[("at_xnb", par, tc)])

            def stage1(ci):
                s, c = divmod(ci, NCH)
                par = ci % 2
                base = s * SEQ + c * CH
                kqs, kKp, kV, kscm = (("at_qs", par), ("at_Kp", par),
                                      ("at_V", par), ("at_scm", par))
                for tc in range(TPCH):
                    srcs = [xnb[par][:, tc * D + dc * 128: tc * D + (dc + 1) * 128]
                            for dc in range(DC)]
                    bank, bkey = tr_batch(srcs, [("at_xnb", par, tc)])
                    ve(lambda q, tc=tc, bank=bank: q.tensor_copy(
                        xnT[:, :].rearrange("p (d t) -> p d t", t=CH)
                        [:, :, tc * 128:(tc + 1) * 128],
                        bank[:, :].rearrange("p (d t) -> p d t", t=128)),
                       reads=[bkey], writes=["at_xnT"])
                for ec in range(DC):
                    bank, bkey = next_bank()
                    p.mm_group(
                        [(lambda q, s2, e2, dc=dc2, ec=ec, bank=bank: q.matmul(
                            bank[:, 0:CH],
                            wqs[:, dc * D + ec * 128: dc * D + ec * 128 + 128],
                            xnT[:, dc * CH:(dc + 1) * CH], start=s2, stop=e2))
                         for dc2 in range(DC)],
                        reads=["at_wq", "at_xnT"], writes=[bkey])
                    act(lambda q, ec=ec, bank=bank: q.activation(
                        qT[:, ec * CH:(ec + 1) * CH], bank[:, 0:CH], AF.Identity,
                        bias=bq_c[:, ec:ec + 1]),
                        reads=[bkey, "at_bqc"], writes=["at_qT"])
                ve(lambda q, par=par: q.tensor_tensor(
                    qs[par][:, :].rearrange("p (e i) -> p e i", i=CH),
                    qT[:, :].rearrange("p (e i) -> p e i", i=CH),
                    lamr_bc[:, :].rearrange("p (o i) -> p o i", o=1)
                    .broadcast_to([128, DC, CH]),
                    ALU.mult),
                   reads=["at_qT", "at_lamrbc"], writes=[kqs])
                for ec in range(DC):
                    bank, bkey = next_bank()
                    p.mm_group(
                        [(lambda q, s2, e2, dc=dc2, ec=ec, bank=bank: q.matmul(
                            bank[:, 0:CH],
                            wks[:, dc * D + ec * 128: dc * D + ec * 128 + 128],
                            xnT[:, dc * CH:(dc + 1) * CH], start=s2, stop=e2))
                         for dc2 in range(DC)],
                        reads=["at_wk", "at_xnT"], writes=[bkey])
                    act(lambda q, ec=ec, bank=bank: q.activation(
                        tmpA[:, :], bank[:, 0:CH], AF.Relu,
                        bias=bk_c[:, ec:ec + 1]),
                        reads=[bkey, "at_bkc"], writes=["at_tmpA"])
                    ve(lambda q, ec=ec, bank=bank: q.tensor_scalar(
                        tmpB[:, :], bank[:, 0:CH], bk_c[:, ec:ec + 1], 0.0,
                        ALU.add, ALU.min),
                       reads=[bkey, "at_bkc"], writes=["at_tmpB"])
                    act(lambda q: q.activation(tmpC[:, :], tmpB[:, :], AF.Exp),
                        reads=["at_tmpB"], writes=["at_tmpC"])
                    ve(lambda q, ec=ec: q.tensor_tensor(
                        kT[:, ec * CH:(ec + 1) * CH], tmpA[:, :], tmpC[:, :],
                        ALU.add),
                       reads=["at_tmpA", "at_tmpC"], writes=["at_kT"])
                for tc in range(TPCH):
                    for h in range(2):
                        bank, bkey = next_bank()
                        p.mm_group(
                            [(lambda q, s2, e2, dc=dc2, tc=tc, h=h, bank=bank:
                              q.matmul(
                                  bank[:, 0:512],
                                  xnT[:, dc * CH + tc * 128: dc * CH + tc * 128 + 128],
                                  wvs[:, dc * D + h * 512: dc * D + h * 512 + 512],
                                  start=s2, stop=e2))
                             for dc2 in range(DC)],
                            reads=["at_wv", "at_xnT"], writes=[bkey])
                        ve(lambda q, tc=tc, h=h, par=par, bank=bank: q.tensor_tensor(
                            Vs[par][:, tc * D + h * 512: tc * D + h * 512 + 512],
                            bank[:, 0:512], bv_bc[:, h * 512:(h + 1) * 512],
                            ALU.add),
                           reads=[bkey, "at_bvbc"], writes=[kV])
                for tc in range(TPCH):
                    srcs = [kT[:, ec * CH + tc * 128: ec * CH + tc * 128 + 128]
                            for ec in range(DC)]
                    bank, bkey = tr_batch(srcs, ["at_kT"])
                    act(lambda q, tc=tc, par=par, bank=bank: q.activation(
                        Kp[par][:, tc * D:(tc + 1) * D], bank[:, :],
                        AF.Identity, scale=lamK[:, tc:tc + 1]),
                        reads=[bkey, "at_lamK"], writes=[kKp])
                for jc in range(TPCH):
                    bank, bkey = next_bank()
                    p.mm_group(
                        [(lambda q, s2, e2, ec=ec2, jc=jc, bank=bank: q.matmul(
                            bank[:, 0:CH],
                            kT[:, ec * CH + jc * 128: ec * CH + jc * 128 + 128],
                            qT[:, ec * CH:(ec + 1) * CH], start=s2, stop=e2))
                         for ec2 in range(DC)],
                        reads=["at_kT", "at_qT"], writes=[bkey])
                    ve(lambda q, jc=jc, par=par, bank=bank: q.tensor_tensor(
                        scm[par][:, jc * CH:(jc + 1) * CH], bank[:, 0:CH],
                        mask[:, jc * CH:(jc + 1) * CH], ALU.mult),
                       reads=[bkey, "at_mask"], writes=[kscm])

            def stage2(ci):
                s, c = divmod(ci, NCH)
                par = ci % 2
                par3 = ci % 3
                base = s * SEQ + c * CH
                kqs, kKp, kV, kscm = (("at_qs", par), ("at_Kp", par),
                                      ("at_V", par), ("at_scm", par))
                kxa = ("at_xa", par3)
                first = (c == 0)
                last = (c == NCH - 1)
                for vc in range(DC):
                    bank, bkey = next_bank()
                    mms = [(lambda q, s2, e2, jc=jc2, vc=vc, par=par, bank=bank:
                            q.matmul(
                                bank[:, 0:CH],
                                Vs[par][:, jc * D + vc * 128: jc * D + vc * 128 + 128],
                                scm[par][:, jc * CH:(jc + 1) * CH],
                                start=s2, stop=e2))
                           for jc2 in range(TPCH)]
                    if not first:
                        mms += [(lambda q, s2, e2, ec=ec2, vc=vc, par=par,
                                 bank=bank: q.matmul(
                                     bank[:, 0:CH],
                                     memb[:, ec * D + vc * 128: ec * D + vc * 128 + 128],
                                     qs[par][:, ec * CH:(ec + 1) * CH],
                                     start=s2, stop=e2))
                                for ec2 in range(DC)]
                    p.mm_group(mms, reads=[kV, kscm, "at_memb", kqs],
                               writes=[bkey])
                    ve(lambda q, vc=vc, bank=bank: q.tensor_copy(
                        aoT[:, vc * CH:(vc + 1) * CH], bank[:, 0:CH]),
                       reads=[bkey], writes=["at_aoT"])
                if not last:
                    for ec in range(DC):
                        for h in range(2):
                            bank, bkey = next_bank()
                            p.mm_group(
                                [(lambda q, s2, e2, tc=tc2, ec=ec, h=h, par=par,
                                  bank=bank: q.matmul(
                                      bank[:, 0:512],
                                      Kp[par][:, tc * D + ec * 128: tc * D + ec * 128 + 128],
                                      Vs[par][:, tc * D + h * 512: tc * D + h * 512 + 512],
                                      start=s2, stop=e2))
                                 for tc2 in range(TPCH)],
                                reads=[kKp, kV], writes=[bkey])
                            off = ec * D + h * 512
                            if first:
                                ve(lambda q, off=off, bank=bank: q.tensor_copy(
                                    mem[:, off:off + 512], bank[:, 0:512]),
                                   reads=[bkey], writes=["at_mem"])
                            else:
                                ve(lambda q, off=off, bank=bank:
                                   q.scalar_tensor_tensor(
                                       mem[:, off:off + 512], mem[:, off:off + 512],
                                       lamC[:, 0:1], bank[:, 0:512],
                                       ALU.mult, ALU.add),
                                   reads=[bkey, "at_lamC", "at_mem"],
                                   writes=["at_mem"])
                        act(lambda q, ec=ec: q.copy(
                            memb[:, ec * D:(ec + 1) * D],
                            mem[:, ec * D:(ec + 1) * D]),
                            reads=["at_mem"], writes=["at_memb"])

                for ic in range(TPCH):
                    t = base // 128 + ic
                    banks = []
                    for h in range(2):
                        bank, bkey = next_bank()
                        banks.append((bank, bkey))
                        p.mm_group(
                            [(lambda q, s2, e2, vc=vc2, ic=ic, h=h, bank=bank:
                              q.matmul(
                                  bank[:, 0:512],
                                  aoT[:, vc * CH + ic * 128: vc * CH + ic * 128 + 128],
                                  wos[:, vc * D + h * 512: vc * D + h * 512 + 512],
                                  start=s2, stop=e2))
                             for vc2 in range(DC)],
                            reads=["at_aoT", "at_wo"], writes=[bkey])
                    for h, (bank, bkey) in enumerate(banks):
                        ve(lambda q, h=h, bank=bank: q.tensor_tensor(
                            xot[:, h * 512:(h + 1) * 512], bank[:, 0:512],
                            bo_bc[:, h * 512:(h + 1) * 512], ALU.add),
                           reads=[bkey, "at_bobc"], writes=["at_xot"])
                    ve(lambda q, ic=ic, par3=par3: q.tensor_tensor(
                        xot[:, :], xot[:, :], xat[par3][:, ic * D:(ic + 1) * D],
                        ALU.add),
                       reads=["at_xot", kxa], writes=["at_xot"])
                    dma("g", x2_h.ap()[t * 128:(t + 1) * 128, :], xot[:, :],
                        reads=["at_xot"], writes=[kx2])
                    if "x2" in dbg:
                        dma("g", dbg["x2"].ap()[t * 128:(t + 1) * 128, :],
                            xot[:, :], reads=["at_xot"], writes=["dbg_x2w"])
                    ve(lambda q: q.bn_stats(bnst[:, 0:6], xot[:, 0:512]),
                       reads=["at_xot"], writes=["at_bn0"])
                    ve(lambda q: q.bn_stats(bnst[:, 6:12], xot[:, 512:1024]),
                       reads=["at_xot"], writes=["at_bn1"])
                    ve(lambda q, t=t: q.bn_aggr(mv2[:, 2 * t:2 * t + 2],
                                                bnst[:, :]),
                       reads=["at_bn0", "at_bn1"], writes=[("f2mv", t), "f2mv"])
                    ve(lambda q, t=t: q.tensor_scalar(
                        ntmp[:, :], mv2[:, 2 * t:2 * t + 1], -1.0, None,
                        ALU.mult),
                       reads=[("f2mv", t)], writes=["ntmp"])
                    act(lambda q, t=t: q.activation(
                        xot[:, :], xot[:, :], AF.Identity, bias=ntmp[:, :],
                        scale=1.0),
                        reads=["at_xot", "ntmp"], writes=["at_xot"])
                    ve(lambda q, t=t: q.tensor_reduce(
                        xnm2[:, t:t + 1], xot[:, :], axis=AX.X, op=ALU.max,
                        apply_absolute_value=True),
                       reads=["at_xot"], writes=["f2xnm"])


            total = SPC * NCH
            stage0(0)
            stage0(1)
            stage1(0)
            for ci in range(total):
                if ci + 2 < total:
                    stage0(ci + 2)
                if ci + 1 < total:
                    stage1(ci + 1)
                stage2(ci)
            p.full_barrier()


    if stage == "ffn1":
        zo = sb("zot", [128, D], F32)
        ve(lambda q: q.memset(zo[:, :], 0.0), writes=["zot"])
        for t in range(TT):
            dma("sp", out_h.ap()[t * 128:(t + 1) * 128, :], zo[:, :],
                reads=["zot"], writes=["outw"])
        return finish()

    attention()

    if stage == "attn":
        zo = sb("zot", [128, D], F32)
        ve(lambda q: q.memset(zo[:, :], 0.0), writes=["zot"])
        for t in range(TT):
            dma("sp", out_h.ap()[t * 128:(t + 1) * 128, :], zo[:, :],
                reads=["zot"], writes=["outw"])
        return finish()

    ffn_call("f2", x2_h, out_h, mv2, rstd2, b2, xnm2, need_pass_a=False,
             fuse_stats_mv=None, dbg_x=None)

    if "scal" in dbg:
        dma("g", dbg["scal"].ap()[0:1, 0:8], scrow[0:1, 0:8],
            reads=["scrow"], writes=["dbgscal"])

    return finish()


_NC_CACHE = {}


def _get_nc(debug_out=(), stage="full"):
    key = (tuple(sorted(debug_out)), stage)
    if key not in _NC_CACHE:
        _NC_CACHE[key] = build_nc(debug_out=key[0], stage=stage)
    return _NC_CACHE[key]


def _in_maps(inputs):
    f32 = lambda a: np.ascontiguousarray(np.asarray(a, dtype=np.float32))
    x = f32(inputs["L0_attn_x"]).reshape(NCORE * SPC, SEQ, D)
    maps = []
    for c in range(NCORE):
        m = {
            "x": x[c * SPC:(c + 1) * SPC].reshape(N, D),
            "Wq": f32(inputs["L0_attn_Wq"]),
            "Wk": f32(inputs["L0_attn_Wk"]),
            "Wv": f32(inputs["L0_attn_Wv"]),
            "Wo": f32(inputs["L0_attn_Wo"]),
            "bq": f32(inputs["L0_attn_bq"]).reshape(1, D),
            "bk": f32(inputs["L0_attn_bk"]).reshape(1, D),
            "bv": f32(inputs["L0_attn_bv"]).reshape(1, D),
            "bo": f32(inputs["L0_attn_bo"]).reshape(1, D),
            "decay": f32(inputs["L0_attn_decay_param"]).reshape(1, 1),
            "w_gate": f32(inputs["L0_ffn_w_gate"]),
            "w_up": f32(inputs["L0_ffn_w_up"]),
            "w_down": f32(inputs["L0_ffn_w_down"]),
        }
        maps.append(m)
    return maps


def kernel(**inputs):
    nc = _get_nc()
    res = run_bass_kernel_spmd(nc, _in_maps(inputs), core_ids=list(range(NCORE)))
    out = np.concatenate([r["out"].reshape(SPC, SEQ, D) for r in res.results],
                         axis=0)
    return out.astype(np.float32)


def kernel_debug(debug_out, stage="full", **inputs):
    nc = _get_nc(debug_out, stage)
    res = run_bass_kernel_spmd(nc, _in_maps(inputs), core_ids=list(range(NCORE)))
    return res


# version 11
# speedup vs baseline: 1.0484x; 1.0027x over previous
import contextlib
import numpy as np

import concourse.bass as bass
import concourse.mybir as mybir
from concourse.bass_utils import run_bass_kernel_spmd

F32 = mybir.dt.float32
BF16 = mybir.dt.bfloat16
AF = mybir.ActivationFunctionType
ALU = mybir.AluOpType
AX = mybir.AxisListType

MAGIC = float(np.float32(1.5 * 2 ** 23))
QMAX = 32767.0
WQMAX = 7.0
EPS_S = 1e-12
LN_EPS = 1e-5
SIGCLIP = float(np.float32(65535.0))
INV65536 = 1.0 / 65536.0

NCORE = 8
SEQ = 2048
D = 1024
F = 3072
SPC = 4
N = SPC * SEQ
TT = N // 128
NB = N // 512
DC = D // 128
FC = F // 128
CH = 256
NCH = SEQ // CH
TPCH = CH // 128

ENGINES = ("pe", "ve", "act", "sp", "g")


class Prog:

    DMA_POOL = {"sp": 12, "g": 4}

    def __init__(self, nc):
        self.nc = nc
        self.streams = {e: [] for e in ENGINES}
        self.cnt = {(e, "c"): 0 for e in ENGINES}
        for e, npool in self.DMA_POOL.items():
            for i in range(npool):
                self.cnt[(e + "_d", i)] = 0
        self.dma_idx = {e: 0 for e in self.DMA_POOL}
        self.sems = {}
        self.last_w = {}
        self.readers = {}
        self.watermark = {e: {} for e in ENGINES}
        self.n_cc = 0

    def _need(self, eng, keys_r, keys_w):
        need = {}
        for kk in keys_r:
            lw = self.last_w.get(kk)
            if lw is not None:
                need[lw[0]] = max(need.get(lw[0], 0), lw[1])
        for kk in keys_w:
            lw = self.last_w.get(kk)
            if lw is not None:
                need[lw[0]] = max(need.get(lw[0], 0), lw[1])
            for (ref, c) in self.readers.get(kk, ()):
                need[ref] = max(need.get(ref, 0), c)
        wm = self.watermark[eng]
        out = []
        for ref, c in need.items():
            if wm.get(ref, 0) >= c:
                continue
            out.append((ref, c))
            wm[ref] = c
        return out

    def _record(self, ref, mycnt, reads, writes):
        for kk in writes:
            self.last_w[kk] = (ref, mycnt)
            self.readers[kk] = []
        for kk in reads:
            self.readers.setdefault(kk, []).append((ref, mycnt))

    @staticmethod
    def _is_psum(k):
        return (isinstance(k, tuple) and k[0] in ("pmm", "ptrb")) or k == "pms"

    def _excl(self, reads, writes):
        pr = [k for k in reads if self._is_psum(k)]
        if not pr:
            return reads, writes
        return [k for k in reads if not self._is_psum(k)], list(writes) + pr

    def emit(self, eng, fn, reads=(), writes=(), kind="c"):
        reads, writes = self._excl(reads, writes)
        waits = self._need(eng, reads, writes)
        if kind == "d":
            npool = self.DMA_POOL[eng]
            si = self.dma_idx[eng] % npool
            self.dma_idx[eng] += 1
            ref = (eng + "_d", si)
            step = 16
            prev = self.cnt[ref]
            if prev > 0 and self.watermark[eng].get(ref, 0) < prev:
                waits = waits + [(ref, prev)]
                self.watermark[eng][ref] = prev
        else:
            ref = (eng, kind)
            step = 1
        self.cnt[ref] += step
        mycnt = self.cnt[ref]

        def thunk(q, waits=waits, fn=fn, ref=ref, step=step):
            for (r, c) in waits:
                q.wait_ge(self.sems[r], c)
            fn(q).then_inc(self.sems[ref], step)

        self.streams[eng].append(thunk)
        self._record(ref, mycnt, reads, writes)

    def emit_cc(self, fn, reads=(), writes=()):
        waits = self._need("g", reads, writes)
        ref = ("cc", self.n_cc)
        self.n_cc += 1
        self.cnt[ref] = 1

        def thunk(q, waits=waits, fn=fn, ref=ref):
            for (r, c) in waits:
                q.wait_ge(self.sems[r], c)
            fn(q).then_inc(self.sems[ref], 1)

        self.streams["g"].append(thunk)
        self._record(ref, 1, reads, writes)

    def mm_group(self, mms, reads=(), writes=()):
        reads, writes = self._excl(reads, writes)
        waits = self._need("pe", reads, writes)
        ref = ("pe", "c")
        self.cnt[ref] += 1
        mycnt = self.cnt[ref]
        n = len(mms)

        def thunk(q, waits=waits, mms=mms, n=n):
            for (r, c) in waits:
                q.wait_ge(self.sems[r], c)
            for i, f in enumerate(mms):
                inst = f(q, i == 0, i == n - 1)
                if i == n - 1:
                    inst.then_inc(self.sems[("pe", "c")], 1)

        self.streams["pe"].append(thunk)
        self._record(ref, mycnt, reads, writes)

    def full_barrier(self):
        snap = [(ref, c) for ref, c in self.cnt.items() if c > 0]
        for e in ENGINES:
            wm = self.watermark[e]
            waits = [(ref, c) for ref, c in snap if wm.get(ref, 0) < c]
            for ref, c in waits:
                wm[ref] = c

            def thunk(q, waits=waits):
                for (r, c) in waits:
                    q.wait_ge(self.sems[r], c)

            self.streams[e].append(thunk)

    def build(self, block, stack):
        for ref in self.cnt:
            self.sems[ref] = stack.enter_context(
                self.nc.semaphore(f"s_{ref[0]}_{ref[1]}"))

        def runner(e):
            def run(q):
                for t in self.streams[e]:
                    t(q)
            return run

        block.tensor(runner("pe"))
        block.vector(runner("ve"))
        block.scalar(runner("act"))
        block.sync(runner("sp"))
        block.gpsimd(runner("g"))



def build_nc(debug_out=(), stage="full"):
    nc = bass.Bass("TRN2", target_bir_lowering=False, debug=False)
    stack = contextlib.ExitStack()
    import ml_dtypes

    x0_h = nc.dram_tensor("x", [N, D], F32, kind="ExternalInput")
    wq_h = nc.dram_tensor("Wq", [D, D], F32, kind="ExternalInput")
    wk_h = nc.dram_tensor("Wk", [D, D], F32, kind="ExternalInput")
    wv_h = nc.dram_tensor("Wv", [D, D], F32, kind="ExternalInput")
    wo_h = nc.dram_tensor("Wo", [D, D], F32, kind="ExternalInput")
    bq_h = nc.dram_tensor("bq", [1, D], F32, kind="ExternalInput")
    bk_h = nc.dram_tensor("bk", [1, D], F32, kind="ExternalInput")
    bv_h = nc.dram_tensor("bv", [1, D], F32, kind="ExternalInput")
    bo_h = nc.dram_tensor("bo", [1, D], F32, kind="ExternalInput")
    dec_h = nc.dram_tensor("decay", [1, 1], F32, kind="ExternalInput")
    wgate_h = nc.dram_tensor("w_gate", [F, D], F32, kind="ExternalInput")
    wup_h = nc.dram_tensor("w_up", [F, D], F32, kind="ExternalInput")
    wdown_h = nc.dram_tensor("w_down", [D, F], F32, kind="ExternalInput")
    out_h = nc.dram_tensor("out", [N, D], F32, kind="ExternalOutput")

    x1_h = nc.dram_tensor("x1s", [N, D], F32)
    x2_h = nc.dram_tensor("x2s", [N, D], F32)
    gate_s = nc.dram_tensor("gate_s", [FC, 128, N], BF16)
    up_s = nc.dram_tensor("up_s", [FC, 128, N], BF16)
    m1_s = nc.dram_tensor("m1_s", [FC, 128, N], BF16)
    o_s = nc.dram_tensor("o_s", [N // 256, 128, FC * 256], BF16)
    wupT_h = nc.dram_tensor("wupT_s", [128, DC * F], BF16)
    wgateT_h = nc.dram_tensor("wgateT_s", [128, DC * F], BF16)
    wdownT_h = nc.dram_tensor("wdownT_s", [128, FC * D], BF16)
    cc_in_h = nc.dram_tensor("cc_in", [1, 4], F32)
    cc_out_h = nc.dram_tensor("cc_out", [1, 4], F32)

    dbg = {}
    for name in debug_out:
        if name in ("x1", "x2"):
            dbg[name] = nc.dram_tensor("dbg_" + name, [N, D], F32,
                                       kind="ExternalOutput")
        elif name in ("gate", "up", "m1"):
            dbg[name] = nc.dram_tensor("dbg_" + name, [FC, 128, N], BF16,
                                       kind="ExternalOutput")
        elif name == "scal":
            dbg[name] = nc.dram_tensor("dbg_scal", [1, 16], F32,
                                       kind="ExternalOutput")

    ident_f_h = nc.inline_tensor(np.eye(128, dtype=np.float32), name="c_idf")
    ident_b_h = nc.inline_tensor(np.eye(128, dtype=ml_dtypes.bfloat16), name="c_idb")
    ones_f_h = nc.inline_tensor(np.ones((1, 128), dtype=np.float32), name="c_ones")

    p = Prog(nc)
    sb = lambda name, shape, dt: stack.enter_context(nc.sbuf_tensor(name, shape, dt))
    psal = lambda name, shape, dt: stack.enter_context(nc.psum_tensor(name, shape, dt))

    idf = sb("idf", [128, 128], F32)
    idb = sb("idb", [128, 128], BF16)
    onesf = sb("onesf", [1, 128], F32)
    mv1 = sb("mv1", [128, 2 * TT], F32)
    mva = sb("mva", [128, 2 * TT], F32)
    mv2 = sb("mv2", [128, 2 * TT], F32)
    rstd1, b1 = sb("rstd1", [128, TT], F32), sb("b1", [128, TT], F32)
    rstda, ba = sb("rstda", [128, TT], F32), sb("ba", [128, TT], F32)
    rstd2, b2 = sb("rstd2", [128, TT], F32), sb("b2", [128, TT], F32)
    xnm1 = sb("xnm1", [128, TT], F32)
    xnm2 = sb("xnm2", [128, TT], F32)
    scbc = sb("scbc", [128, 16], F32)
    scrow = sb("scrow", [1, 8], F32)
    ntmp = sb("ntmp", [128, 1], F32)

    SC_A = 0
    SC_B = 2
    SC_NM = 6
    SC_C = 7
    SC_D = 10

    pmm = [psal(f"pmm{i}", [128, 512], F32) for i in range(4)]
    ptrb = [psal(f"ptrb{i}", [128, 1024], BF16) for i in range(3)]
    pms = psal("pms", [128, 512], F32)
    mmctr = [0]

    def next_bank():
        mmctr[0] += 1
        return pmm[mmctr[0] % 4], ("pmm", mmctr[0] % 4)

    ptrctr = [0]

    def dma(queue, out, in_, reads=(), writes=()):
        p.emit(queue, lambda q, o=out, i=in_: q.dma_start(out=o, in_=i),
               reads=reads, writes=writes, kind="d")

    def ve(fn, reads=(), writes=()):
        p.emit("ve", fn, reads=reads, writes=writes)

    def act(fn, reads=(), writes=()):
        p.emit("act", fn, reads=reads, writes=writes)

    K_IDF, K_IDB, K_ONES = "idf", "idb", "onesf"

    def bcast_row(row_ap, out_ap, row_keys, out_keys, ncols):
        p.mm_group([lambda q, s, e: q.matmul(pms[:, 0:ncols], onesf[0:1, :],
                                             row_ap, start=s, stop=e)],
                   reads=[K_ONES] + list(row_keys), writes=["pms"])
        ve(lambda q: q.tensor_copy(out_ap, pms[:, 0:ncols]),
           reads=["pms"], writes=list(out_keys))

    def tr_batch(src_aps, read_keys):
        i = ptrctr[0] % 3
        ptrctr[0] += 1
        bank, bkey = ptrb[i], ("ptrb", i)
        mms = [(lambda q, s2, e2, j=j, src=src: q.transpose(
            bank[:, j * 128:(j + 1) * 128], src, idb[:, :]))
               for j, src in enumerate(src_aps)]
        p.mm_group(mms, reads=list(read_keys) + [K_IDB], writes=[bkey])
        return bank, bkey

    def colmax_to_row(col_ap, col_key, out_col):
        p.mm_group([lambda q, s, e: q.transpose(pms[0:1, 0:128], col_ap,
                                                idf[:, :])],
                   reads=[col_key, K_IDF], writes=["pms"])
        ve(lambda q: q.tensor_reduce(scrow[0:1, out_col:out_col + 1],
                                     pms[0:1, 0:128], axis=AX.X, op=ALU.max),
           reads=["pms"], writes=[("scrow", out_col)])

    def collective_max(ncols):
        dma("g", cc_in_h.ap()[0:1, 0:ncols], scrow[0:1, 0:ncols],
            reads=[("scrow", c) for c in range(ncols)], writes=["cc_in"])
        p.emit_cc(lambda q: q.collective_compute(
            "AllReduce", ALU.max, replica_groups=[list(range(NCORE))],
            ins=[cc_in_h.ap()[0:1, 0:ncols].opt()],
            outs=[cc_out_h.ap()[0:1, 0:ncols].opt()]),
            reads=["cc_in"], writes=["cc_out"])
        dma("g", scrow[0:1, 0:ncols], cc_out_h.ap()[0:1, 0:ncols],
            reads=["cc_out"],
            writes=["scrow"] + [("scrow", c) for c in range(ncols)])

    def scale_pair(src_col, dst_cols, pre=1.0):
        c0, c1 = dst_cols
        ve(lambda q: q.tensor_scalar(scrow[0:1, c0:c0 + 1],
                                     scrow[0:1, src_col:src_col + 1],
                                     pre / QMAX, EPS_S, ALU.mult, ALU.max),
           reads=[("scrow", src_col)], writes=[("scrow", c0)])
        ve(lambda q: q.reciprocal(scrow[0:1, c1:c1 + 1], scrow[0:1, c0:c0 + 1]),
           reads=[("scrow", c0)], writes=[("scrow", c1)])

    dma("sp", idf[:, :], ident_f_h.ap(), writes=[K_IDF])
    dma("sp", idb[:, :], ident_b_h.ap(), writes=[K_IDB])
    dma("sp", onesf[:, :], ones_f_h.ap(), writes=[K_ONES])
    zcol = sb("zcol", [128, 1], F32)
    onecol = sb("onecol", [128, 1], F32)
    mgcol = sb("mgcol", [128, 1], F32)
    ve(lambda q: q.memset(zcol[:, :], 0.0), writes=["zcol"])
    ve(lambda q: q.memset(onecol[:, :], 1.0), writes=["onecol"])
    ve(lambda q: q.memset(mgcol[:, :], MAGIC), writes=["mgcol"])
    nc.const_aps.register(F32, 0.0, zcol[:, 0:1])
    nc.const_aps.register(F32, 1.0, onecol[:, 0:1])
    nc.const_aps.register(F32, MAGIC, mgcol[:, 0:1])

    import os
    PREP_OC = int(os.environ.get("KPREP_OC", "999"))
    PREP_SKIP = os.environ.get("KPREP_SKIP", "")

    def prep_ffn_weight(w_h, wT_h, O, I, name):
        OC = min(O // 128, PREP_OC)
        IC = I // 128
        nblk = I // 32
        with contextlib.ExitStack() as st2:
            wst = [st2.enter_context(nc.sbuf_tensor(f"pw_{name}_w{i}", [128, I], F32))
                   for i in range(2)]
            sblk = [st2.enter_context(nc.sbuf_tensor(f"pw_{name}_s{i}", [128, nblk],
                                                     F32)) for i in range(2)]
            invb = st2.enter_context(nc.sbuf_tensor(f"pw_{name}_i", [128, nblk], F32))
            r1 = st2.enter_context(nc.sbuf_tensor(f"pw_{name}_r", [128, I], F32))
            deq = [st2.enter_context(nc.sbuf_tensor(f"pw_{name}_d{i}", [128, I],
                                                    BF16)) for i in range(2)]
            tout = st2.enter_context(nc.sbuf_tensor(f"pw_{name}_t", [128, 1024], BF16))

            def w_front(oc):
                par = oc % 2
                dma("sp", wst[par][:, :], w_h.ap()[oc * 128:(oc + 1) * 128, :],
                    writes=[(name, "w", par)])
                ve(lambda q, par=par: q.tensor_reduce(
                    sblk[par][:, :],
                    wst[par][:, :].rearrange("p (b c) -> p b c", c=32),
                    axis=AX.X, op=ALU.max, apply_absolute_value=True),
                   reads=[(name, "w", par)], writes=[(name, "s", par)])

            def w_rest(oc):
                par = oc % 2
                ve(lambda q, par=par: q.tensor_scalar(
                    sblk[par][:, :], sblk[par][:, :],
                    1.0 / WQMAX, EPS_S, ALU.mult, ALU.max),
                   reads=[(name, "s", par)], writes=[(name, "s", par)])
                ve(lambda q, par=par: q.reciprocal(invb[:, :], sblk[par][:, :]),
                   reads=[(name, "s", par)], writes=[(name, "inv")])
                ve(lambda q, par=par: q.tensor_tensor(
                    r1[:, :].rearrange("p (b c) -> p b c", c=32),
                    wst[par][:, :].rearrange("p (b c) -> p b c", c=32),
                    invb[:, :].rearrange("p (b o) -> p b o", o=1)
                        .broadcast_to([128, nblk, 32]),
                    ALU.mult),
                   reads=[(name, "w", par), (name, "inv")], writes=[(name, "r1")])
                ve(lambda q: q.tensor_scalar(r1[:, :], r1[:, :], MAGIC, MAGIC,
                                             ALU.add, ALU.subtract),
                   reads=[(name, "r1")], writes=[(name, "r1")])
                ve(lambda q, par=par: q.tensor_tensor(
                    deq[par][:, :].rearrange("p (b c) -> p b c", c=32),
                    r1[:, :].rearrange("p (b c) -> p b c", c=32),
                    sblk[par][:, :].rearrange("p (b o) -> p b o", o=1)
                        .broadcast_to([128, nblk, 32]),
                    ALU.mult),
                   reads=[(name, "r1"), (name, "s", par)],
                   writes=[(name, "deq", par)])
                _w_tail(oc, par)

            def _w_tail(oc, par):
                for ic0 in range(0, IC, 8):
                    if "T" in PREP_SKIP:
                        continue
                    nb = min(8, IC - ic0)
                    srcs = [deq[par][:, (ic0 + j) * 128:(ic0 + j + 1) * 128]
                            for j in range(nb)]
                    bank, bkey = tr_batch(srcs, [(name, "deq", par)])
                    tk = (name, "tout")
                    ve(lambda q, nb=nb, bank=bank: q.tensor_copy(
                        tout[:, 0:nb * 128], bank[:, 0:nb * 128]),
                       reads=[bkey], writes=[tk])
                    if "S" in PREP_SKIP:
                        continue
                    dma("g",
                        wT_h.ap()[:, :].rearrange("p (i t) -> p i t", t=O)
                        [:, ic0:ic0 + nb, oc * 128:(oc + 1) * 128],
                        tout[:, 0:nb * 128].rearrange("p (i t) -> p i t", t=128),
                        reads=[tk], writes=[(name, "spill")])

            w_front(0)
            for oc in range(OC):
                if oc + 1 < OC:
                    w_front(oc + 1)
                w_rest(oc)
            p.full_barrier()

    def zero_out_finish():
        zo = sb("zot", [128, D], F32)
        ve(lambda q: q.memset(zo[:, :], 0.0), writes=["zot"])
        for t in range(TT):
            dma("sp", out_h.ap()[t * 128:(t + 1) * 128, :], zo[:, :],
                reads=["zot"], writes=["outw"])

    if stage == "consts":
        zero_out_finish()
        p.full_barrier()
        with nc.Block() as block:
            p.build(block, stack)
        stack.close()
        return nc

    prep_ffn_weight(wup_h, wupT_h, F, D, "wup")
    if stage == "prep1":
        zero_out_finish()
        p.full_barrier()
        with nc.Block() as block:
            p.build(block, stack)
        stack.close()
        return nc
    prep_ffn_weight(wgate_h, wgateT_h, F, D, "wgate")
    prep_ffn_weight(wdown_h, wdownT_h, D, F, "wdown")

    def ffn_call(tag, xin_h, xout_h, mv_t, rstd_t, b_t, xnm_t,
                 need_pass_a, fuse_stats_mv=None, dbg_x=None, upto="E"):
        kmv = tag + "mv"
        krstd, kb, kxnm = tag + "rstd", tag + "b", tag + "xnm"
        kxin, kxout = xin_h.name, xout_h.name

        if need_pass_a:
            with contextlib.ExitStack() as st2:
                xa = [st2.enter_context(nc.sbuf_tensor(f"{tag}Axa{i}", [128, D], F32))
                      for i in range(2)]
                bnst = st2.enter_context(nc.sbuf_tensor(f"{tag}Abn", [128, 12], F32))
                tsub = [st2.enter_context(nc.sbuf_tensor(f"{tag}Ats{i}", [128, D], F32))
                        for i in range(2)]
                for t in range(TT):
                    xk = (tag, "Axa", t % 2)
                    dma("sp", xa[t % 2][:, :], xin_h.ap()[t * 128:(t + 1) * 128, :],
                        reads=[kxin], writes=[xk])
                    ve(lambda q, t=t: q.bn_stats(bnst[:, 0:6], xa[t % 2][:, 0:512]),
                       reads=[xk], writes=[(tag, "Abn0")])
                    ve(lambda q, t=t: q.bn_stats(bnst[:, 6:12], xa[t % 2][:, 512:1024]),
                       reads=[xk], writes=[(tag, "Abn1")])
                    ve(lambda q, t=t: q.bn_aggr(mv_t[:, 2 * t:2 * t + 2], bnst[:, :]),
                       reads=[(tag, "Abn0"), (tag, "Abn1")],
                       writes=[(kmv, t), kmv])
                    ve(lambda q, t=t: q.tensor_scalar(
                        ntmp[:, :], mv_t[:, 2 * t:2 * t + 1], -1.0, None, ALU.mult),
                       reads=[(kmv, t)], writes=["ntmp"])
                    act(lambda q, t=t: q.activation(
                        tsub[t % 2][:, :], xa[t % 2][:, :], AF.Identity,
                        bias=ntmp[:, :], scale=1.0),
                        reads=[xk, "ntmp"], writes=[(tag, "Atsub", t % 2)])
                    ve(lambda q, t=t: q.tensor_reduce(
                        xnm_t[:, t:t + 1], tsub[t % 2][:, :], axis=AX.X,
                        op=ALU.max, apply_absolute_value=True),
                       reads=[(tag, "Atsub", t % 2)], writes=[kxnm])
                p.full_barrier()

        var_view = mv_t[:, :].rearrange("p (t two) -> p t two", two=2)[:, :, 1]
        mean_view = mv_t[:, :].rearrange("p (t two) -> p t two", two=2)[:, :, 0]
        ve(lambda q: q.tensor_scalar(rstd_t[:, :], var_view, LN_EPS, None, ALU.add),
           reads=[kmv], writes=[krstd])
        act(lambda q: q.activation(rstd_t[:, :], rstd_t[:, :], AF.Sqrt),
            reads=[krstd], writes=[krstd])
        ve(lambda q: q.reciprocal(rstd_t[:, :], rstd_t[:, :]),
           reads=[krstd], writes=[krstd])
        ve(lambda q: q.scalar_tensor_tensor(b_t[:, :], mean_view, -1.0,
                                            rstd_t[:, :], ALU.mult, ALU.mult),
           reads=[kmv, krstd], writes=[kb])
        ve(lambda q: q.tensor_tensor(xnm_t[:, :], xnm_t[:, :], rstd_t[:, :],
                                     ALU.mult),
           reads=[kxnm, krstd], writes=[kxnm])
        ve(lambda q: q.tensor_reduce(scbc[:, 15:16], xnm_t[:, :], axis=AX.X,
                                     op=ALU.max),
           reads=[kxnm], writes=[(tag, "xcol")])
        colmax_to_row(scbc[:, 15:16], (tag, "xcol"), 0)
        collective_max(1)
        scale_pair(0, (0, 1))
        bcast_row(scrow[0:1, 0:2], scbc[:, SC_A:SC_A + 2],
                  [("scrow", 0), ("scrow", 1)], ["scbcA"], 2)
        if upto == "A":
            return

        with contextlib.ExitStack() as st2:
            wupT = st2.enter_context(nc.sbuf_tensor(f"{tag}BwU", [128, DC * F], BF16))
            wgateT = st2.enter_context(nc.sbuf_tensor(f"{tag}BwG", [128, DC * F], BF16))
            xba = [st2.enter_context(nc.sbuf_tensor(f"{tag}Bxa{i}", [128, D], F32))
                   for i in range(2)]
            xn = [st2.enter_context(nc.sbuf_tensor(f"{tag}Bxn{i}", [128, D], F32))
                  for i in range(2)]
            xq = [st2.enter_context(nc.sbuf_tensor(f"{tag}Bxq{i}", [128, D], BF16))
                  for i in range(2)]
            xqT = [st2.enter_context(nc.sbuf_tensor(f"{tag}BxqT{i}", [128, DC * 512],
                                                    BF16)) for i in range(2)]
            evu = [st2.enter_context(nc.sbuf_tensor(f"{tag}Bevu{i}", [128, 4 * 512],
                                                    BF16)) for i in range(2)]
            evg = [st2.enter_context(nc.sbuf_tensor(f"{tag}Bevg{i}", [128, 4 * 512],
                                                    BF16)) for i in range(2)]
            upmaxs = st2.enter_context(nc.sbuf_tensor(f"{tag}Bum", [128, FC * NB], F32))
            gatemaxs = st2.enter_context(nc.sbuf_tensor(f"{tag}Bgm", [128, FC * NB],
                                                        F32))
            dma("sp", wupT[:, :], wupT_h.ap(), reads=[("wup", "spill")],
                writes=[(tag, "BwU")])
            dma("sp", wgateT[:, :], wgateT_h.ap(), reads=[("wgate", "spill")],
                writes=[(tag, "BwG")])
            def b_prep(blk):
                par = blk % 2
                kxqT = (tag, "BxqT", par)
                for ti in range(4):
                    t = blk * 4 + ti
                    tp = t % 2
                    xk = (tag, "Bxa", tp)
                    dma("sp", xba[tp][:, :], xin_h.ap()[t * 128:(t + 1) * 128, :],
                        reads=[kxin], writes=[xk])
                    act(lambda q, t=t, tp=tp: q.activation(
                        xn[tp][:, :], xba[tp][:, :], AF.Identity,
                        bias=b_t[:, t:t + 1], scale=rstd_t[:, t:t + 1]),
                        reads=[xk, krstd, kb], writes=[(tag, "Bxn", tp)])
                    ve(lambda q, tp=tp: q.tensor_scalar(
                        xn[tp][:, :], xn[tp][:, :], scbc[:, SC_A + 1:SC_A + 2],
                        MAGIC, ALU.mult, ALU.add),
                       reads=[(tag, "Bxn", tp), "scbcA"],
                       writes=[(tag, "Bxn", tp)])
                    ve(lambda q, tp=tp: q.tensor_scalar(
                        xq[tp][:, :], xn[tp][:, :], MAGIC,
                        scbc[:, SC_A:SC_A + 1], ALU.subtract, ALU.mult),
                       reads=[(tag, "Bxn", tp), "scbcA"],
                       writes=[(tag, "Bxq", tp)])
                    srcs = [xq[tp][:, dc * 128:(dc + 1) * 128] for dc in range(DC)]
                    bank, bkey = tr_batch(srcs, [(tag, "Bxq", tp)])
                    ve(lambda q, ti=ti, par=par, bank=bank: q.tensor_copy(
                        xqT[par][:, :].rearrange("p (d t) -> p d t", t=512)
                        [:, :, ti * 128:(ti + 1) * 128],
                        bank[:, :].rearrange("p (d t) -> p d t", t=128)),
                       reads=[bkey], writes=[kxqT])
            def b_mms(blk):
                par = blk % 2
                kxqT = (tag, "BxqT", par)
                for (wT, kwT, scr_h, ev2, kev, maxsl, kmaxsl) in (
                        (wupT, (tag, "BwU"), up_s, evu, (tag, "Bevu"), upmaxs,
                         "upmaxs"),
                        (wgateT, (tag, "BwG"), gate_s, evg, (tag, "Bevg"),
                         gatemaxs, "gatemaxs")):
                    for fc in range(FC):
                        bank, bkey = next_bank()
                        p.mm_group(
                            [(lambda q, s, e, dc=dc2, bank=bank, wT=wT, fc=fc,
                              par=par, xqT=xqT: q.matmul(
                                  bank[:, 0:512],
                                  wT[:, dc * F + fc * 128: dc * F + fc * 128 + 128],
                                  xqT[par][:, dc * 512:(dc + 1) * 512],
                                  start=s, stop=e))
                             for dc2 in range(DC)],
                            reads=[kwT, kxqT], writes=[bkey])
                        gi = (fc // 4) % 2
                        sl = fc % 4
                        act(lambda q, bank=bank, ev2=ev2, gi=gi, sl=sl: q.copy(
                            ev2[gi][:, sl * 512:(sl + 1) * 512], bank[:, :]),
                            reads=[bkey], writes=[(kev, gi, sl)])
                        ve(lambda q, bank=bank, maxsl=maxsl, col=blk * FC + fc:
                           q.tensor_reduce(maxsl[:, col:col + 1], bank[:, :],
                                           axis=AX.X, op=ALU.max,
                                           apply_absolute_value=True),
                           reads=[bkey], writes=[kmaxsl])
                        if sl == 3:
                            dma("g",
                                scr_h.ap()[fc - 3:fc + 1, :,
                                           blk * 512:(blk + 1) * 512]
                                .rearrange("f p t -> p f t"),
                                ev2[gi][:, :].rearrange("p (f t) -> p f t", t=512),
                                reads=[(kev, gi, 0), (kev, gi, 1), (kev, gi, 2),
                                       (kev, gi, 3)],
                                writes=[scr_h.name])

            b_prep(0)
            for blk in range(NB):
                if blk + 1 < NB:
                    b_prep(blk + 1)
                b_mms(blk)
            ve(lambda q, upmaxs=upmaxs: q.tensor_reduce(
                scbc[:, 15:16], upmaxs[:, :], axis=AX.X,
                op=ALU.max), reads=["upmaxs"],
               writes=[(tag, "ucol")])
            colmax_to_row(scbc[:, 15:16], (tag, "ucol"), 0)
            ve(lambda q, gatemaxs=gatemaxs: q.tensor_reduce(
                scbc[:, 15:16], gatemaxs[:, :], axis=AX.X,
                op=ALU.max), reads=["gatemaxs"],
               writes=[(tag, "gcol")])
            colmax_to_row(scbc[:, 15:16], (tag, "gcol"), 1)
            p.full_barrier()

        collective_max(2)
        scale_pair(0, (2, 3))
        scale_pair(1, (4, 5))
        ve(lambda q: q.tensor_scalar(scrow[0:1, 6:7], scrow[0:1, 4:5],
                                     -MAGIC, None, ALU.mult),
           reads=[("scrow", 4)], writes=[("scrow", 6)])
        bcast_row(scrow[0:1, 2:7], scbc[:, SC_B:SC_B + 5],
                  [("scrow", c) for c in (2, 3, 4, 5, 6)], ["scbcB"], 5)
        if upto == "B":
            return

        HP = N // 2
        with contextlib.ExitStack() as st2:
            gtp = [st2.enter_context(nc.sbuf_tensor(f"{tag}Cg{i}", [128, HP], BF16))
                   for i in range(2)]
            ct1 = [st2.enter_context(nc.sbuf_tensor(f"{tag}Ct1{i}", [128, HP], F32))
                   for i in range(2)]
            ct2 = [st2.enter_context(nc.sbuf_tensor(f"{tag}Ct2{i}", [128, HP], F32))
                   for i in range(2)]
            ct3 = [st2.enter_context(nc.sbuf_tensor(f"{tag}Ct3{i}", [128, HP], F32))
                   for i in range(2)]
            m1p = [st2.enter_context(nc.sbuf_tensor(f"{tag}Cm{i}", [128, HP], BF16))
                   for i in range(2)]
            m1maxs = st2.enter_context(nc.sbuf_tensor(f"{tag}Cmm", [128, 2 * FC],
                                                      F32))
            def c_front(idx):
                fc, h = idx // 2, idx % 2
                par = idx % 2
                kg = (tag, "Cg", par)
                dma("sp", gtp[par][:, :], gate_s.ap()[fc, :, h * HP:(h + 1) * HP],
                    reads=[gate_s.name], writes=[kg])
                act(lambda q, par=par: q.activation(
                    ct1[par][:, :], gtp[par][:, :], AF.Identity,
                    scale=scbc[:, SC_B + 3:SC_B + 4], bias=mgcol[:, :]),
                    reads=[kg, "scbcB", "mgcol"], writes=[(tag, "Ct1", par)])

            def c_rest(idx):
                fc, h = idx // 2, idx % 2
                par = idx % 2
                km = (tag, "Cm", par)
                act(lambda q, par=par: q.activation(
                    ct3[par][:, :], ct1[par][:, :], AF.Sigmoid,
                    scale=scbc[:, SC_B + 2:SC_B + 3],
                    bias=scbc[:, SC_NM:SC_NM + 1]),
                    reads=[(tag, "Ct1", par), "scbcB"], writes=[(tag, "Ct3", par)])
                act(lambda q, par=par: q.activation(
                    ct2[par][:, :], ct3[par][:, :], AF.Identity,
                    scale=65536.0, bias=mgcol[:, :]),
                    reads=[(tag, "Ct3", par), "mgcol"], writes=[(tag, "Ct2", par)])
                ve(lambda q, par=par: q.tensor_scalar(
                    ct3[par][:, :], ct2[par][:, :], MAGIC,
                    SIGCLIP, ALU.subtract, ALU.min),
                   reads=[(tag, "Ct2", par)], writes=[(tag, "Ct3", par)])
                ve(lambda q, par=par: q.scalar_tensor_tensor(
                    m1p[par][:, :], ct1[par][:, :], MAGIC, ct3[par][:, :],
                    ALU.subtract, ALU.mult),
                   reads=[(tag, "Ct1", par), (tag, "Ct3", par)], writes=[km])
                ve(lambda q, par=par, idx=idx, m1maxs=m1maxs, m1p=m1p:
                   q.tensor_reduce(m1maxs[:, idx:idx + 1], m1p[par][:, :],
                                   axis=AX.X, op=ALU.max,
                                   apply_absolute_value=True),
                   reads=[km], writes=["m1maxs"])
                dma("g", m1_s.ap()[fc, :, h * HP:(h + 1) * HP], m1p[par][:, :],
                    reads=[km], writes=[m1_s.name])

            c_front(0)
            c_front(1)
            for idx in range(2 * FC):
                c_rest(idx)
                if idx + 2 < 2 * FC:
                    c_front(idx + 2)
            ve(lambda q, m1maxs=m1maxs: q.tensor_reduce(
                scbc[:, 15:16], m1maxs[:, :], axis=AX.X,
                op=ALU.max), reads=["m1maxs"],
               writes=[(tag, "mcol")])
            colmax_to_row(scbc[:, 15:16], (tag, "mcol"), 0)
            p.full_barrier()

        collective_max(1)
        ve(lambda q: q.tensor_tensor(scrow[0:1, 0:1], scrow[0:1, 0:1],
                                     scrow[0:1, 4:5], ALU.mult),
           reads=[("scrow", 0), ("scrow", 4)], writes=[("scrow", 0)])
        scale_pair(0, (0, 1), pre=INV65536)
        ve(lambda q: q.tensor_scalar(scrow[0:1, 2:3], scrow[0:1, 1:2],
                                     scrow[0:1, 4:5], INV65536, ALU.mult,
                                     ALU.mult),
           reads=[("scrow", 1), ("scrow", 4)], writes=[("scrow", 2)])
        bcast_row(scrow[0:1, 0:3], scbc[:, SC_C:SC_C + 3],
                  [("scrow", 0), ("scrow", 1), ("scrow", 2)], ["scbcC"], 3)
        if upto == "C":
            return

        with contextlib.ExitStack() as st2:
            m1d = [st2.enter_context(nc.sbuf_tensor(f"{tag}Dm{i}", [128, HP], BF16))
                   for i in range(2)]
            upd = [st2.enter_context(nc.sbuf_tensor(f"{tag}Du{i}", [128, HP], BF16))
                   for i in range(2)]
            dt1 = [st2.enter_context(nc.sbuf_tensor(f"{tag}Dt1{i}", [128, HP], F32))
                   for i in range(2)]
            dt2 = [st2.enter_context(nc.sbuf_tensor(f"{tag}Dt2{i}", [128, HP], F32))
                   for i in range(2)]
            dt3 = [st2.enter_context(nc.sbuf_tensor(f"{tag}Dt3{i}", [128, HP], BF16))
                   for i in range(2)]
            omaxs = st2.enter_context(nc.sbuf_tensor(f"{tag}Dom", [128, 2 * FC],
                                                     F32))
            def d_front(idx):
                fc, h = idx // 2, idx % 2
                par = idx % 2
                km = (tag, "Dm", par)
                ku = (tag, "Du", par)
                dma("sp", m1d[par][:, :], m1_s.ap()[fc, :, h * HP:(h + 1) * HP],
                    reads=[m1_s.name], writes=[km])
                dma("sp", upd[par][:, :], up_s.ap()[fc, :, h * HP:(h + 1) * HP],
                    reads=[up_s.name], writes=[ku])
                act(lambda q, par=par: q.activation(
                    dt1[par][:, :], m1d[par][:, :], AF.Identity,
                    scale=scbc[:, SC_C + 2:SC_C + 3], bias=mgcol[:, :]),
                    reads=[km, "scbcC", "mgcol"], writes=[(tag, "Dt1", par)])
                act(lambda q, par=par: q.activation(
                    dt2[par][:, :], upd[par][:, :], AF.Identity,
                    scale=scbc[:, SC_B + 1:SC_B + 2], bias=mgcol[:, :]),
                    reads=[ku, "scbcB", "mgcol"], writes=[(tag, "Dt2", par)])

            def d_rest(idx):
                fc, h = idx // 2, idx % 2
                par = idx % 2
                ve(lambda q, par=par: q.tensor_scalar(
                    dt1[par][:, :], dt1[par][:, :], MAGIC, scbc[:, SC_C:SC_C + 1],
                    ALU.subtract, ALU.mult),
                   reads=[(tag, "Dt1", par), "scbcC"], writes=[(tag, "Dt1", par)])
                p.emit("g", lambda q, par=par: q.tensor_scalar(
                    dt2[par][:, :], dt2[par][:, :], -MAGIC, scbc[:, SC_B:SC_B + 1],
                    ALU.add, ALU.mult),
                       reads=[(tag, "Dt2", par), "scbcB"],
                       writes=[(tag, "Dt2", par)])
                eng_o = "g" if (idx % 2 == 0) else "ve"
                p.emit(eng_o, lambda q, par=par: q.tensor_tensor(
                    dt3[par][:, :], dt1[par][:, :], dt2[par][:, :], ALU.mult),
                       reads=[(tag, "Dt1", par), (tag, "Dt2", par)],
                       writes=[(tag, "Dt3", par)])
                ve(lambda q, idx=idx, par=par, omaxs=omaxs: q.tensor_reduce(
                    omaxs[:, idx:idx + 1], dt3[par][:, :], axis=AX.X, op=ALU.max,
                    apply_absolute_value=True),
                   reads=[(tag, "Dt3", par)], writes=["omaxs"])
                ng_h = HP // 256
                dma("g",
                    o_s.ap()[h * ng_h:(h + 1) * ng_h, :, fc * 256:(fc + 1) * 256]
                    .rearrange("g p t -> p g t"),
                    dt3[par][:, :].rearrange("p (g t) -> p g t", t=256),
                    reads=[(tag, "Dt3", par)], writes=[o_s.name])

            d_front(0)
            d_front(1)
            for idx in range(2 * FC):
                d_rest(idx)
                if idx + 2 < 2 * FC:
                    d_front(idx + 2)
            ve(lambda q, omaxs=omaxs: q.tensor_reduce(
                scbc[:, 15:16], omaxs[:, :], axis=AX.X,
                op=ALU.max), reads=["omaxs"],
               writes=[(tag, "ocol")])
            colmax_to_row(scbc[:, 15:16], (tag, "ocol"), 0)
            p.full_barrier()

        collective_max(1)
        scale_pair(0, (0, 1))
        ve(lambda q: q.tensor_scalar(scrow[0:1, 2:3], scrow[0:1, 0:1], 0.5,
                                     None, ALU.mult),
           reads=[("scrow", 0)], writes=[("scrow", 2)])
        ve(lambda q: q.tensor_copy(scrow[0:1, 3:4], scrow[0:1, 1:2]),
           reads=[("scrow", 1)], writes=[("scrow", 3)])
        bcast_row(scrow[0:1, 2:4], scbc[:, SC_D:SC_D + 2],
                  [("scrow", 2), ("scrow", 3)], ["scbcD"], 2)
        if upto == "D":
            return

        EG = 256
        NEG = N // EG
        W = FC * EG
        with contextlib.ExitStack() as st2:
            wdT = st2.enter_context(nc.sbuf_tensor(f"{tag}EwD", [128, FC * D], BF16))
            oe = [st2.enter_context(nc.sbuf_tensor(f"{tag}Eo{i}", [128, W], BF16))
                  for i in range(2)]
            xe = [st2.enter_context(nc.sbuf_tensor(f"{tag}Ex{i}", [128, 2 * D], F32))
                  for i in range(2)]
            et1 = [st2.enter_context(nc.sbuf_tensor(f"{tag}Et1{i}", [128, W], F32))
                   for i in range(2)]
            oq = [st2.enter_context(nc.sbuf_tensor(f"{tag}Eoq{i}", [128, W], BF16))
                  for i in range(2)]
            xo = [st2.enter_context(nc.sbuf_tensor(f"{tag}Exo{i}", [128, 2 * D], F32))
                  for i in range(2)]
            ebn = st2.enter_context(nc.sbuf_tensor(f"{tag}Ebn", [128, 12], F32))
            dma("sp", wdT[:, :], wdownT_h.ap(), reads=[("wdown", "spill")],
                writes=[(tag, "EwD")])
            def e_front(g):
                par = g % 2
                ko = (tag, "Eo", par)
                kx = (tag, "Ex", par)
                dma("sp", oe[par][:, :], o_s.ap()[g, :, :],
                    reads=[o_s.name], writes=[ko])
                dma("sp", xe[par][:, :].rearrange("p (a b) -> p a b", b=D),
                    xin_h.ap()[g * EG:(g + 1) * EG, :]
                    .rearrange("(a t) b -> t a b", t=128),
                    reads=[kxin], writes=[kx])
                act(lambda q, par=par: q.activation(
                    et1[par][:, :], oe[par][:, :], AF.Identity,
                    scale=scbc[:, SC_D + 1:SC_D + 2], bias=mgcol[:, :]),
                    reads=[ko, "scbcD", "mgcol"], writes=[(tag, "Et1", par)])

            def e_rest(g):
                par = g % 2
                kx = (tag, "Ex", par)
                koq = (tag, "Eoq", par)
                kxo = (tag, "Exo", par)
                ve(lambda q, par=par: q.tensor_scalar(
                    oq[par][:, :], et1[par][:, :], MAGIC, scbc[:, SC_D:SC_D + 1],
                    ALU.subtract, ALU.mult),
                   reads=[(tag, "Et1", par), "scbcD"], writes=[koq])
                for tc in range(2):
                    banks = []
                    for hh in range(2):
                        bank, bkey = next_bank()
                        banks.append((bank, bkey))
                        p.mm_group(
                            [(lambda q, s, e, fc=fc2, bank=bank, hh=hh, par=par,
                              tc=tc, oq=oq: q.matmul(
                                  bank[:, 0:512],
                                  oq[par][:, fc * EG + tc * 128:
                                          fc * EG + tc * 128 + 128],
                                  wdT[:, fc * D + hh * 512:
                                      fc * D + hh * 512 + 512],
                                  start=s, stop=e))
                             for fc2 in range(FC)],
                            reads=[koq, (tag, "EwD")], writes=[bkey])
                    for hh, (bank, bkey) in enumerate(banks):
                        ve(lambda q, bank=bank, hh=hh, tc=tc, par=par:
                           q.tensor_tensor(
                               xo[par][:, tc * D + hh * 512:
                                       tc * D + hh * 512 + 512],
                               bank[:, 0:512],
                               xe[par][:, tc * D + hh * 512:
                                       tc * D + hh * 512 + 512],
                               ALU.add),
                           reads=[bkey, kx], writes=[kxo])
                dma("g",
                    xout_h.ap()[g * EG:(g + 1) * EG, :]
                    .rearrange("(a t) b -> t a b", t=128),
                    xo[par][:, :].rearrange("p (a b) -> p a b", b=D),
                    reads=[kxo], writes=[kxout])
                if dbg_x is not None:
                    dma("g",
                        dbg_x.ap()[g * EG:(g + 1) * EG, :]
                        .rearrange("(a t) b -> t a b", t=128),
                        xo[par][:, :].rearrange("p (a b) -> p a b", b=D),
                        reads=[kxo], writes=["dbgx" + tag])
                if fuse_stats_mv is not None:
                    fmv, kfmv = fuse_stats_mv
                    for tc in range(2):
                        tt2 = g * 2 + tc
                        ve(lambda q, par=par, tc=tc: q.bn_stats(
                            ebn[:, 0:6], xo[par][:, tc * D:tc * D + 512]),
                           reads=[kxo], writes=[(tag, "Ebn0")])
                        ve(lambda q, par=par, tc=tc: q.bn_stats(
                            ebn[:, 6:12], xo[par][:, tc * D + 512:tc * D + 1024]),
                           reads=[kxo], writes=[(tag, "Ebn1")])
                        ve(lambda q, tt2=tt2, fmv=fmv: q.bn_aggr(
                            fmv[:, 2 * tt2:2 * tt2 + 2], ebn[:, :]),
                           reads=[(tag, "Ebn0"), (tag, "Ebn1")],
                           writes=[(kfmv, tt2), kfmv])

            e_front(0)
            for g in range(NEG):
                e_rest(g)
                if g + 1 < NEG:
                    e_front(g + 1)
            p.full_barrier()


    def finish():
        p.full_barrier()
        with nc.Block() as block:
            p.build(block, stack)
        stack.close()
        return nc

    if stage == "prep":
        zo = sb("zot", [128, D], F32)
        ve(lambda q: q.memset(zo[:, :], 0.0), writes=["zot"])
        for t in range(TT):
            dma("sp", out_h.ap()[t * 128:(t + 1) * 128, :], zo[:, :],
                reads=["zot"], writes=["outw"])
        return finish()

    _upto = stage[2] if stage.startswith("f1") and len(stage) > 2 else "E"
    ffn_call("f1", x0_h, x1_h, mv1, rstd1, b1, xnm1, need_pass_a=True,
             fuse_stats_mv=(mva, "amv"), dbg_x=dbg.get("x1"), upto=_upto)
    if stage.startswith("f1"):
        zero_out_finish()
        return finish()
    for nm, scr in (("gate", gate_s), ("up", up_s), ("m1", m1_s)):
        if nm in dbg:
            for fc in range(FC):
                dma("g", dbg[nm].ap()[fc, :, :], scr.ap()[fc, :, :],
                    reads=[scr.name], writes=["dbg_" + nm])

    def attention():
        kx1, kx2 = x1_h.name, x2_h.name
        with contextlib.ExitStack() as st2:
            asb = lambda nm, sh, dt: st2.enter_context(nc.sbuf_tensor(nm, sh, dt))
            wqs = asb("at_wq", [128, DC * D], BF16)
            wks = asb("at_wk", [128, DC * D], BF16)
            wvs = asb("at_wv", [128, DC * D], BF16)
            wos = asb("at_wo", [128, DC * D], BF16)
            brow = asb("at_brow", [1, D], F32)
            bq_c = asb("at_bqc", [128, DC], F32)
            bk_c = asb("at_bkc", [128, DC], F32)
            bv_bc = asb("at_bvbc", [128, D], BF16)
            bo_bc = asb("at_bobc", [128, D], BF16)
            lam = asb("at_lam", [1, 4], F32)
            lnl_bc = asb("at_lnlbc", [128, 1], F32)
            lamC = asb("at_lamC", [128, 1], F32)
            lamr_bc = asb("at_lamrbc", [128, CH], BF16)
            lamK = asb("at_lamK", [128, TPCH], F32)
            mask = asb("at_mask", [128, TPCH * CH], BF16)
            xot = asb("at_xot", [128, D], F32)
            iot = xot[:, 0:CH]
            msk1 = xot[:, CH:2 * CH]
            msk2 = xot[:, 2 * CH:3 * CH]
            lamrow = xot[0:1, 3 * CH:3 * CH + CH]
            mem = asb("at_mem", [128, DC * D], F32)
            memb = asb("at_memb", [128, DC * D], BF16)
            xat = [asb(f"at_xa{i}", [128, TPCH * D], F32) for i in range(3)]
            xnb = [asb(f"at_xnb{i}", [128, TPCH * D], BF16) for i in range(2)]
            xnT = asb("at_xnT", [128, DC * CH], BF16)
            qT = asb("at_qT", [128, DC * CH], BF16)
            qs = [asb(f"at_qs{i}", [128, DC * CH], BF16) for i in range(2)]
            kT = asb("at_kT", [128, DC * CH], BF16)
            tmpA = asb("at_tmpA", [128, CH], F32)
            tmpB = asb("at_tmpB", [128, CH], F32)
            tmpC = asb("at_tmpC", [128, CH], F32)
            Kp = [asb(f"at_Kp{i}", [128, TPCH * D], BF16) for i in range(2)]
            Vs = [asb(f"at_V{i}", [128, TPCH * D], BF16) for i in range(2)]
            scm = [asb(f"at_scm{i}", [128, TPCH * CH], BF16) for i in range(2)]
            aoT = asb("at_aoT", [128, DC * CH], BF16)
            bnst = asb("at_bn", [128, 12], F32)

            for (w_h, ws, kws) in ((wq_h, wqs, "at_wq"), (wk_h, wks, "at_wk"),
                                   (wv_h, wvs, "at_wv"), (wo_h, wos, "at_wo")):
                dma("g", ws[:, :].rearrange("p (c d) -> p c d", d=D),
                    w_h.ap().rearrange("(c p) d -> p c d", p=128),
                    writes=[kws])
            for (b_h, dst, kdst) in ((bq_h, bq_c, "at_bqc"), (bk_h, bk_c, "at_bkc")):
                dma("sp", brow[:, :], b_h.ap(), writes=["at_brow"])
                for ec in range(DC):
                    p.mm_group([lambda q, s, e, ec=ec: q.transpose(
                        pms[0:128, 0:1], brow[0:1, ec * 128:(ec + 1) * 128],
                        idf[0:1, 0:1])],
                        reads=["at_brow", K_IDF], writes=["pms"])
                    ve(lambda q, ec=ec, dst=dst: q.tensor_copy(
                        dst[:, ec:ec + 1], pms[:, 0:1]),
                       reads=["pms"], writes=[kdst])
            for (b_h, dst, kdst) in ((bv_h, bv_bc, "at_bvbc"), (bo_h, bo_bc, "at_bobc")):
                dma("sp", brow[:, :], b_h.ap(), writes=["at_brow"])
                for h in range(2):
                    bcast_row(brow[0:1, h * 512:(h + 1) * 512],
                              dst[:, h * 512:(h + 1) * 512], ["at_brow"], [kdst], 512)
            dma("sp", lam[0:1, 0:1], dec_h.ap(), writes=["at_lam"])
            act(lambda q: q.activation(lam[0:1, 1:2], lam[0:1, 0:1], AF.Sigmoid),
                reads=["at_lam"], writes=[("at_lam", 1)])
            ve(lambda q: q.tensor_scalar(lam[0:1, 2:3], lam[0:1, 1:2],
                                         float(2.0 ** -60), None, ALU.max),
               reads=[("at_lam", 1)], writes=[("at_lam", 2)])
            act(lambda q: q.activation(lam[0:1, 3:4], lam[0:1, 2:3], AF.Ln),
                reads=[("at_lam", 2)], writes=[("at_lam", 3)])
            bcast_row(lam[0:1, 3:4], lnl_bc[:, :], [("at_lam", 3)], ["at_lnlbc"], 1)
            act(lambda q: q.activation(lamC[:, :], lnl_bc[:, :], AF.Exp,
                                       scale=float(CH)),
                reads=["at_lnlbc"], writes=["at_lamC"])
            p.emit("g", lambda q: q.iota(lamrow, pattern=[[1, CH]], base=0,
                                         channel_multiplier=0,
                                         allow_small_or_imprecise_dtypes=True),
                   writes=["at_xot"])
            act(lambda q: q.activation(lamrow, lamrow, AF.Exp,
                                       scale=lam[0:1, 3:4]),
                reads=["at_xot", ("at_lam", 3)], writes=["at_xot"])
            bcast_row(lamrow, lamr_bc[:, :], ["at_xot"], ["at_lamrbc"], CH)
            p.emit("g", lambda q: q.iota(iot[:, 0:TPCH], pattern=[[-128, TPCH]],
                                         base=CH - 1, channel_multiplier=-1,
                                         allow_small_or_imprecise_dtypes=True),
                   writes=["at_xot"])
            act(lambda q: q.activation(lamK[:, :], iot[:, 0:TPCH], AF.Exp,
                                       scale=lnl_bc[:, 0:1]),
                reads=["at_xot", "at_lnlbc"], writes=["at_lamK"])
            for jc in range(TPCH):
                p.emit("g", lambda q, jc=jc: q.iota(
                    iot, pattern=[[1, CH]], base=-1 - jc * 128,
                    channel_multiplier=-1,
                    allow_small_or_imprecise_dtypes=True),
                    writes=["at_xot"])
                act(lambda q: q.activation(msk1, iot, AF.Relu),
                    reads=["at_xot"], writes=["at_xot"])
                act(lambda q: q.activation(msk2, iot, AF.Relu,
                                           bias=onecol[:, :]),
                    reads=["at_xot", "onecol"], writes=["at_xot"])
                ve(lambda q: q.tensor_tensor(msk2, msk2, msk1,
                                             ALU.subtract),
                   reads=["at_xot", "at_xot"], writes=["at_xot"])
                act(lambda q: q.activation(msk1, msk1, AF.Exp,
                                           scale=lnl_bc[:, 0:1]),
                    reads=["at_xot", "at_lnlbc"], writes=["at_xot"])
                ve(lambda q, jc=jc: q.tensor_tensor(
                    mask[:, jc * CH:(jc + 1) * CH], msk1, msk2,
                    ALU.mult),
                   reads=["at_xot", "at_xot"], writes=["at_mask"])

            var_view = mva[:, :].rearrange("p (t two) -> p t two", two=2)[:, :, 1]
            mean_view = mva[:, :].rearrange("p (t two) -> p t two", two=2)[:, :, 0]
            ve(lambda q: q.tensor_scalar(rstda[:, :], var_view, LN_EPS, None,
                                         ALU.add), reads=["amv"], writes=["arstd"])
            act(lambda q: q.activation(rstda[:, :], rstda[:, :], AF.Sqrt),
                reads=["arstd"], writes=["arstd"])
            ve(lambda q: q.reciprocal(rstda[:, :], rstda[:, :]),
               reads=["arstd"], writes=["arstd"])
            ve(lambda q: q.scalar_tensor_tensor(ba[:, :], mean_view, -1.0,
                                                rstda[:, :], ALU.mult, ALU.mult),
               reads=["amv", "arstd"], writes=["ab"])

            def stage0(ci):
                s, c = divmod(ci, NCH)
                par3 = ci % 3
                par = ci % 2
                base = s * SEQ + c * CH
                kxa = ("at_xa", par3)
                for tc in range(TPCH):
                    t = base // 128 + tc
                    dma("sp", xat[par3][:, tc * D:(tc + 1) * D],
                        x1_h.ap()[t * 128:(t + 1) * 128, :],
                        reads=[kx1], writes=[kxa])
                    act(lambda q, t=t, tc=tc, par3=par3, par=par: q.activation(
                        xnb[par][:, tc * D:(tc + 1) * D],
                        xat[par3][:, tc * D:(tc + 1) * D], AF.Identity,
                        bias=ba[:, t:t + 1], scale=rstda[:, t:t + 1]),
                        reads=[kxa, "arstd", "ab"], writes=[("at_xnb", par, tc)])

            def stage1(ci):
                s, c = divmod(ci, NCH)
                par = ci % 2
                base = s * SEQ + c * CH
                kqs, kKp, kV, kscm = (("at_qs", par), ("at_Kp", par),
                                      ("at_V", par), ("at_scm", par))
                for tc in range(TPCH):
                    srcs = [xnb[par][:, tc * D + dc * 128: tc * D + (dc + 1) * 128]
                            for dc in range(DC)]
                    bank, bkey = tr_batch(srcs, [("at_xnb", par, tc)])
                    ve(lambda q, tc=tc, bank=bank: q.tensor_copy(
                        xnT[:, :].rearrange("p (d t) -> p d t", t=CH)
                        [:, :, tc * 128:(tc + 1) * 128],
                        bank[:, :].rearrange("p (d t) -> p d t", t=128)),
                       reads=[bkey], writes=["at_xnT"])
                for ec in range(DC):
                    bank, bkey = next_bank()
                    p.mm_group(
                        [(lambda q, s2, e2, dc=dc2, ec=ec, bank=bank: q.matmul(
                            bank[:, 0:CH],
                            wqs[:, dc * D + ec * 128: dc * D + ec * 128 + 128],
                            xnT[:, dc * CH:(dc + 1) * CH], start=s2, stop=e2))
                         for dc2 in range(DC)],
                        reads=["at_wq", "at_xnT"], writes=[bkey])
                    act(lambda q, ec=ec, bank=bank: q.activation(
                        qT[:, ec * CH:(ec + 1) * CH], bank[:, 0:CH], AF.Identity,
                        bias=bq_c[:, ec:ec + 1]),
                        reads=[bkey, "at_bqc"], writes=["at_qT"])
                ve(lambda q, par=par: q.tensor_tensor(
                    qs[par][:, :].rearrange("p (e i) -> p e i", i=CH),
                    qT[:, :].rearrange("p (e i) -> p e i", i=CH),
                    lamr_bc[:, :].rearrange("p (o i) -> p o i", o=1)
                    .broadcast_to([128, DC, CH]),
                    ALU.mult),
                   reads=["at_qT", "at_lamrbc"], writes=[kqs])
                for ec in range(DC):
                    bank, bkey = next_bank()
                    p.mm_group(
                        [(lambda q, s2, e2, dc=dc2, ec=ec, bank=bank: q.matmul(
                            bank[:, 0:CH],
                            wks[:, dc * D + ec * 128: dc * D + ec * 128 + 128],
                            xnT[:, dc * CH:(dc + 1) * CH], start=s2, stop=e2))
                         for dc2 in range(DC)],
                        reads=["at_wk", "at_xnT"], writes=[bkey])
                    act(lambda q, ec=ec, bank=bank: q.activation(
                        tmpA[:, :], bank[:, 0:CH], AF.Relu,
                        bias=bk_c[:, ec:ec + 1]),
                        reads=[bkey, "at_bkc"], writes=["at_tmpA"])
                    ve(lambda q, ec=ec, bank=bank: q.tensor_scalar(
                        tmpB[:, :], bank[:, 0:CH], bk_c[:, ec:ec + 1], 0.0,
                        ALU.add, ALU.min),
                       reads=[bkey, "at_bkc"], writes=["at_tmpB"])
                    act(lambda q: q.activation(tmpC[:, :], tmpB[:, :], AF.Exp),
                        reads=["at_tmpB"], writes=["at_tmpC"])
                    ve(lambda q, ec=ec: q.tensor_tensor(
                        kT[:, ec * CH:(ec + 1) * CH], tmpA[:, :], tmpC[:, :],
                        ALU.add),
                       reads=["at_tmpA", "at_tmpC"], writes=["at_kT"])
                for tc in range(TPCH):
                    for h in range(2):
                        bank, bkey = next_bank()
                        p.mm_group(
                            [(lambda q, s2, e2, dc=dc2, tc=tc, h=h, bank=bank:
                              q.matmul(
                                  bank[:, 0:512],
                                  xnT[:, dc * CH + tc * 128: dc * CH + tc * 128 + 128],
                                  wvs[:, dc * D + h * 512: dc * D + h * 512 + 512],
                                  start=s2, stop=e2))
                             for dc2 in range(DC)],
                            reads=["at_wv", "at_xnT"], writes=[bkey])
                        ve(lambda q, tc=tc, h=h, par=par, bank=bank: q.tensor_tensor(
                            Vs[par][:, tc * D + h * 512: tc * D + h * 512 + 512],
                            bank[:, 0:512], bv_bc[:, h * 512:(h + 1) * 512],
                            ALU.add),
                           reads=[bkey, "at_bvbc"], writes=[kV])
                for tc in range(TPCH):
                    srcs = [kT[:, ec * CH + tc * 128: ec * CH + tc * 128 + 128]
                            for ec in range(DC)]
                    bank, bkey = tr_batch(srcs, ["at_kT"])
                    act(lambda q, tc=tc, par=par, bank=bank: q.activation(
                        Kp[par][:, tc * D:(tc + 1) * D], bank[:, :],
                        AF.Identity, scale=lamK[:, tc:tc + 1]),
                        reads=[bkey, "at_lamK"], writes=[kKp])
                for jc in range(TPCH):
                    bank, bkey = next_bank()
                    p.mm_group(
                        [(lambda q, s2, e2, ec=ec2, jc=jc, bank=bank: q.matmul(
                            bank[:, 0:CH],
                            kT[:, ec * CH + jc * 128: ec * CH + jc * 128 + 128],
                            qT[:, ec * CH:(ec + 1) * CH], start=s2, stop=e2))
                         for ec2 in range(DC)],
                        reads=["at_kT", "at_qT"], writes=[bkey])
                    ve(lambda q, jc=jc, par=par, bank=bank: q.tensor_tensor(
                        scm[par][:, jc * CH:(jc + 1) * CH], bank[:, 0:CH],
                        mask[:, jc * CH:(jc + 1) * CH], ALU.mult),
                       reads=[bkey, "at_mask"], writes=[kscm])

            def stage2(ci):
                s, c = divmod(ci, NCH)
                par = ci % 2
                par3 = ci % 3
                base = s * SEQ + c * CH
                kqs, kKp, kV, kscm = (("at_qs", par), ("at_Kp", par),
                                      ("at_V", par), ("at_scm", par))
                kxa = ("at_xa", par3)
                first = (c == 0)
                last = (c == NCH - 1)
                for vc in range(DC):
                    bank, bkey = next_bank()
                    mms = [(lambda q, s2, e2, jc=jc2, vc=vc, par=par, bank=bank:
                            q.matmul(
                                bank[:, 0:CH],
                                Vs[par][:, jc * D + vc * 128: jc * D + vc * 128 + 128],
                                scm[par][:, jc * CH:(jc + 1) * CH],
                                start=s2, stop=e2))
                           for jc2 in range(TPCH)]
                    if not first:
                        mms += [(lambda q, s2, e2, ec=ec2, vc=vc, par=par,
                                 bank=bank: q.matmul(
                                     bank[:, 0:CH],
                                     memb[:, ec * D + vc * 128: ec * D + vc * 128 + 128],
                                     qs[par][:, ec * CH:(ec + 1) * CH],
                                     start=s2, stop=e2))
                                for ec2 in range(DC)]
                    p.mm_group(mms, reads=[kV, kscm, "at_memb", kqs],
                               writes=[bkey])
                    ve(lambda q, vc=vc, bank=bank: q.tensor_copy(
                        aoT[:, vc * CH:(vc + 1) * CH], bank[:, 0:CH]),
                       reads=[bkey], writes=["at_aoT"])
                if not last:
                    for ec in range(DC):
                        for h in range(2):
                            bank, bkey = next_bank()
                            p.mm_group(
                                [(lambda q, s2, e2, tc=tc2, ec=ec, h=h, par=par,
                                  bank=bank: q.matmul(
                                      bank[:, 0:512],
                                      Kp[par][:, tc * D + ec * 128: tc * D + ec * 128 + 128],
                                      Vs[par][:, tc * D + h * 512: tc * D + h * 512 + 512],
                                      start=s2, stop=e2))
                                 for tc2 in range(TPCH)],
                                reads=[kKp, kV], writes=[bkey])
                            off = ec * D + h * 512
                            if first:
                                ve(lambda q, off=off, bank=bank: q.tensor_copy(
                                    mem[:, off:off + 512], bank[:, 0:512]),
                                   reads=[bkey], writes=["at_mem"])
                            else:
                                ve(lambda q, off=off, bank=bank:
                                   q.scalar_tensor_tensor(
                                       mem[:, off:off + 512], mem[:, off:off + 512],
                                       lamC[:, 0:1], bank[:, 0:512],
                                       ALU.mult, ALU.add),
                                   reads=[bkey, "at_lamC", "at_mem"],
                                   writes=["at_mem"])
                        act(lambda q, ec=ec: q.copy(
                            memb[:, ec * D:(ec + 1) * D],
                            mem[:, ec * D:(ec + 1) * D]),
                            reads=["at_mem"], writes=["at_memb"])

                for ic in range(TPCH):
                    t = base // 128 + ic
                    banks = []
                    for h in range(2):
                        bank, bkey = next_bank()
                        banks.append((bank, bkey))
                        p.mm_group(
                            [(lambda q, s2, e2, vc=vc2, ic=ic, h=h, bank=bank:
                              q.matmul(
                                  bank[:, 0:512],
                                  aoT[:, vc * CH + ic * 128: vc * CH + ic * 128 + 128],
                                  wos[:, vc * D + h * 512: vc * D + h * 512 + 512],
                                  start=s2, stop=e2))
                             for vc2 in range(DC)],
                            reads=["at_aoT", "at_wo"], writes=[bkey])
                    for h, (bank, bkey) in enumerate(banks):
                        ve(lambda q, h=h, bank=bank: q.tensor_tensor(
                            xot[:, h * 512:(h + 1) * 512], bank[:, 0:512],
                            bo_bc[:, h * 512:(h + 1) * 512], ALU.add),
                           reads=[bkey, "at_bobc"], writes=["at_xot"])
                    ve(lambda q, ic=ic, par3=par3: q.tensor_tensor(
                        xot[:, :], xot[:, :], xat[par3][:, ic * D:(ic + 1) * D],
                        ALU.add),
                       reads=["at_xot", kxa], writes=["at_xot"])
                    dma("g", x2_h.ap()[t * 128:(t + 1) * 128, :], xot[:, :],
                        reads=["at_xot"], writes=[kx2])
                    if "x2" in dbg:
                        dma("g", dbg["x2"].ap()[t * 128:(t + 1) * 128, :],
                            xot[:, :], reads=["at_xot"], writes=["dbg_x2w"])
                    ve(lambda q: q.bn_stats(bnst[:, 0:6], xot[:, 0:512]),
                       reads=["at_xot"], writes=["at_bn0"])
                    ve(lambda q: q.bn_stats(bnst[:, 6:12], xot[:, 512:1024]),
                       reads=["at_xot"], writes=["at_bn1"])
                    ve(lambda q, t=t: q.bn_aggr(mv2[:, 2 * t:2 * t + 2],
                                                bnst[:, :]),
                       reads=["at_bn0", "at_bn1"], writes=[("f2mv", t), "f2mv"])
                    ve(lambda q, t=t: q.tensor_scalar(
                        ntmp[:, :], mv2[:, 2 * t:2 * t + 1], -1.0, None,
                        ALU.mult),
                       reads=[("f2mv", t)], writes=["ntmp"])
                    act(lambda q, t=t: q.activation(
                        xot[:, :], xot[:, :], AF.Identity, bias=ntmp[:, :],
                        scale=1.0),
                        reads=["at_xot", "ntmp"], writes=["at_xot"])
                    ve(lambda q, t=t: q.tensor_reduce(
                        xnm2[:, t:t + 1], xot[:, :], axis=AX.X, op=ALU.max,
                        apply_absolute_value=True),
                       reads=["at_xot"], writes=["f2xnm"])


            total = SPC * NCH
            stage0(0)
            stage0(1)
            stage1(0)
            for ci in range(total):
                if ci + 2 < total:
                    stage0(ci + 2)
                if ci + 1 < total:
                    stage1(ci + 1)
                stage2(ci)
            p.full_barrier()


    if stage == "ffn1":
        zo = sb("zot", [128, D], F32)
        ve(lambda q: q.memset(zo[:, :], 0.0), writes=["zot"])
        for t in range(TT):
            dma("sp", out_h.ap()[t * 128:(t + 1) * 128, :], zo[:, :],
                reads=["zot"], writes=["outw"])
        return finish()

    attention()

    if stage == "attn":
        zo = sb("zot", [128, D], F32)
        ve(lambda q: q.memset(zo[:, :], 0.0), writes=["zot"])
        for t in range(TT):
            dma("sp", out_h.ap()[t * 128:(t + 1) * 128, :], zo[:, :],
                reads=["zot"], writes=["outw"])
        return finish()

    ffn_call("f2", x2_h, out_h, mv2, rstd2, b2, xnm2, need_pass_a=False,
             fuse_stats_mv=None, dbg_x=None)

    if "scal" in dbg:
        dma("g", dbg["scal"].ap()[0:1, 0:8], scrow[0:1, 0:8],
            reads=["scrow"], writes=["dbgscal"])

    return finish()


_NC_CACHE = {}


def _get_nc(debug_out=(), stage="full"):
    key = (tuple(sorted(debug_out)), stage)
    if key not in _NC_CACHE:
        _NC_CACHE[key] = build_nc(debug_out=key[0], stage=stage)
    return _NC_CACHE[key]


def _in_maps(inputs):
    f32 = lambda a: np.ascontiguousarray(np.asarray(a, dtype=np.float32))
    x = f32(inputs["L0_attn_x"]).reshape(NCORE * SPC, SEQ, D)
    maps = []
    for c in range(NCORE):
        m = {
            "x": x[c * SPC:(c + 1) * SPC].reshape(N, D),
            "Wq": f32(inputs["L0_attn_Wq"]),
            "Wk": f32(inputs["L0_attn_Wk"]),
            "Wv": f32(inputs["L0_attn_Wv"]),
            "Wo": f32(inputs["L0_attn_Wo"]),
            "bq": f32(inputs["L0_attn_bq"]).reshape(1, D),
            "bk": f32(inputs["L0_attn_bk"]).reshape(1, D),
            "bv": f32(inputs["L0_attn_bv"]).reshape(1, D),
            "bo": f32(inputs["L0_attn_bo"]).reshape(1, D),
            "decay": f32(inputs["L0_attn_decay_param"]).reshape(1, 1),
            "w_gate": f32(inputs["L0_ffn_w_gate"]),
            "w_up": f32(inputs["L0_ffn_w_up"]),
            "w_down": f32(inputs["L0_ffn_w_down"]),
        }
        maps.append(m)
    return maps


def kernel(**inputs):
    nc = _get_nc()
    res = run_bass_kernel_spmd(nc, _in_maps(inputs), core_ids=list(range(NCORE)))
    out = np.concatenate([r["out"].reshape(SPC, SEQ, D) for r in res.results],
                         axis=0)
    return out.astype(np.float32)


def kernel_debug(debug_out, stage="full", **inputs):
    nc = _get_nc(debug_out, stage)
    res = run_bass_kernel_spmd(nc, _in_maps(inputs), core_ids=list(range(NCORE)))
    return res
